# version 08
# baseline (speedup 1.0000x reference)
import itertools
import os
import contextlib
from collections import defaultdict

import numpy as np
import ml_dtypes
import concourse.bass as bass
import concourse.mybir as mybir
from concourse.bass_utils import run_bass_kernel_spmd

F32 = mybir.dt.float32
BF16 = mybir.dt.bfloat16
I32 = mybir.dt.int32
AF = mybir.ActivationFunctionType
ALU = mybir.AluOpType
AX = mybir.AxisListType

B, S, E = 4, 2048, 2048
NCH = E // 128
NT = 1024
NTB = NT // 128
D_C, D_ROT = 512, 64
HEADS, HD = 16, 128
INTER0, INTER1 = 5504, 8192
LUTN = 16
SCALE = float(1.0 / np.sqrt(128.0))
EPS = 1e-5
NEG = -1e30
INV_E = 1.0 / 2048.0

G_ROWS = 768
D_ROWS = 256
QKVO_R, WGU_R, WDN_R = 256, 2048, 256
BF = ml_dtypes.bfloat16

_kvrr = itertools.cycle([0, 1, 2, 3])
_castrr = itertools.cycle([0, 1])

BLK = 128 * 16 * 128
OFF_WQ_B = 0
OFF_WK_B = 2 * BLK
OFF_WV_B = 4 * BLK
OFF_WO_B = OFF_WV_B + 16 * 128 * 256
OFF_WGU_B = OFF_WO_B + 2 * BLK
OFF_WDN_B = OFF_WGU_B + 16 * BLK
OFF_WO0_B = OFF_WDN_B + 8 * BLK
DENSE_SEG = OFF_WO0_B + 2 * BLK
L0_GRPS = [list(range(0, 16)), list(range(16, 32)), list(range(32, 43))]
DOWN_GRP_OFF = [0, 262144, 524288]
DOWN_SEG = 2 * 704512

PHASES = ["p1", "p2", "p3", "p4", "p5", "p6", "out"]


class Res:
    __slots__ = ("w", "r")
    def __init__(self):
        self.w = None
        self.r = []


class Em:
    NAMES = ("pe", "dve", "act", "pool", "qs", "qa", "qg")

    def __init__(self, nc, stack):
        self.nc = nc
        self.eng = {"pe": nc.tensor, "dve": nc.vector, "act": nc.scalar,
                    "pool": nc.gpsimd, "qs": nc.sync, "qa": nc.scalar,
                    "qg": nc.gpsimd}
        self.canon = {"pe": "pe", "dve": "dve", "act": "act", "pool": "pool",
                      "qs": "qs", "qa": "act", "qg": "pool"}
        self.sem = {n: stack.enter_context(nc.semaphore(f"sem_{n}")) for n in self.NAMES}
        self.dma_lanes = {"qs", "qa", "qg"}
        self.last_group = {"qs": None, "qa": None, "qg": None}
        self.auto_group = 0
        self.cc_sems = []
        self.stack = stack
        self.cnt = defaultdict(int)
        self.known = defaultdict(int)

    def lane(self, name):
        if name not in self.sem:
            self.sem[name] = self.stack.enter_context(self.nc.semaphore(f"lane_{name}"))
            self.dma_lanes.add(name)
            self.last_group[name] = None
        return name

    def _wait_one(self, waiter, sname, val):
        w = self.canon[waiter]
        if w == "pe" and sname == "pe":
            return
        if isinstance(sname, tuple):
            if self.known[(w, sname)] >= 1:
                return
            self.eng[waiter].wait_ge(self.cc_sems[sname[1]], 1)
            self.known[(w, sname)] = 1
        else:
            if self.known[(w, sname)] >= val:
                return
            self.eng[waiter].wait_ge(self.sem[sname], val)
            self.known[(w, sname)] = val

    def _deps(self, waiter, reads, writes):
        need = {}
        def add(s, t):
            need[s] = max(need.get(s, 0), t)
        for r in reads:
            if r.w is not None:
                s, t = r.w
                if isinstance(s, tuple):
                    self._wait_one(waiter, s, 1)
                else:
                    add(s, t)
        for wr in writes:
            if wr.w is not None:
                s, t = wr.w
                if isinstance(s, tuple):
                    self._wait_one(waiter, s, 1)
                else:
                    add(s, t)
            for (s, t) in wr.r:
                add(s, t)
        for s in sorted(need):
            self._wait_one(waiter, s, need[s])

    def op(self, eng, fn, reads=(), writes=()):
        self._deps(eng, reads, writes)
        instr = fn(self.eng[eng])
        instr.then_inc(self.sem[eng], 1)
        self.cnt[eng] += 1
        t = self.cnt[eng]
        for r in reads:
            r.r.append((eng, t))
        for w in writes:
            w.w = (eng, t)
            w.r = []
        return t

    def dma(self, q, out, in_, reads=(), writes=(), transpose=False, lane=None,
            group=None, **kw):
        ln_ = self.lane(lane) if lane else q
        if group is None:
            self.auto_group += 1
            group = ("__auto", self.auto_group)
        if self.last_group.get(ln_) != group:
            if self.cnt[ln_] > 0:
                self._wait_one(q, ln_, self.cnt[ln_])
            self.last_group[ln_] = group
        self._deps(q, reads, writes)
        e = self.eng[q]
        if transpose:
            instr = e.dma_start_transpose(out, in_)
        else:
            instr = e.dma_start(out=out, in_=in_, **kw)
        instr.then_inc(self.sem[ln_], 16)
        self.cnt[ln_] += 16
        t = self.cnt[ln_]
        for r in reads:
            r.r.append((ln_, t))
        for w in writes:
            w.w = (ln_, t)
            w.r = []
        return t

    def collective(self, kind, op, groups, ins, outs, reads=(), writes=()):
        self._deps("qg", reads, writes)
        idx = len(self.cc_sems)
        sem = self.stack.enter_context(self.nc.semaphore(f"cc{idx}"))
        self.cc_sems.append(sem)
        self.eng["qg"].collective_compute(
            kind, op, replica_groups=groups, ins=ins, outs=outs).then_inc(sem)
        for w in writes:
            w.w = (("__cc", idx), 1)
            w.r = []

    def barrier(self):
        snap = dict(self.cnt)
        for waiter in self.NAMES:
            for s, v in snap.items():
                if v:
                    self._wait_one(waiter, s, v)


class Acc:
    def __init__(self, em, bank_res, n):
        self.em = em
        self.bank = bank_res
        self.n = n
        self.i = 0
        self.touched = []

    def add(self, out_ap, lhsT_ap, rhs_ap, reads):
        em = self.em
        if self.i == 0:
            em._deps("pe", [], [self.bank])
        em._deps("pe", reads, [])
        last = (self.i == self.n - 1)
        instr = em.eng["pe"].matmul(out_ap, lhsT_ap, rhs_ap,
                                    start=(self.i == 0), stop=last)
        instr.then_inc(em.sem["pe"], 1)
        em.cnt["pe"] += 1
        t = em.cnt["pe"]
        for r in reads:
            r.r.append(("pe", t))
        self.i += 1
        if last:
            mode = os.environ.get("MK_NONOP", "1")
            if mode == "2":
                em.eng["pe"].nop(nofuse=True)
            elif mode != "1":
                em.eng["pe"].nop(nofuse=True).then_inc(em.sem["pe"], 1)
                em.cnt["pe"] += 1
            self.bank.w = ("pe", em.cnt["pe"])
            self.bank.r = []


def _positions(h):
    return np.concatenate([np.arange((2 * j + h) * 128, (2 * j + h + 1) * 128)
                           for j in range(NTB)])


def _rope0_fields(h):
    pos = _positions(h).astype(np.float64)
    inv = 1.0 / 10000.0 ** (np.arange(0, D_ROT, 2) / D_ROT)
    ang = pos[None, :] * inv[:, None]
    cf = np.ones((128, NT)); sf = np.zeros((128, NT))
    cf[0:32] = np.cos(ang); cf[32:64] = np.cos(ang)
    sf[0:32] = np.sin(ang); sf[32:64] = np.sin(ang)
    return cf.astype(BF), sf.astype(BF)


def _rope1_fields(h, scale):
    pos = _positions(h).astype(np.float64)
    inv = 10000.0 ** (-np.arange(0, HD, 2) / HD)
    ang = pos[None, :] * inv[:, None]
    cf = np.empty((128, NT)); sf = np.empty((128, NT))
    cf[0::2] = np.cos(ang); cf[1::2] = np.cos(ang)
    sf[0::2] = np.sin(ang); sf[1::2] = np.sin(ang)
    return (cf * scale).astype(BF), (sf * scale).astype(BF)


def _rot0T():
    r = np.zeros((128, 128), np.float32)
    for i in range(32):
        r[32 + i, i] = -1.0
        r[i, 32 + i] = 1.0
    return r.astype(BF)


def _rot1T():
    r = np.zeros((128, 128), np.float32)
    for i in range(64):
        r[2 * i + 1, 2 * i] = -1.0
        r[2 * i, 2 * i + 1] = 1.0
    return r.astype(BF)


def _masks(h):
    m = np.zeros((NTB, 2, 128, 128), np.float32)
    qi = np.arange(128); ki = np.arange(128)
    for j in range(NTB):
        for r in range(2):
            bad = ((2 * j + r) * 128 + ki[None, :]) > ((2 * j + h) * 128 + qi[:, None])
            m[j, r][bad] = NEG
    return m.astype(BF)


def build_program(dbg_stop=""):
    nc = bass.Bass("TRN2", target_bir_lowering=False)
    stack = contextlib.ExitStack()
    d = {}

    def din(name, shape, dtype):
        d[name] = nc.dram_tensor(name, shape, dtype, kind="ExternalInput").ap()
        return d[name]

    def dint(name, shape, dtype):
        d[name] = nc.dram_tensor(name, shape, dtype).ap()
        return d[name]

    def dshare(name, shape, dtype):
        d[name] = nc.dram_tensor(name, shape, dtype, addr_space="Shared").ap()
        return d[name]

    x_in = din("x_shard", [E, NT], F32)
    wc_in = din("W_c", [E, D_C], F32)
    wd_in = din("W_d", [D_C, 3 * E], F32)
    wo0_in = din("W_o0", [E, E], F32)
    wq_s = din("wq_s", [QKVO_R, E], F32)
    wk_s = din("wk_s", [QKVO_R, E], F32)
    wv_s = din("wv_s", [QKVO_R, E], F32)
    wo_s = din("wo_s", [QKVO_R, E], F32)
    wo0t_s = din("wo0t_s", [QKVO_R, E], F32)
    wgu_s = din("wgu_s", [WGU_R, E], F32)
    wdn_s = din("wdn_s", [WDN_R, INTER1], F32)
    gix_s = din("gix_s", [G_ROWS, E], I32)
    uix_s = din("uix_s", [G_ROWS, E], I32)
    dix_s = din("dix_s", [D_ROWS, INTER0], I32)
    lutvals = din("lutvals", [1, 2 * LUTN], F32)
    ident_in = din("ident", [128, 128], BF16)
    r0t_in = din("r0t", [128, 128], BF16)
    r1t_in = din("r1t", [128, 128], BF16)
    cf0_in = din("cf0", [128, NT], BF16)
    sf0_in = din("sf0", [128, NT], BF16)
    cf1q_in = din("cf1q", [128, NT], BF16)
    sf1q_in = din("sf1q", [128, NT], BF16)
    cf1k_in = din("cf1k", [128, NT], BF16)
    sf1k_in = din("sf1k", [128, NT], BF16)
    masks_in = din("masks", [NTB, 2, 128, 128], BF16)
    bc_in = din("bc_col", [128, 4], F32)
    bdqk_in = din("bdqk_col", [128, 32], F32)
    bdv_in = din("bdv_row", [1, E], F32)
    bo0_in = din("bo0_col", [128, NCH], F32)
    bq1_in = din("bq1_col", [128, NCH], F32)
    bk1_in = din("bk1_col", [128, NCH], F32)
    bv1_in = din("bv1_row", [1, E], F32)
    bo1_in = din("bo1_col", [128, NCH], F32)

    y_out = nc.dram_tensor("y", [E, NT], F32, kind="ExternalOutput").ap()

    gate_c = dint("gate_c", [G_ROWS, E], BF16)
    up_c = dint("up_c", [G_ROWS, E], BF16)
    down_c = dint("down_c", [D_ROWS, INTER0], BF16)
    gate_g = dshare("gate_g", [8 * G_ROWS, E], BF16)
    up_g = dshare("up_g", [8 * G_ROWS, E], BF16)
    down_g = dshare("down_g", [8 * D_ROWS, INTER0], BF16)
    dense_c = dint("dense_c", [4352, E], BF16)
    dense_g = dshare("dense_g", [8 * 4352, E], BF16)
    wo0_c = dint("wo0_c", [256, E], BF16)
    wo0_g = dshare("wo0_g", [8 * 256, E], BF16)
    kvk_c = [dint(f"kvk_c{l}", [HEADS * 128, NT], BF16) for l in range(2)]
    kvv_c = [dint(f"kvv_c{l}", [HEADS * NT, 128], BF16) for l in range(2)]
    kvk_g = [dint(f"kvk_g{l}", [2 * HEADS * 128, NT], BF16) for l in range(2)]
    kvv_g = [dint(f"kvv_g{l}", [2 * HEADS * NT, 128], BF16) for l in range(2)]

    sbufs = {}
    def sbuf(name, shape, dtype):
        t = stack.enter_context(nc.sbuf_tensor(name, shape, dtype))
        sbufs[name] = t
        return t

    x_sb = sbuf("x_sb", [128, NCH, NT], F32)
    xn_sb = sbuf("xn_sb", [128, NCH, NT], BF16)
    q_sb = sbuf("q_sb", [128, NCH, NT], BF16)
    ident = sbuf("ident_sb", [128, 128], BF16)
    r0t = sbuf("r0t_sb", [128, 128], BF16)
    r1t = sbuf("r1t_sb", [128, 128], BF16)
    kst = sbuf("kst", [128, NT], BF16)
    vst = sbuf("vst", [128, 512], BF16)
    vwt = sbuf("vwt", [128, 512], BF16)
    sg_sb = sbuf("sg_sb", [128, 512], BF16)
    wbuf = sbuf("wbuf", [128, 3, NCH, 128], BF16)
    xsq = sbuf("xsq", [128, 2, 512], F32)
    lnt = sbuf("lnt", [128, 512], F32)
    vbias = sbuf("vbias", [128, 512], F32)
    stat = sbuf("stat", [1, 3, 512], F32)
    mx = sbuf("mx", [128, 8], F32)
    lsum = sbuf("lsum", [128, 8], F32)
    bc_sb = sbuf("bc_sb", [128, 4], F32)
    bdqk_sb = sbuf("bdqk_sb", [128, 32], F32)
    bdqs_sb = sbuf("bdqs_sb", [128, 16], F32)
    bo0_sb = sbuf("bo0_sb", [128, NCH], F32)
    bq1_sb = sbuf("bq1_sb", [128, NCH], F32)
    bk1_sb = sbuf("bk1_sb", [128, NCH], F32)
    bo1_sb = sbuf("bo1_sb", [128, NCH], F32)
    lutrow = sbuf("lutrow", [1, 2 * LUTN], F32)
    onesr = sbuf("onesr", [1, 128], F32)
    onesrb = sbuf("onesrb", [1, 128], BF16)
    onescol = sbuf("onescol", [128, 1], F32)
    epscol = sbuf("epscol", [128, 1], F32)
    lutb = sbuf("lutb", [128, LUTN], F32)
    dec_i = sbuf("dec_i", [128, 512], I32)
    AR = 22528
    abf = sbuf("abf", [128, AR], BF16)

    ps32 = stack.enter_context(nc.psum_tensor("ps32", [128, 3584], F32))
    psbf = stack.enter_context(nc.psum_tensor("psbf", [128, 1024], BF16))

    def pb(b):
        return ps32[:, 512 * b:512 * (b + 1)]

    def carve(off, *dims):
        n = int(np.prod(dims))
        ap = abf[:, off:off + n]
        if len(dims) > 1:
            pat = " ".join(f"d{i}" for i in range(len(dims)))
            ap = ap.rearrange(f"p ({pat}) -> p {pat}",
                              **{f"d{i}": dims[i] for i in range(len(dims))})
        return ap, off + n

    dbg_outs = []

    with stack:
        with nc.Block():
            em = Em(nc, stack)
            R = defaultdict(Res)
            bank = [R[f"bank{i}"] for i in range(7)]
            slotbf = [R[f"psbf{i}"] for i in range(8)]
            xr = [R[f"x{c}"] for c in range(NCH)]
            xnr = [R[f"xn{c}"] for c in range(NCH)]
            qr = [R[f"q{c}"] for c in range(NCH)]
            hr = [R[f"h{c}"] for c in range(NCH)]
            osr = [R[f"os{c}"] for c in range(NCH)]
            ptr = [R[f"pt{k}"] for k in range(16)]

            stop = PHASES.index(dbg_stop) if dbg_stop in PHASES else 99

            def dump(nm, shape, dt, ap, rds):
                o = nc.dram_tensor(nm, shape, dt, kind="ExternalOutput").ap()
                idxall = tuple(slice(None) for _ in shape)
                em.dma("qg", o[idxall], ap, reads=rds, writes=[R["dump" + nm]])
                dbg_outs.append(nm)

            em.dma("qs", x_sb[:, :, :], x_in.rearrange("(c p) t -> p c t", p=128),
                   writes=xr, lane="lx")
            for nm, dst, src in [("ident", ident, ident_in), ("r0t", r0t, r0t_in),
                                 ("r1t", r1t, r1t_in), ("bc", bc_sb, bc_in),
                                 ("bdqk", bdqk_sb, bdqk_in), ("bo0", bo0_sb, bo0_in),
                                 ("bq1", bq1_sb, bq1_in), ("bk1", bk1_sb, bk1_in),
                                 ("bo1", bo1_sb, bo1_in), ("lutrow", lutrow, lutvals)]:
                em.dma("qs", dst[:, :], src[:, :], writes=[R[nm]], lane="l" + nm)
            em.op("pool", lambda e: e.memset(onesr[:, :], 1.0), writes=[R["onesr"]])
            em.op("pool", lambda e: e.memset(onesrb[:, :], 1.0), writes=[R["onesrb"]])
            em.op("pool", lambda e: e.memset(onescol[:, :], 1.0), writes=[R["onescol"]])
            em.op("pool", lambda e: e.memset(epscol[:, :], EPS), writes=[R["eps"]])
            em.op("dve", lambda e: e.tensor_scalar(bdqs_sb[:, :], bdqk_sb[:, 0:16],
                  SCALE, None, ALU.mult), reads=[R["bdqk"]], writes=[R["bdqs"]])
            ga = Acc(em, bank[6], 1)
            ga.add(ps32[:, 6 * 512:6 * 512 + LUTN], onesr[0:1, :], lutrow[0:1, 0:LUTN],
                   [R["onesr"], R["lutrow"]])
            em.op("dve", lambda e: e.tensor_copy(lutb[:, :],
                  ps32[:, 6 * 512:6 * 512 + LUTN]), reads=[bank[6]],
                  writes=[R["lutb"]])

            dcw = R["dense_c"]
            dc_flat = dense_c.rearrange("a b -> (a b)")
            o0 = 0
            stgi_ap, o0 = carve(o0, 4, E)
            stgo_ap, o0 = carve(o0, 4, NCH, 128)
            prep_bi = [0]

            def prep_block(src_slice, dst_off, dstf=None, wres=None):
                s = prep_bi[0] % 4
                prep_bi[0] += 1
                sir, sor = R[f"stgi{s}"], R[f"stgo{s}"]
                em.dma("qg", stgi_ap[:, s, :], src_slice, writes=[sir],
                       lane=f"lcin{s}")
                for ec in range(NCH):
                    sl_ = ec % 8
                    em.op("pe", lambda e, ec=ec, sl_=sl_, s=s: e.transpose(
                        psbf[:, sl_ * 128:(sl_ + 1) * 128],
                        stgi_ap.rearrange("p t (a b) -> p t a b", a=NCH)[:, s, ec, :],
                        ident[:, :]),
                        reads=[sir, R["ident"]], writes=[slotbf[sl_]])
                    if sl_ == 7:
                        em.op("dve", lambda e, ec=ec, s=s: e.tensor_copy(
                            stgo_ap[:, s, ec - 7:ec + 1, :],
                            psbf.ap().rearrange("p (kb q) -> p kb q", kb=8)),
                            reads=[slotbf[i] for i in range(8)],
                            writes=[sor])
                em.dma("qs", (dc_flat if dstf is None else dstf)
                       [dst_off:dst_off + BLK]
                       .rearrange("(p a b) -> p a b", p=128, a=NCH),
                       stgo_ap[:, s, :, :], reads=[sor],
                       writes=[dcw if wres is None else wres],
                       lane=f"lcout{s}")

            wo0c_flat = wo0_c.rearrange("a b -> (a b)")
            for b in range(2):
                prep_block(wq_s[b * 128:(b + 1) * 128, :], OFF_WQ_B + b * BLK)
                prep_block(wk_s[b * 128:(b + 1) * 128, :], OFF_WK_B + b * BLK)
                prep_block(wo_s[b * 128:(b + 1) * 128, :], OFF_WO_B + b * BLK)
                prep_block(wo0t_s[b * 128:(b + 1) * 128, :], b * BLK,
                           dstf=wo0c_flat, wres=R["wo0_c"])
            for j in range(16):
                prep_block(wgu_s[j * 128:(j + 1) * 128, :], OFF_WGU_B + j * BLK)
            for gi in range(4):
                for b in range(2):
                    prep_block(wdn_s[b * 128:(b + 1) * 128, gi * E:(gi + 1) * E],
                               OFF_WDN_B + (gi * 2 + b) * BLK)
            vv = dc_flat[OFF_WV_B:OFF_WV_B + 16 * 128 * 256]\
                .rearrange("(ec p f2) -> p ec f2", ec=16, p=128)
            for fr in range(2):
                s = prep_bi[0] % 4
                prep_bi[0] += 1
                sir, sor = R[f"stgi{s}"], R[f"stgo{s}"]
                em.dma("qg", stgi_ap[:, s, :], wv_s[fr * 128:(fr + 1) * 128, :],
                       writes=[sir], lane=f"lcin{s}")
                for ec in range(NCH):
                    sl_ = ec % 8
                    em.op("pe", lambda e, ec=ec, sl_=sl_, s=s: e.transpose(
                        psbf[:, sl_ * 128:(sl_ + 1) * 128],
                        stgi_ap.rearrange("p t (a b) -> p t a b", a=NCH)[:, s, ec, :],
                        ident[:, :]),
                        reads=[sir, R["ident"]], writes=[slotbf[sl_]])
                    if sl_ == 7:
                        em.op("dve", lambda e, ec=ec, s=s: e.tensor_copy(
                            stgo_ap[:, s, ec - 7:ec + 1, :],
                            psbf.ap().rearrange("p (kb q) -> p kb q", kb=8)),
                            reads=[slotbf[i] for i in range(8)],
                            writes=[sor])
                em.dma("qs", vv[:, :, fr * 128:(fr + 1) * 128], stgo_ap[:, s, :, :],
                       reads=[sor], writes=[dcw], lane=f"lcout{s}")
            em.barrier()

            def ln():
                for th in range(2):
                    sl = slice(512 * th, 512 * (th + 1))
                    a1 = Acc(em, bank[4], NCH)
                    a2 = Acc(em, bank[5], NCH)
                    for ec in range(NCH):
                        em.op("act", lambda e, ec=ec: e.activation(
                            xsq[:, ec % 2, :], x_sb[:, ec, sl], AF.Square),
                            reads=[xr[ec]], writes=[R[f"xsq{ec % 2}"]])
                        a1.add(pb(4)[0:1, :], onescol[:, :], x_sb[:, ec, sl],
                               [R["onescol"], xr[ec]])
                        a2.add(pb(5)[0:1, :], onescol[:, :], xsq[:, ec % 2, :],
                               [R["onescol"], R[f"xsq{ec % 2}"]])
                    em.op("dve", lambda e: e.tensor_scalar(stat[:, 0, :], pb(4)[0:1, :],
                          INV_E, None, ALU.mult), reads=[bank[4]], writes=[R["st0"]])
                    em.op("dve", lambda e: e.tensor_scalar(stat[:, 1, :], pb(5)[0:1, :],
                          INV_E, None, ALU.mult), reads=[bank[5]], writes=[R["st1"]])
                    em.op("dve", lambda e: e.tensor_tensor(stat[:, 2, :], stat[:, 0, :],
                          stat[:, 0, :], ALU.mult), reads=[R["st0"]], writes=[R["st2"]])
                    em.op("dve", lambda e: e.tensor_tensor(stat[:, 1, :], stat[:, 1, :],
                          stat[:, 2, :], ALU.subtract), reads=[R["st1"], R["st2"]],
                          writes=[R["st1"]])
                    em.op("act", lambda e: e.activation(stat[:, 1, :], stat[:, 1, :],
                          AF.Ln, bias=epscol[0:1, :]), reads=[R["st1"], R["eps"]],
                          writes=[R["st1"]])
                    em.op("act", lambda e: e.activation(stat[:, 1, :], stat[:, 1, :],
                          AF.Exp, scale=-0.5), reads=[R["st1"]], writes=[R["st1"]])
                    em.op("dve", lambda e: e.scalar_tensor_tensor(stat[:, 2, :],
                          stat[:, 0, :], -1.0, stat[:, 1, :], ALU.mult, ALU.mult),
                          reads=[R["st0"], R["st1"]], writes=[R["st2"]])
                    aa = Acc(em, bank[4], 1)
                    aa.add(pb(4), onesr[0:1, :], stat[0:1, 1, :], [R["onesr"], R["st1"]])
                    ab = Acc(em, bank[5], 1)
                    ab.add(pb(5), onesr[0:1, :], stat[0:1, 2, :], [R["onesr"], R["st2"]])
                    for ec in range(NCH):
                        em.op("dve", lambda e, ec=ec: e.tensor_tensor(
                            lnt[:, :], x_sb[:, ec, sl], pb(4), ALU.mult),
                            reads=[xr[ec], bank[4]], writes=[R["lnt"]])
                        em.op("dve", lambda e, ec=ec: e.tensor_tensor(
                            xn_sb[:, ec, sl], lnt[:, :], pb(5), ALU.add),
                            reads=[R["lnt"], bank[5]], writes=[xnr[ec]])

            o = 0
            wc_ap, o = carve(o, NCH, D_C)
            wdq_ap, o = carve(o, 4, 1536)
            c_ap, o = carve(o, 4, NT)
            cf0s_ap, o = carve(o, 2, NT)
            bdvrow_ap, o = carve(o, E)
            assert o <= AR

            ln()
            if stop <= 0:
                dump("xn_d", [128, NCH, NT], BF16, xn_sb[:, :, :], list(xnr))

            if stop > 0:
                cr = [R[f"c{c}"] for c in range(4)]
                em.dma("qg", wc_ap, wc_in.rearrange("(c p) f -> p c f", p=128),
                       writes=[R["wc"]], lane="lwc")
                em.dma("qs", cf0s_ap[:, 0, :], cf0_in[:, :], writes=[R["cf0"]], lane="lcf")
                em.dma("qs", cf0s_ap[:, 1, :], sf0_in[:, :], writes=[R["sf0"]], lane="lsf")
                em.dma("qg", bdvrow_ap[0:1, :], bdv_in[:, :], writes=[R["bdv"]], lane="lbdv")
                for fc in range(4):
                    for th in range(2):
                        sl = slice(512 * th, 512 * (th + 1))
                        bk = 5 + (fc + th) % 2
                        a = Acc(em, bank[bk], NCH)
                        for ec in range(NCH):
                            a.add(pb(bk), wc_ap[:, ec, fc * 128:(fc + 1) * 128],
                                  xn_sb[:, ec, sl], [R["wc"], xnr[ec]])
                        em.op("act", lambda e, fc=fc, sl=sl, bk=bk: e.activation(
                            c_ap[:, fc, sl], pb(bk), AF.Identity, bias=bc_sb[:, fc:fc + 1]),
                            reads=[R["bc"], bank[bk]], writes=[cr[fc]])
                for th in range(2):
                    sl = slice(512 * th, 512 * (th + 1))
                    a = Acc(em, bank[4], 1)
                    a.add(pb(4), r0t[:, :], c_ap[:, 0, sl], [R["r0t"], cr[0]])
                    em.op("dve", lambda e, sl=sl: e.tensor_tensor(
                        pb(3), pb(4), cf0s_ap[:, 1, sl], ALU.mult),
                        reads=[bank[4], R["sf0"]], writes=[bank[3]])
                    em.op("dve", lambda e, sl=sl: e.tensor_tensor(
                        sg_sb[:, :], c_ap[:, 0, sl], cf0s_ap[:, 0, sl], ALU.mult),
                        reads=[cr[0], R["cf0"]], writes=[R["sg0"]])
                    em.op("dve", lambda e, sl=sl: e.tensor_tensor(
                        c_ap[:, 0, sl], sg_sb[:, :], pb(3), ALU.add),
                        reads=[R["sg0"], bank[3]], writes=[cr[0]])

                def wd_cols(qd):
                    em.dma("qg", wdq_ap,
                           wd_in.rearrange("(c p) f -> p c f", p=128)
                           [:, :, qd * 1536:(qd + 1) * 1536], writes=[R["wd"]], lane="lwd")

                def hcol(fcol, width, evac):
                    qd = fcol // 1536
                    loc = fcol % 1536
                    for th in range(2):
                        sl = slice(512 * th, 512 * (th + 1))
                        bk = 5 + th % 2
                        a = Acc(em, bank[bk], 4)
                        for cc in range(4):
                            a.add(pb(bk), wdq_ap[:, cc, loc:loc + 128],
                                  c_ap[:, cc, sl], [R["wd"], cr[cc]])
                        evac(bk, th, sl)

                wd_cols(0)
                for hd in range(12):
                    hcol(hd * 128, 128, lambda bk, th, sl, hd=hd: em.op(
                        "act", lambda e: e.activation(q_sb[:, hd, sl], pb(bk), AF.Identity,
                                                      bias=bdqs_sb[:, hd:hd + 1],
                                                      scale=SCALE),
                        reads=[R["bdqs"], bank[bk]], writes=[qr[hd]]))
                wd_cols(1)
                for hd in range(12, 16):
                    hcol(hd * 128, 128, lambda bk, th, sl, hd=hd: em.op(
                        "act", lambda e: e.activation(q_sb[:, hd, sl], pb(bk), AF.Identity,
                                                      bias=bdqs_sb[:, hd:hd + 1],
                                                      scale=SCALE),
                        reads=[R["bdqs"], bank[bk]], writes=[qr[hd]]))
                def k_evac(hd):
                    kres = R["kst"]
                    def f(bk, th, sl):
                        em.op("act", lambda e: e.activation(
                            kst[:, sl], pb(bk), AF.Identity,
                            bias=bdqk_sb[:, 16 + hd:17 + hd]),
                            reads=[R["bdqk"], bank[bk]], writes=[kres])
                        if th == 1:
                            em.dma("qg", kvk_c[0][hd * 128:(hd + 1) * 128, :],
                                   kst[:, :], reads=[kres], writes=[R["kvk_c0"]], lane=f"lkvc{next(_kvrr)}")
                    return f
                for hd in range(8):
                    hcol(2048 + hd * 128, 128, k_evac(hd))
                wd_cols(2)
                for hd in range(8, 16):
                    hcol(2048 + hd * 128, 128, k_evac(hd))
                def vpart(fv):
                    qd = (4096 + fv * 512) // 1536
                    loc = (4096 + fv * 512) % 1536
                    a = Acc(em, bank[6], 1)
                    a.add(pb(6), onesrb[0:1, :], bdvrow_ap[0:1, fv * 512:(fv + 1) * 512],
                          [R["onesrb"], R["bdv"]])
                    em.op("dve", lambda e: e.tensor_copy(vbias[:, :], pb(6)),
                          reads=[bank[6]], writes=[R["vb0"]])
                    for ts_ in range(8):
                        vres = R["vst"]
                        bk = 4 + ts_ % 2
                        a = Acc(em, bank[bk], 4)
                        for cc in range(4):
                            a.add(pb(bk), c_ap[:, cc, ts_ * 128:(ts_ + 1) * 128],
                                  wdq_ap[:, cc, loc:loc + 512], [cr[cc], R["wd"]])
                        em.op("dve", lambda e, bk=bk: e.tensor_tensor(
                            vst[:, :], pb(bk), vbias[:, :], ALU.add),
                            reads=[bank[bk], R["vb0"]], writes=[vres])
                        for hl in range(4):
                            hd = fv * 4 + hl
                            em.dma("qg",
                                   kvv_c[0][hd * NT + ts_ * 128: hd * NT + (ts_ + 1) * 128, :],
                                   vst[:, hl * 128:(hl + 1) * 128],
                                   reads=[vres], writes=[R["kvv_c0"]], lane=f"lkvc{next(_kvrr)}")
                vpart(0)
                wd_cols(3)
                for fv in range(1, 4):
                    vpart(fv)

                em.collective("AllGather", ALU.bypass, [[0, 1], [2, 3], [4, 5], [6, 7]],
                              [kvk_c[0].opt()], [kvk_g[0].opt()],
                              reads=[R["kvk_c0"]], writes=[R["kvk_g0"]])
                em.collective("AllGather", ALU.bypass, [[0, 1], [2, 3], [4, 5], [6, 7]],
                              [kvv_c[0].opt()], [kvv_g[0].opt()],
                              reads=[R["kvv_c0"]], writes=[R["kvv_g0"]])
                em.collective("AllGather", ALU.bypass, [list(range(8))],
                              [wo0_c.opt()], [wo0_g.opt()],
                              reads=[R["wo0_c"]], writes=[R["wo0_done"]])
                em.collective("AllGather", ALU.bypass, [list(range(8))],
                              [dense_c.opt()], [dense_g.opt()],
                              reads=[dcw], writes=[R["dense_done"]])
            if stop == 1:
                dump("q_d", [128, NCH, NT], BF16, q_sb[:, :, :], list(qr))
                dump("c_d", [128, 4, NT], BF16, c_ap, [R[f"c{c}"] for c in range(4)])

            em.barrier()
            o = 0
            decb_ap, o = carve(o, 512)
            dect_ap, o = carve(o, 512)
            deca_ap, o = carve(o, 512)
            dstg_ap, o = carve(o, 4, 128)
            kh_ap, o = carve(o, 2 * NT)
            vh_ap, o = carve(o, 2, 16, 128)
            p_ap, o = carve(o, 2, 2 * NT)
            pt_ap, o = carve(o, 16, 128)
            osb_ap, o = carve(o, NCH, 512)
            assert o <= AR, o

            gate_f = gate_c.rearrange("a b -> (a b)")
            up_f = up_c.rearrange("a b -> (a b)")
            down_f = down_c.rearrange("a b -> (a b)")

            def decode_steps():
                tiles = []
                for t in range(6):
                    tiles.append(("g", gix_s, gate_f, t, E))
                for t in range(6):
                    tiles.append(("u", uix_s, up_f, t, E))
                for t in range(2):
                    tiles.append(("d", dix_s, down_f, t, INTER0))
                for (which, isrc, dstf, t, width) in tiles:
                    nsub = (width + 511) // 512
                    nic = width // 128
                    icq = []
                    for su in range(nsub):
                        w = min(512, width - su * 512)
                        ires = R["dec_i"]
                        ares = R["dec_a"]
                        def load(isrc=isrc, t=t, su=su, w=w, ires=ires):
                            em.dma("qs", dec_i[:, 0:w],
                                   isrc[t * 128:(t + 1) * 128, su * 512:su * 512 + w],
                                   writes=[ires], lane="ldeci")
                        yield load
                        def conv(w=w, ires=ires):
                            em.op("dve", lambda e: e.tensor_copy(
                                decb_ap[:, 0:w], dec_i[:, 0:w]),
                                reads=[ires], writes=[R["dec_b"]])
                        yield conv
                        for v in range(LUTN):
                            def step(v=v, w=w, ares=ares):
                                if v == 0:
                                    em.op("dve", lambda e: e.tensor_scalar(
                                        deca_ap[:, 0:w], decb_ap[:, 0:w],
                                        0.0, lutb[:, 0:1],
                                        ALU.is_equal, ALU.mult),
                                        reads=[R["dec_b"], R["lutb"]],
                                        writes=[ares])
                                else:
                                    em.op("dve", lambda e: e.tensor_scalar(
                                        dect_ap[:, 0:w], decb_ap[:, 0:w],
                                        float(v), lutb[:, v:v + 1],
                                        ALU.is_equal, ALU.mult),
                                        reads=[R["dec_b"], R["lutb"]],
                                        writes=[R["dec_t"]])
                                    em.op("dve", lambda e: e.tensor_tensor(
                                        deca_ap[:, 0:w], deca_ap[:, 0:w],
                                        dect_ap[:, 0:w], ALU.add),
                                        reads=[R["dec_t"]], writes=[ares])
                            yield step
                        nj = w // 128
                        ic0 = su * 4
                        if which in ("g", "u"):
                            dst = dstf[t * BLK:(t + 1) * BLK]\
                                .rearrange("(p a b) -> p a b", p=128, a=NCH)\
                                [:, ic0:ic0 + nj, :]
                        else:
                            gi = ic0 // 16
                            ii0 = ic0 - 16 * gi
                            base = t * 704512 + DOWN_GRP_OFF[gi]
                            glen = len(L0_GRPS[gi])
                            dst = dstf[base:base + 128 * glen * 128]\
                                .rearrange("(p a b) -> p a b", p=128, a=glen)\
                                [:, ii0:ii0 + nj, :]
                        def tpflush(nj=nj, dst=dst, ares=ares):
                            for j in range(nj):
                                em.op("pe", lambda e, j=j: e.transpose(
                                    psbf[:, j * 128:(j + 1) * 128],
                                    deca_ap[:, j * 128:(j + 1) * 128], ident[:, :]),
                                    reads=[ares, R["ident"]], writes=[slotbf[j]])
                            em.op("dve", lambda e: e.tensor_copy(
                                dstg_ap[:, 0:nj, :],
                                psbf.ap().rearrange("p (kb q) -> p kb q", kb=8)
                                [:, 0:nj, :]),
                                reads=[slotbf[i] for i in range(nj)],
                                writes=[R["dstg"]])
                            em.dma("qs", dst, dstg_ap[:, 0:nj, :],
                                   reads=[R["dstg"]],
                                   writes=[R["gh_cw"]], lane="lghc")
                        yield tpflush
                    if (which, t) == ("g", 5):
                        def cc():
                            em.collective("AllGather", ALU.bypass, [list(range(8))],
                                          [gate_c.opt()], [gate_g.opt()],
                                          reads=[R["gh_cw"]], writes=[R["gate_done"]])
                        yield cc
                    if (which, t) == ("u", 5):
                        def cc():
                            em.collective("AllGather", ALU.bypass, [list(range(8))],
                                          [up_c.opt()], [up_g.opt()],
                                          reads=[R["gh_cw"]], writes=[R["up_done"]])
                        yield cc
                    if (which, t) == ("d", 1):
                        def cc():
                            em.collective("AllGather", ALU.bypass, [list(range(8))],
                                          [down_c.opt()], [down_g.opt()],
                                          reads=[R["gh_cw"]], writes=[R["down_done"]])
                        yield cc

            dec_gen = decode_steps() if stop > 1 else iter(())

            def pump(k):
                for _ in range(k):
                    s = next(dec_gen, None)
                    if s is None:
                        return
                    s()

            def attention(layer):
                kvkg, kvvg = kvk_g[layer], kvv_g[layer]
                gres_k, gres_v = R[f"kvk_g{layer}"], R[f"kvv_g{layer}"]
                maskres = R["masks"]

                def kv_load(u):
                    g_, hd_ = divmod(u, HEADS)
                    sb_ = u % 2
                    for r in range(2):
                        em.dma("qa", kh_ap[:, r * NT:(r + 1) * NT],
                               kvkg[(r * HEADS + hd_) * 128:(r * HEADS + hd_ + 1) * 128, :],
                               reads=[gres_k], writes=[R["kh"]],
                               lane="lkh", group=("kh", layer, g_, hd_))
                        em.dma("qa", vh_ap[:, sb_, r * 8:(r + 1) * 8, :],
                               kvvg[(r * HEADS + hd_) * NT:(r * HEADS + hd_ + 1) * NT, :]
                               .rearrange("(kb p) dd -> p kb dd", p=128),
                               reads=[gres_v], writes=[R[f"vh{sb_}"]],
                               lane=f"lvh{sb_}", group=("vh", layer, g_, hd_))

                kv_load(0)
                for g in range(2):
                    for hd in range(HEADS):
                        u = g * HEADS + hd
                        sb = u % 2
                        khres, vhres = R["kh"], R[f"vh{sb}"]

                        def s_stage(qs_):
                            j = g * 4 + qs_
                            nblk = (j + 1) if layer == 1 else 8
                            L = 2 * nblk * 128
                            qt = q_sb[:, hd, j * 128:(j + 1) * 128]
                            pieces = []
                            for r in range(2):
                                seg0 = r * nblk * 128
                                rem = nblk * 128
                                off = 0
                                while rem > 0:
                                    col = seg0 + off
                                    n = min(512 - col % 512, rem, 512)
                                    pieces.append((r, off, col, n))
                                    off += n
                                    rem -= n
                            for (r, off, col, n) in pieces:
                                b0 = col // 512
                                a = Acc(em, bank[b0], 1)
                                a.add(ps32[:, col:col + n], qt,
                                      kh_ap[:, r * NT + off: r * NT + off + n],
                                      [qr[hd], khres])
                            return nblk, L

                        def softmax_stage(qs_, nblk, L):
                            j = g * 4 + qs_
                            pp = qs_ % 2
                            nbk = (L + 511) // 512
                            if layer == 1:
                                for r in range(2):
                                    cm = (nblk * (r + 1) - 1) * 128
                                    em.op("dve", lambda e, cm=cm, r=r, j=j: e.tensor_tensor(
                                        ps32[:, cm:cm + 128], ps32[:, cm:cm + 128],
                                        masks_ap[:, j * 2 + r, :], ALU.add),
                                        reads=[maskres, bank[cm // 512]],
                                        writes=[bank[cm // 512]])
                            for b0 in range(nbk):
                                n = min(512, L - b0 * 512)
                                em.op("dve", lambda e, b0=b0, n=n: e.reduce_max(
                                    mx[:, b0:b0 + 1], ps32[:, b0 * 512:b0 * 512 + n],
                                    axis=AX.X), reads=[bank[b0]], writes=[R["mx"]])
                            em.op("dve", lambda e, nbk=nbk: e.reduce_max(
                                mx[:, 6:7], mx[:, 0:nbk], axis=AX.X),
                                reads=[R["mx"]], writes=[R["mx6"]])
                            em.op("dve", lambda e: e.tensor_scalar(
                                mx[:, 7:8], mx[:, 6:7], -1.0, None, ALU.mult),
                                reads=[R["mx6"]], writes=[R["mx7"]])
                            for b0 in range(nbk):
                                n = min(512, L - b0 * 512)
                                em.op("act", lambda e, b0=b0, n=n, pp=pp: e.activation(
                                    p_ap[:, pp, b0 * 512:b0 * 512 + n],
                                    ps32[:, b0 * 512:b0 * 512 + n], AF.Exp,
                                    bias=mx[:, 7:8], accum_out=lsum[:, b0:b0 + 1]),
                                    reads=[R["mx7"], bank[b0]],
                                    writes=[R[f"p{pp}"], R["ls"]])
                            em.op("dve", lambda e, nbk=nbk: e.reduce_sum(
                                lsum[:, 6:7], lsum[:, 0:nbk], axis=AX.X),
                                reads=[R["ls"]], writes=[R["ls6"]])
                            em.op("dve", lambda e: e.reciprocal(lsum[:, 7:8],
                                  lsum[:, 6:7]), reads=[R["ls6"]], writes=[R["ls7"]])
                            em.op("act", lambda e, L=L, pp=pp: e.activation(
                                p_ap[:, pp, 0:L], p_ap[:, pp, 0:L], AF.Identity,
                                scale=lsum[:, 7:8]),
                                reads=[R["ls7"]], writes=[R[f"p{pp}"]])

                        def pe_stage(qs_, nblk):
                            pp = qs_ % 2
                            nkb = 2 * nblk
                            for kb in range(nkb):
                                sl_ = kb % 8
                                em.op("pe", lambda e, kb=kb, sl_=sl_, pp=pp: e.transpose(
                                    psbf[:, sl_ * 128:(sl_ + 1) * 128],
                                    p_ap[:, pp, kb * 128:(kb + 1) * 128], ident[:, :]),
                                    reads=[R[f"p{pp}"], R["ident"]], writes=[slotbf[sl_]])
                                if sl_ == 7 or kb == nkb - 1:
                                    k0 = kb - sl_
                                    em.op("dve", lambda e, k0=k0, sl_=sl_, kb=kb:
                                          e.tensor_copy(
                                              pt_ap[:, k0:kb + 1, :],
                                              psbf.ap().rearrange(
                                                  "p (kb qq) -> p kb qq", kb=8)
                                              [:, 0:sl_ + 1, :]),
                                          reads=[slotbf[i] for i in range(sl_ + 1)],
                                          writes=[ptr[i] for i in range(k0, kb + 1)])
                            a = Acc(em, bank[4], nkb)
                            for kb in range(nkb):
                                r, loc = divmod(kb, nblk)
                                vb = r * 8 + loc
                                a.add(ps32[:, 4 * 512:4 * 512 + 128],
                                      vh_ap[:, sb, vb, :], pt_ap[:, kb, :],
                                      [vhres, ptr[kb]])
                            em.op("act", lambda e, hd=hd, qs_=qs_: e.activation(
                                osb_ap[:, hd, qs_ * 128:(qs_ + 1) * 128],
                                ps32[:, 4 * 512:4 * 512 + 128], AF.Copy),
                                reads=[bank[4]], writes=[osr[hd]])

                        meta = {}
                        meta[0] = s_stage(0)
                        softmax_stage(0, *meta[0])
                        for j in range(1, 4):
                            meta[j] = s_stage(j)
                            if j == 3 and u + 1 < 2 * HEADS:
                                kv_load(u + 1)
                            pe_stage(j - 1, meta[j - 1][0])
                            softmax_stage(j, *meta[j])
                            if layer == 0:
                                pump(11)
                        pe_stage(3, meta[3][0])
                        if layer == 0:
                            pump(12)
                    tsl = slice(g * 512, (g + 1) * 512)
                    if True:
                        wo_bias = bo0_sb if layer == 0 else bo1_sb
                        wo_bres = R["bo0"] if layer == 0 else R["bo1"]
                        def wo_fill(fc, q):
                            wres = R[f"wb{fc % 2}"]
                            rank, b = fc // 2, fc % 2
                            if layer == 0:
                                flat = wo0_g.rearrange("a b -> (a b)")
                                off = rank * 2 * BLK + b * BLK
                                rd = R["wo0_done"]
                            else:
                                flat = dense_g.rearrange("a b -> (a b)")
                                off = rank * DENSE_SEG + OFF_WO_B + b * BLK
                                rd = R["dense_done"]
                            em.dma(q, wbuf[:, fc % 2, :, :],
                                   flat[off:off + BLK].rearrange(
                                       "(p a c) -> p a c", p=128, a=NCH),
                                   reads=[rd],
                                   writes=[wres], lane=f"lwb{fc % 2}",
                                   group=("wo", layer, g, fc))
                            return wres
                        wo_fill(0, "qs")
                        for fc in range(NCH):
                            wres = R[f"wb{fc % 2}"]
                            if fc + 1 < NCH:
                                wo_fill(fc + 1, "qs")
                            a = Acc(em, bank[5], NCH)
                            for ec in range(NCH):
                                a.add(pb(5), wbuf[:, fc % 2, ec, :], osb_ap[:, ec, :],
                                      [wres, osr[ec]])
                            em.op("dve", lambda e, fc=fc, wo_bias=wo_bias,
                                  wo_bres=wo_bres: e.scalar_tensor_tensor(
                                x_sb[:, fc, tsl], pb(5), wo_bias[:, fc:fc + 1],
                                x_sb[:, fc, tsl], ALU.add, ALU.add),
                                reads=[bank[5], wo_bres], writes=[xr[fc]])

            def ffn(layer, h_ap, wx_ap):
                hgroups = ([list(range(0, 16)), list(range(16, 32)), list(range(32, 43))]
                           if layer == 0 else
                           [list(range(g * 16, (g + 1) * 16)) for g in range(4)])

                def slotap(s):
                    return wbuf[:, s, :, :] if s < 3 else wx_ap[:, s - 3, :, :]

                def fill(which, ic, s, q):
                    wres = R[f"ws{s}"]
                    if which == "g":
                        if layer == 0:
                            rank, lo = ic // 6, ic % 6
                            flat = gate_g.rearrange("a b -> (a b)")
                            off = rank * G_ROWS * E + lo * BLK
                            rd = R["gate_done"]
                        else:
                            rank, lo = ic // 16, ic % 16
                            flat = dense_g.rearrange("a b -> (a b)")
                            off = rank * DENSE_SEG + OFF_WGU_B + lo * BLK
                            rd = R["dense_done"]
                        dst = slotap(s)[:, :, :]
                        srcap = flat[off:off + BLK].rearrange(
                            "(p a b) -> p a b", p=128, a=NCH)
                        em.dma(q, dst, srcap, reads=[rd], writes=[wres],
                               lane=f"lws{s}", group=("f", layer, which, str(ic)))
                        return wres
                    if which == "u":
                        if layer == 0:
                            rank, lo = ic // 6, ic % 6
                            flat = up_g.rearrange("a b -> (a b)")
                            off = rank * G_ROWS * E + lo * BLK
                            rd = R["up_done"]
                        else:
                            icu = 64 + ic
                            rank, lo = icu // 16, icu % 16
                            flat = dense_g.rearrange("a b -> (a b)")
                            off = rank * DENSE_SEG + OFF_WGU_B + lo * BLK
                            rd = R["dense_done"]
                        dst = slotap(s)[:, :, :]
                        srcap = flat[off:off + BLK].rearrange(
                            "(p a b) -> p a b", p=128, a=NCH)
                        em.dma(q, dst, srcap, reads=[rd], writes=[wres],
                               lane=f"lws{s}", group=("f", layer, which, str(ic)))
                        return wres
                    gi, fc, grp = ic
                    rank, b = fc // 2, fc % 2
                    glen = len(grp)
                    if layer == 0:
                        flat = down_g.rearrange("a b -> (a b)")
                        off = rank * DOWN_SEG + b * 704512 + DOWN_GRP_OFF[gi]
                        rd = R["down_done"]
                    else:
                        flat = dense_g.rearrange("a b -> (a b)")
                        off = rank * DENSE_SEG + OFF_WDN_B + (gi * 2 + b) * BLK
                        rd = R["dense_done"]
                    dst = slotap(s)[:, 0:glen, :]
                    srcap = flat[off:off + 128 * glen * 128].rearrange(
                        "(p a b) -> p a b", p=128, a=glen)
                    em.dma(q, dst, srcap, reads=[rd], writes=[wres],
                           lane=f"lws{s}", group=("f", layer, which, str(ic)))
                    return wres

                allc = [ic for grp in hgroups for ic in grp]
                fill("g", allc[0], 0, "qs")
                fill("u", allc[0], 2, "qs")
                k = 0
                for gi, grp in enumerate(hgroups):
                    for ii, ic in enumerate(grp):
                        if k + 1 < len(allc):
                            fill("g", allc[k + 1], (k + 1) % 2, "qs")
                            fill("u", allc[k + 1], 2 + (k + 1) % 2, "qs")
                        wg = R[f"ws{k % 2}"]
                        wu = R[f"ws{2 + k % 2}"]
                        gs = slotap(k % 2)
                        us = slotap(2 + k % 2)
                        for th in range(2):
                            sl = slice(512 * th, 512 * (th + 1))
                            a5 = Acc(em, bank[5], NCH)
                            for ec in range(NCH):
                                a5.add(pb(5), gs[:, ec, :], xn_sb[:, ec, sl],
                                       [wg, xnr[ec]])
                            a6 = Acc(em, bank[6], NCH)
                            for ec in range(NCH):
                                a6.add(pb(6), us[:, ec, :], xn_sb[:, ec, sl],
                                       [wu, xnr[ec]])
                            em.op("act", lambda e: e.activation(
                                sg_sb[:, :], pb(5), AF.Silu),
                                reads=[bank[5]], writes=[R["sg0"]])
                            em.op("dve", lambda e, ii=ii, sl=sl: e.tensor_tensor(
                                h_ap[:, ii, sl], pb(6), sg_sb[:, :], ALU.mult),
                                reads=[bank[6], R["sg0"]], writes=[hr[ii]])
                        k += 1
                    fill("d", (gi, 0, grp), 4, "qs")
                    for fc in range(NCH):
                        if fc + 1 < NCH:
                            fill("d", (gi, fc + 1, grp), 4 + (fc + 1) % 2, "qs")
                        wres = R[f"ws{4 + fc % 2}"]
                        ds = slotap(4 + fc % 2)
                        for th in range(2):
                            sl = slice(512 * th, 512 * (th + 1))
                            a = Acc(em, bank[4], len(grp))
                            for ii in range(len(grp)):
                                a.add(pb(4), ds[:, ii, :], h_ap[:, ii, sl],
                                      [wres, hr[ii]])
                            em.op("dve", lambda e, fc=fc, sl=sl: e.tensor_tensor(
                                x_sb[:, fc, sl], x_sb[:, fc, sl], pb(4), ALU.add),
                                reads=[bank[4]], writes=[xr[fc]])

            masks_ap = None
            if stop > 1:
                attention(0)
                pump(10 ** 9)
            if stop == 2:
                dump("x_d", [128, NCH, NT], F32, x_sb[:, :, :], list(xr))

            if stop > 2:
                em.barrier()
                o = 0
                h_ap, o = carve(o, NCH, NT)
                wx_ap, o = carve(o, 3, NCH, 128)
                ffn(0, h_ap, wx_ap)
            if stop == 3:
                dump("x_d", [128, NCH, NT], F32, x_sb[:, :, :], list(xr))

            if stop > 3:
                em.barrier()
                o = 0
                cf1_ap, o = carve(o, 4, NT)
                bv1row_ap, o = carve(o, E)
                vw2_ap, o = carve(o, 2, 512)
                ln()
                em.dma("qs", cf1_ap[:, 0, :], cf1q_in[:, :], writes=[R["cf1q"]], lane="lcf")
                em.dma("qs", cf1_ap[:, 1, :], sf1q_in[:, :], writes=[R["sf1q"]], lane="lsf")
                em.dma("qs", cf1_ap[:, 2, :], cf1k_in[:, :], writes=[R["cf1k"]], lane="lcf2")
                em.dma("qs", cf1_ap[:, 3, :], sf1k_in[:, :], writes=[R["sf1k"]], lane="lsf2")
                em.dma("qg", bv1row_ap[0:1, :], bv1_in[:, :], writes=[R["bv1"]], lane="lbdv")

                dg_flat = dense_g.rearrange("a b -> (a b)")
                LIN_OFF = {0: OFF_WQ_B, QKVO_R: OFF_WK_B, 3 * QKVO_R: OFF_WO_B}

                def lin_fill(base, fc, slot, q):
                    wres = R[f"wb{slot}"]
                    rank, b = fc // 2, fc % 2
                    off = rank * DENSE_SEG + LIN_OFF[base] + b * BLK
                    em.dma(q, wbuf[:, slot, :, :],
                           dg_flat[off:off + BLK].rearrange(
                               "(p a c) -> p a c", p=128, a=NCH),
                           reads=[R["dense_done"]], writes=[wres],
                           lane=f"lwb{slot}", group=("lin", base, fc))
                    return wres

                def lin_comp(fc, slot, out_fn):
                    wres = R[f"wb{slot}"]
                    for th in range(2):
                        sl = slice(512 * th, 512 * (th + 1))
                        bk = 5 + th % 2
                        a = Acc(em, bank[bk], NCH)
                        for ec in range(NCH):
                            a.add(pb(bk), wbuf[:, slot, ec, :], xn_sb[:, ec, sl],
                                  [wres, xnr[ec]])
                        out_fn(bk, th, sl)

                lin_fill(0, 0, 0, "qs")
                for fc in range(NCH):
                    def qevac(bk, th, sl, fc=fc):
                        em.op("act", lambda e: e.activation(
                            q_sb[:, fc, sl], pb(bk), AF.Identity,
                            bias=bq1_sb[:, fc:fc + 1]),
                            reads=[R["bq1"], bank[bk]], writes=[qr[fc]])
                    if fc + 1 < NCH:
                        lin_fill(0, fc + 1, (fc + 1) % 2, "qs")
                    lin_comp(fc, fc % 2, qevac)
                    for th in range(2):
                        sl = slice(512 * th, 512 * (th + 1))
                        a = Acc(em, bank[4], 1)
                        a.add(pb(4), r1t[:, :], q_sb[:, fc, sl], [R["r1t"], qr[fc]])
                        em.op("dve", lambda e, sl=sl: e.tensor_tensor(
                            pb(3), pb(4), cf1_ap[:, 1, sl], ALU.mult),
                            reads=[bank[4], R["sf1q"]], writes=[bank[3]])
                        em.op("dve", lambda e, fc=fc, sl=sl: e.tensor_tensor(
                            sg_sb[:, :], q_sb[:, fc, sl], cf1_ap[:, 0, sl], ALU.mult),
                            reads=[qr[fc], R["cf1q"]], writes=[R["sg0"]])
                        em.op("dve", lambda e, fc=fc, sl=sl: e.tensor_tensor(
                            q_sb[:, fc, sl], sg_sb[:, :], pb(3), ALU.add),
                            reads=[R["sg0"], bank[3]], writes=[qr[fc]])
                lin_fill(QKVO_R, 0, 0, "qs")
                for fc in range(NCH):
                    kres = R["kst"]
                    def kevac(bk, th, sl, fc=fc, kres=kres):
                        em.op("act", lambda e: e.activation(
                            kst[:, sl], pb(bk), AF.Identity,
                            bias=bk1_sb[:, fc:fc + 1]),
                            reads=[R["bk1"], bank[bk]], writes=[kres])
                    if fc + 1 < NCH:
                        lin_fill(QKVO_R, fc + 1, (fc + 1) % 2, "qs")
                    lin_comp(fc, fc % 2, kevac)
                    for th in range(2):
                        sl = slice(512 * th, 512 * (th + 1))
                        a = Acc(em, bank[4], 1)
                        a.add(pb(4), r1t[:, :], kst[:, sl], [R["r1t"], kres])
                        em.op("dve", lambda e, sl=sl: e.tensor_tensor(
                            pb(3), pb(4), cf1_ap[:, 3, sl], ALU.mult),
                            reads=[bank[4], R["sf1k"]], writes=[bank[3]])
                        em.op("dve", lambda e, sl=sl: e.tensor_tensor(
                            sg_sb[:, :], kst[:, sl], cf1_ap[:, 2, sl], ALU.mult),
                            reads=[kres, R["cf1k"]], writes=[R["sg0"]])
                        em.op("dve", lambda e, sl=sl: e.tensor_tensor(
                            kst[:, sl], sg_sb[:, :], pb(3), ALU.add),
                            reads=[R["sg0"], bank[3]], writes=[kres])
                    em.dma("qg", kvk_c[1][fc * 128:(fc + 1) * 128, :], kst[:, :],
                           reads=[kres], writes=[R["kvk_c1"]], lane=f"lkvc{next(_kvrr)}")
                em.collective("AllGather", ALU.bypass, [[0, 1], [2, 3], [4, 5], [6, 7]],
                              [kvk_c[1].opt()], [kvk_g[1].opt()],
                              reads=[R["kvk_c1"]], writes=[R["kvk_g1"]])
                for fv in range(4):
                    a = Acc(em, bank[2], 1)
                    a.add(pb(2), onesrb[0:1, :], bv1row_ap[0:1, fv * 512:(fv + 1) * 512],
                          [R["onesrb"], R["bv1"]])
                    em.op("dve", lambda e: e.tensor_copy(vbias[:, :], pb(2)),
                          reads=[bank[2]], writes=[R["vb0"]])
                    def v_wload(fv_, ec_):
                        s_ = ec_ % 2
                        vw_ = R[f"vw2{s_}"]
                        for rr in range(2):
                            rank = 2 * fv_ + rr
                            off = rank * DENSE_SEG + OFF_WV_B + ec_ * 32768
                            em.dma("qs", vw2_ap[:, s_, rr * 256:(rr + 1) * 256],
                                   dg_flat[off:off + 32768].rearrange(
                                       "(p f2) -> p f2", p=128),
                                   reads=[R["dense_done"]],
                                   writes=[vw_], lane=f"lvw2{s_}",
                                   group=("vw", fv_, ec_))
                        return vw_
                    for half in range(2):
                        v_wload(fv, 0)
                        for ec in range(NCH):
                            vw = R[f"vw2{ec % 2}"]
                            if ec + 1 < NCH:
                                v_wload(fv, ec + 1)
                            for tg in range(4):
                                ts_ = half * 4 + tg
                                bk = 3 + tg
                                em._deps("pe", [vw, xnr[ec]],
                                         [bank[bk]] if ec == 0 else [])
                                instr = em.eng["pe"].matmul(
                                    pb(bk), xn_sb[:, ec, ts_ * 128:(ts_ + 1) * 128],
                                    vw2_ap[:, ec % 2, :], start=(ec == 0),
                                    stop=(ec == NCH - 1))
                                instr.then_inc(em.sem["pe"], 1)
                                em.cnt["pe"] += 1
                                t = em.cnt["pe"]
                                vw.r.append(("pe", t))
                                xnr[ec].r.append(("pe", t))
                                if ec == NCH - 1:
                                    bank[bk].w = ("pe", t)
                                    bank[bk].r = []
                        for tg in range(4):
                            ts_ = half * 4 + tg
                            bk = 3 + tg
                            vres = R["vst"]
                            em.op("dve", lambda e, bk=bk: e.tensor_tensor(
                                vst[:, :], pb(bk), vbias[:, :], ALU.add),
                                reads=[bank[bk], R["vb0"]], writes=[vres])
                            for hl in range(4):
                                hd = fv * 4 + hl
                                em.dma("qa",
                                       kvv_c[1][hd * NT + ts_ * 128:
                                                hd * NT + (ts_ + 1) * 128, :],
                                       vst[:, hl * 128:(hl + 1) * 128],
                                       reads=[vres], writes=[R["kvv_c1"]], lane=f"lkvc{next(_kvrr)}")
                em.collective("AllGather", ALU.bypass, [[0, 1], [2, 3], [4, 5], [6, 7]],
                              [kvv_c[1].opt()], [kvv_g[1].opt()],
                              reads=[R["kvv_c1"]], writes=[R["kvv_g1"]])
            if stop == 4:
                dump("q_d", [128, NCH, NT], BF16, q_sb[:, :, :], list(qr))
                dump("xn_d", [128, NCH, NT], BF16, xn_sb[:, :, :], list(xnr))

            if stop > 4:
                em.barrier()
                o = 0
                kh_ap, o = carve(o, 2 * NT)
                vh_ap, o = carve(o, 2, 16, 128)
                p_ap, o = carve(o, 2, 2 * NT)
                pt_ap, o = carve(o, 16, 128)
                osb_ap, o = carve(o, NCH, 512)
                masks_ap, o = carve(o, NTB * 2, 128)
                em.dma("qs", masks_ap, masks_in.rearrange("j r p k -> p (j r) k"),
                       writes=[R["masks"]], lane="lmask")
                attention(1)
            if stop == 5:
                dump("x_d", [128, NCH, NT], F32, x_sb[:, :, :], list(xr))

            if stop > 5:
                em.barrier()
                o = 0
                h_ap, o = carve(o, NCH, NT)
                wx_ap, o = carve(o, 3, NCH, 128)
                ffn(1, h_ap, wx_ap)
                em.dma("qg", y_out.rearrange("(c p) t -> p c t", p=128), x_sb[:, :, :],
                       reads=list(xr), writes=[R["y"]])

            for s, v in sorted(em.cnt.items()):
                if v and s != "pool":
                    em.eng["qg"].wait_ge(em.sem[s], v)
    return nc, dbg_outs


def _core_inputs(inputs, c):
    b, h = divmod(c, 2)
    f32 = np.float32
    x4 = np.asarray(inputs["L0_attn_x"], f32)[b]
    xs = x4.reshape(16, 128, E)[h::2].reshape(NT, E)

    gix = np.asarray(inputs["L0_ffn_gate_idx"], np.int32)
    uix = np.asarray(inputs["L0_ffn_up_idx"], np.int32)
    dix = np.asarray(inputs["L0_ffn_down_idx"], np.int32)

    def gslice(a):
        out = np.zeros((G_ROWS, E), np.int32)
        lo = c * G_ROWS
        hi = min(INTER0, lo + G_ROWS)
        if hi > lo:
            out[0:hi - lo] = a[lo:hi]
        return out

    cf0, sf0 = _rope0_fields(h)
    cf1q, sf1q = _rope1_fields(h, SCALE)
    cf1k, sf1k = _rope1_fields(h, 1.0)
    lut = np.asarray(inputs["L0_ffn_lut"], f32).reshape(1, LUTN)
    lutvals = np.concatenate([lut, np.arange(LUTN, dtype=f32).reshape(1, LUTN)], 1)
    b_d = np.asarray(inputs["L0_attn_b_d"], f32)

    ck = lambda a: np.ascontiguousarray(a)
    return {
        "x_shard": ck(xs.T),
        "W_c": np.asarray(inputs["L0_attn_W_c"], f32),
        "W_d": np.asarray(inputs["L0_attn_W_d"], f32),
        "W_o0": np.asarray(inputs["L0_attn_W_o"], f32),
        "wq_s": ck(np.asarray(inputs["L1_attn_Wq"], f32)[c * QKVO_R:(c + 1) * QKVO_R]),
        "wk_s": ck(np.asarray(inputs["L1_attn_Wk"], f32)[c * QKVO_R:(c + 1) * QKVO_R]),
        "wv_s": ck(np.asarray(inputs["L1_attn_Wv"], f32)[c * QKVO_R:(c + 1) * QKVO_R]),
        "wo_s": ck(np.asarray(inputs["L1_attn_Wo"], f32)[c * QKVO_R:(c + 1) * QKVO_R]),
        "wo0t_s": ck(np.asarray(inputs["L0_attn_W_o"], f32).T[c * QKVO_R:(c + 1) * QKVO_R]),
        "wgu_s": ck(np.asarray(inputs["L1_ffn_w_gate_up"], f32)[c * WGU_R:(c + 1) * WGU_R]),
        "wdn_s": ck(np.asarray(inputs["L1_ffn_w_down"], f32)[c * WDN_R:(c + 1) * WDN_R]),
        "gix_s": gslice(gix),
        "uix_s": gslice(uix),
        "dix_s": ck(dix[c * D_ROWS:(c + 1) * D_ROWS]),
        "lutvals": lutvals,
        "ident": np.eye(128, dtype=BF),
        "r0t": _rot0T(), "r1t": _rot1T(),
        "cf0": cf0, "sf0": sf0,
        "cf1q": cf1q, "sf1q": sf1q, "cf1k": cf1k, "sf1k": sf1k,
        "masks": _masks(h),
        "bc_col": ck(np.asarray(inputs["L0_attn_b_c"], f32).reshape(4, 128).T),
        "bdqk_col": ck(b_d[:4096].reshape(32, 128).T),
        "bdv_row": ck(b_d[4096:].reshape(1, E)),
        "bo0_col": ck(np.asarray(inputs["L0_attn_b_o"], f32).reshape(NCH, 128).T),
        "bq1_col": ck(np.asarray(inputs["L1_attn_bq"], f32).reshape(NCH, 128).T),
        "bk1_col": ck(np.asarray(inputs["L1_attn_bk"], f32).reshape(NCH, 128).T),
        "bv1_row": ck(np.asarray(inputs["L1_attn_bv"], f32).reshape(1, E)),
        "bo1_col": ck(np.asarray(inputs["L1_attn_bo"], f32).reshape(NCH, 128).T),
    }


def kernel(**inputs):
    dbg = os.environ.get("MK_DBG", "")
    nc, dbg_names = build_program(dbg)
    in_maps = [_core_inputs(inputs, c) for c in range(8)]
    res = run_bass_kernel_spmd(nc, in_maps, core_ids=list(range(8)))
    if dbg:
        return res
    out = np.empty((B, S, E), np.float32)
    for c in range(8):
        b, h = divmod(c, 2)
        y = res.results[c]["y"]
        out[b].reshape(16, 128, E)[h::2] = y.T.reshape(NTB, 128, E)
    return out


# version 10
# speedup vs baseline: 1.1115x; 1.1115x over previous
import itertools
import os
import contextlib
from collections import defaultdict

import numpy as np
import ml_dtypes
import concourse.bass as bass
import concourse.mybir as mybir
from concourse.bass_utils import run_bass_kernel_spmd

F32 = mybir.dt.float32
BF16 = mybir.dt.bfloat16
I32 = mybir.dt.int32
AF = mybir.ActivationFunctionType
ALU = mybir.AluOpType
AX = mybir.AxisListType

B, S, E = 4, 2048, 2048
NCH = E // 128
NT = 1024
NTB = NT // 128
D_C, D_ROT = 512, 64
HEADS, HD = 16, 128
INTER0, INTER1 = 5504, 8192
LUTN = 16
SCALE = float(1.0 / np.sqrt(128.0))
EPS = 1e-5
NEG = -1e30
INV_E = 1.0 / 2048.0

G_ROWS = 768
D_ROWS = 256
QKVO_R, WGU_R, WDN_R = 256, 2048, 256
BF = ml_dtypes.bfloat16

_kvrr = itertools.cycle([0, 1, 2, 3])
_castrr = itertools.cycle([0, 1])

BLK = 128 * 16 * 128
OFF_WQ_B = 0
OFF_WK_B = 2 * BLK
OFF_WV_B = 4 * BLK
OFF_WO_B = OFF_WV_B + 16 * 128 * 256
OFF_WGU_B = OFF_WO_B + 2 * BLK
OFF_WDN_B = OFF_WGU_B + 16 * BLK
OFF_WO0_B = OFF_WDN_B + 8 * BLK
DENSE_SEG = OFF_WO0_B + 2 * BLK
L0_GRPS = [list(range(0, 16)), list(range(16, 32)), list(range(32, 43))]
DOWN_GRP_OFF = [0, 262144, 524288]
DOWN_SEG = 2 * 704512

PHASES = ["p1", "p2", "p3", "p4", "p5", "p6", "out"]


class Res:
    __slots__ = ("w", "r")
    def __init__(self):
        self.w = None
        self.r = []


class Em:
    NAMES = ("pe", "dve", "act", "pool", "qs", "qa", "qg")

    def __init__(self, nc, stack):
        self.nc = nc
        self.eng = {"pe": nc.tensor, "dve": nc.vector, "act": nc.scalar,
                    "pool": nc.gpsimd, "qs": nc.sync, "qa": nc.scalar,
                    "qg": nc.gpsimd}
        self.canon = {"pe": "pe", "dve": "dve", "act": "act", "pool": "pool",
                      "qs": "qs", "qa": "act", "qg": "pool"}
        self.sem = {n: stack.enter_context(nc.semaphore(f"sem_{n}")) for n in self.NAMES}
        self.dma_lanes = {"qs", "qa", "qg"}
        self.last_group = {"qs": None, "qa": None, "qg": None}
        self.auto_group = 0
        self.cc_sems = []
        self.stack = stack
        self.cnt = defaultdict(int)
        self.known = defaultdict(int)

    def lane(self, name):
        if name not in self.sem:
            self.sem[name] = self.stack.enter_context(self.nc.semaphore(f"lane_{name}"))
            self.dma_lanes.add(name)
            self.last_group[name] = None
        return name

    def _wait_one(self, waiter, sname, val):
        w = self.canon[waiter]
        if w == "pe" and sname == "pe":
            return
        if isinstance(sname, tuple):
            if self.known[(w, sname)] >= 1:
                return
            self.eng[waiter].wait_ge(self.cc_sems[sname[1]], 1)
            self.known[(w, sname)] = 1
        else:
            if self.known[(w, sname)] >= val:
                return
            self.eng[waiter].wait_ge(self.sem[sname], val)
            self.known[(w, sname)] = val

    def _deps(self, waiter, reads, writes):
        need = {}
        def add(s, t):
            need[s] = max(need.get(s, 0), t)
        for r in reads:
            if r.w is not None:
                s, t = r.w
                if isinstance(s, tuple):
                    self._wait_one(waiter, s, 1)
                else:
                    add(s, t)
        for wr in writes:
            if wr.w is not None:
                s, t = wr.w
                if isinstance(s, tuple):
                    self._wait_one(waiter, s, 1)
                else:
                    add(s, t)
            for (s, t) in wr.r:
                add(s, t)
        for s in sorted(need):
            self._wait_one(waiter, s, need[s])

    def op(self, eng, fn, reads=(), writes=()):
        self._deps(eng, reads, writes)
        instr = fn(self.eng[eng])
        instr.then_inc(self.sem[eng], 1)
        self.cnt[eng] += 1
        t = self.cnt[eng]
        for r in reads:
            r.r.append((eng, t))
        for w in writes:
            w.w = (eng, t)
            w.r = []
        return t

    def dma(self, q, out, in_, reads=(), writes=(), transpose=False, lane=None,
            group=None, **kw):
        ln_ = self.lane(lane) if lane else q
        if group is None:
            self.auto_group += 1
            group = ("__auto", self.auto_group)
        if self.last_group.get(ln_) != group:
            if self.cnt[ln_] > 0:
                self._wait_one(q, ln_, self.cnt[ln_])
            self.last_group[ln_] = group
        self._deps(q, reads, writes)
        e = self.eng[q]
        if transpose:
            instr = e.dma_start_transpose(out, in_)
        else:
            instr = e.dma_start(out=out, in_=in_, **kw)
        instr.then_inc(self.sem[ln_], 16)
        self.cnt[ln_] += 16
        t = self.cnt[ln_]
        for r in reads:
            r.r.append((ln_, t))
        for w in writes:
            w.w = (ln_, t)
            w.r = []
        return t

    def collective(self, kind, op, groups, ins, outs, reads=(), writes=()):
        self._deps("qg", reads, writes)
        idx = len(self.cc_sems)
        sem = self.stack.enter_context(self.nc.semaphore(f"cc{idx}"))
        self.cc_sems.append(sem)
        self.eng["qg"].collective_compute(
            kind, op, replica_groups=groups, ins=ins, outs=outs).then_inc(sem)
        for w in writes:
            w.w = (("__cc", idx), 1)
            w.r = []

    def barrier(self):
        snap = dict(self.cnt)
        for waiter in self.NAMES:
            for s, v in snap.items():
                if v:
                    self._wait_one(waiter, s, v)


class Acc:
    def __init__(self, em, bank_res, n):
        self.em = em
        self.bank = bank_res
        self.n = n
        self.i = 0
        self.touched = []

    def add(self, out_ap, lhsT_ap, rhs_ap, reads):
        em = self.em
        if self.i == 0:
            em._deps("pe", [], [self.bank])
        em._deps("pe", reads, [])
        last = (self.i == self.n - 1)
        instr = em.eng["pe"].matmul(out_ap, lhsT_ap, rhs_ap,
                                    start=(self.i == 0), stop=last)
        instr.then_inc(em.sem["pe"], 1)
        em.cnt["pe"] += 1
        t = em.cnt["pe"]
        for r in reads:
            r.r.append(("pe", t))
        self.i += 1
        if last:
            mode = os.environ.get("MK_NONOP", "1")
            if mode == "2":
                em.eng["pe"].nop(nofuse=True)
            elif mode != "1":
                em.eng["pe"].nop(nofuse=True).then_inc(em.sem["pe"], 1)
                em.cnt["pe"] += 1
            self.bank.w = ("pe", em.cnt["pe"])
            self.bank.r = []


def _positions(h):
    return np.concatenate([np.arange((2 * j + h) * 128, (2 * j + h + 1) * 128)
                           for j in range(NTB)])


def _rope0_fields(h):
    pos = _positions(h).astype(np.float64)
    inv = 1.0 / 10000.0 ** (np.arange(0, D_ROT, 2) / D_ROT)
    ang = pos[None, :] * inv[:, None]
    cf = np.ones((128, NT)); sf = np.zeros((128, NT))
    cf[0:32] = np.cos(ang); cf[32:64] = np.cos(ang)
    sf[0:32] = np.sin(ang); sf[32:64] = np.sin(ang)
    return cf.astype(BF), sf.astype(BF)


def _rope1_fields(h, scale):
    pos = _positions(h).astype(np.float64)
    inv = 10000.0 ** (-np.arange(0, HD, 2) / HD)
    ang = pos[None, :] * inv[:, None]
    cf = np.empty((128, NT)); sf = np.empty((128, NT))
    cf[0::2] = np.cos(ang); cf[1::2] = np.cos(ang)
    sf[0::2] = np.sin(ang); sf[1::2] = np.sin(ang)
    return (cf * scale).astype(BF), (sf * scale).astype(BF)


def _rot0T():
    r = np.zeros((128, 128), np.float32)
    for i in range(32):
        r[32 + i, i] = -1.0
        r[i, 32 + i] = 1.0
    return r.astype(BF)


def _rot1T():
    r = np.zeros((128, 128), np.float32)
    for i in range(64):
        r[2 * i + 1, 2 * i] = -1.0
        r[2 * i, 2 * i + 1] = 1.0
    return r.astype(BF)


def _masks(h):
    m = np.zeros((NTB, 2, 128, 128), np.float32)
    qi = np.arange(128); ki = np.arange(128)
    for j in range(NTB):
        for r in range(2):
            bad = ((2 * j + r) * 128 + ki[None, :]) > ((2 * j + h) * 128 + qi[:, None])
            m[j, r][bad] = NEG
    return m.astype(BF)


def build_program(dbg_stop=""):
    nc = bass.Bass("TRN2", target_bir_lowering=False)
    stack = contextlib.ExitStack()
    d = {}

    def din(name, shape, dtype):
        d[name] = nc.dram_tensor(name, shape, dtype, kind="ExternalInput").ap()
        return d[name]

    def dint(name, shape, dtype):
        d[name] = nc.dram_tensor(name, shape, dtype).ap()
        return d[name]

    def dshare(name, shape, dtype):
        d[name] = nc.dram_tensor(name, shape, dtype, addr_space="Shared").ap()
        return d[name]

    x_in = din("x_shard", [E, NT], F32)
    wc_in = din("W_c", [E, D_C], F32)
    wd_in = din("W_d", [D_C, 3 * E], F32)
    wo0_in = din("W_o0", [E, E], F32)
    wq_s = din("wq_s", [QKVO_R, E], F32)
    wk_s = din("wk_s", [QKVO_R, E], F32)
    wv_s = din("wv_s", [QKVO_R, E], F32)
    wo_s = din("wo_s", [QKVO_R, E], F32)
    wo0t_s = din("wo0t_s", [QKVO_R, E], F32)
    wgu_s = din("wgu_s", [WGU_R, E], F32)
    wdn_s = din("wdn_s", [WDN_R, INTER1], F32)
    gix_s = din("gix_s", [G_ROWS, E], I32)
    uix_s = din("uix_s", [G_ROWS, E], I32)
    dix_s = din("dix_s", [D_ROWS, INTER0], I32)
    lutvals = din("lutvals", [1, 2 * LUTN], F32)
    ident_in = din("ident", [128, 128], BF16)
    r0t_in = din("r0t", [128, 128], BF16)
    r1t_in = din("r1t", [128, 128], BF16)
    cf0_in = din("cf0", [128, NT], BF16)
    sf0_in = din("sf0", [128, NT], BF16)
    cf1q_in = din("cf1q", [128, NT], BF16)
    sf1q_in = din("sf1q", [128, NT], BF16)
    cf1k_in = din("cf1k", [128, NT], BF16)
    sf1k_in = din("sf1k", [128, NT], BF16)
    masks_in = din("masks", [NTB, 2, 128, 128], BF16)
    bc_in = din("bc_col", [128, 4], F32)
    bdqk_in = din("bdqk_col", [128, 32], F32)
    bdv_in = din("bdv_row", [1, E], F32)
    bo0_in = din("bo0_col", [128, NCH], F32)
    bq1_in = din("bq1_col", [128, NCH], F32)
    bk1_in = din("bk1_col", [128, NCH], F32)
    bv1_in = din("bv1_row", [1, E], F32)
    bo1_in = din("bo1_col", [128, NCH], F32)

    y_out = nc.dram_tensor("y", [E, NT], F32, kind="ExternalOutput").ap()

    gate_c = dint("gate_c", [G_ROWS, E], BF16)
    up_c = dint("up_c", [G_ROWS, E], BF16)
    down_c = dint("down_c", [D_ROWS, INTER0], BF16)
    gate_g = dshare("gate_g", [8 * G_ROWS, E], BF16)
    up_g = dshare("up_g", [8 * G_ROWS, E], BF16)
    down_g = dshare("down_g", [8 * D_ROWS, INTER0], BF16)
    dense_c = dint("dense_c", [4352, E], BF16)
    dense_g = dshare("dense_g", [8 * 4352, E], BF16)
    wo0_c = dint("wo0_c", [256, E], BF16)
    wo0_g = dshare("wo0_g", [8 * 256, E], BF16)
    kvk_c = [dint(f"kvk_c{l}", [HEADS * 128, NT], BF16) for l in range(2)]
    kvv_c = [dint(f"kvv_c{l}", [HEADS * NT, 128], BF16) for l in range(2)]
    kvk_g = [dint(f"kvk_g{l}", [2 * HEADS * 128, NT], BF16) for l in range(2)]
    kvv_g = [dint(f"kvv_g{l}", [2 * HEADS * NT, 128], BF16) for l in range(2)]

    sbufs = {}
    def sbuf(name, shape, dtype):
        t = stack.enter_context(nc.sbuf_tensor(name, shape, dtype))
        sbufs[name] = t
        return t

    x_sb = sbuf("x_sb", [128, NCH, NT], F32)
    xn_sb = sbuf("xn_sb", [128, NCH, NT], BF16)
    q_sb = sbuf("q_sb", [128, NCH, NT], BF16)
    ident = sbuf("ident_sb", [128, 128], BF16)
    r0t = sbuf("r0t_sb", [128, 128], BF16)
    r1t = sbuf("r1t_sb", [128, 128], BF16)
    kst = sbuf("kst", [128, NT], BF16)
    vst = sbuf("vst", [128, 512], BF16)
    vwt = sbuf("vwt", [128, 512], BF16)
    sg_sb = sbuf("sg_sb", [128, 512], BF16)
    wbuf = sbuf("wbuf", [128, 3, NCH, 128], BF16)
    xsq = sbuf("xsq", [128, 2, 512], F32)
    lnt = sbuf("lnt", [128, 512], F32)
    vbias = sbuf("vbias", [128, 512], F32)
    stat = sbuf("stat", [1, 3, 512], F32)
    mx = sbuf("mx", [128, 8], F32)
    lsum = sbuf("lsum", [128, 8], F32)
    bc_sb = sbuf("bc_sb", [128, 4], F32)
    bdqk_sb = sbuf("bdqk_sb", [128, 32], F32)
    bdqs_sb = sbuf("bdqs_sb", [128, 16], F32)
    bo0_sb = sbuf("bo0_sb", [128, NCH], F32)
    bq1_sb = sbuf("bq1_sb", [128, NCH], F32)
    bk1_sb = sbuf("bk1_sb", [128, NCH], F32)
    bo1_sb = sbuf("bo1_sb", [128, NCH], F32)
    lutrow = sbuf("lutrow", [1, 2 * LUTN], F32)
    onesr = sbuf("onesr", [1, 128], F32)
    onesrb = sbuf("onesrb", [1, 128], BF16)
    onescol = sbuf("onescol", [128, 1], F32)
    epscol = sbuf("epscol", [128, 1], F32)
    lutb = sbuf("lutb", [128, LUTN], F32)
    dec_i = sbuf("dec_i", [128, 512], I32)
    AR = 22528
    abf = sbuf("abf", [128, AR], BF16)

    ps32 = stack.enter_context(nc.psum_tensor("ps32", [128, 3584], F32))
    psbf = stack.enter_context(nc.psum_tensor("psbf", [128, 1024], BF16))

    def pb(b):
        return ps32[:, 512 * b:512 * (b + 1)]

    def carve(off, *dims):
        n = int(np.prod(dims))
        ap = abf[:, off:off + n]
        if len(dims) > 1:
            pat = " ".join(f"d{i}" for i in range(len(dims)))
            ap = ap.rearrange(f"p ({pat}) -> p {pat}",
                              **{f"d{i}": dims[i] for i in range(len(dims))})
        return ap, off + n

    dbg_outs = []

    with stack:
        with nc.Block():
            em = Em(nc, stack)
            R = defaultdict(Res)
            bank = [R[f"bank{i}"] for i in range(7)]
            slotbf = [R[f"psbf{i}"] for i in range(8)]
            xr = [R[f"x{c}"] for c in range(NCH)]
            xnr = [R[f"xn{c}"] for c in range(NCH)]
            qr = [R[f"q{c}"] for c in range(NCH)]
            hr = [R[f"h{c}"] for c in range(NCH)]
            osr = [R[f"os{c}"] for c in range(NCH)]
            ptr = [R[f"pt{k}"] for k in range(16)]

            stop = PHASES.index(dbg_stop) if dbg_stop in PHASES else 99

            def dump(nm, shape, dt, ap, rds):
                o = nc.dram_tensor(nm, shape, dt, kind="ExternalOutput").ap()
                idxall = tuple(slice(None) for _ in shape)
                em.dma("qg", o[idxall], ap, reads=rds, writes=[R["dump" + nm]])
                dbg_outs.append(nm)

            em.dma("qs", x_sb[:, :, :], x_in.rearrange("(c p) t -> p c t", p=128),
                   writes=xr, lane="lx")
            for nm, dst, src in [("ident", ident, ident_in), ("r0t", r0t, r0t_in),
                                 ("r1t", r1t, r1t_in), ("bc", bc_sb, bc_in),
                                 ("bdqk", bdqk_sb, bdqk_in), ("bo0", bo0_sb, bo0_in),
                                 ("bq1", bq1_sb, bq1_in), ("bk1", bk1_sb, bk1_in),
                                 ("bo1", bo1_sb, bo1_in), ("lutrow", lutrow, lutvals)]:
                em.dma("qs", dst[:, :], src[:, :], writes=[R[nm]], lane="l" + nm)
            em.op("pool", lambda e: e.memset(onesr[:, :], 1.0), writes=[R["onesr"]])
            em.op("pool", lambda e: e.memset(onesrb[:, :], 1.0), writes=[R["onesrb"]])
            em.op("pool", lambda e: e.memset(onescol[:, :], 1.0), writes=[R["onescol"]])
            em.op("pool", lambda e: e.memset(epscol[:, :], EPS), writes=[R["eps"]])
            em.op("dve", lambda e: e.tensor_scalar(bdqs_sb[:, :], bdqk_sb[:, 0:16],
                  SCALE, None, ALU.mult), reads=[R["bdqk"]], writes=[R["bdqs"]])
            ga = Acc(em, bank[6], 1)
            ga.add(ps32[:, 6 * 512:6 * 512 + LUTN], onesr[0:1, :], lutrow[0:1, 0:LUTN],
                   [R["onesr"], R["lutrow"]])
            em.op("dve", lambda e: e.tensor_copy(lutb[:, :],
                  ps32[:, 6 * 512:6 * 512 + LUTN]), reads=[bank[6]],
                  writes=[R["lutb"]])

            dcw = R["dense_c"]
            dc_flat = dense_c.rearrange("a b -> (a b)")
            o0 = 0
            stgi_ap, o0 = carve(o0, 4, E)
            stgo_ap, o0 = carve(o0, 4, NCH, 128)
            prep_bi = [0]

            def prep_block(src_slice, dst_off, dstf=None, wres=None):
                s = prep_bi[0] % 4
                prep_bi[0] += 1
                sir, sor = R[f"stgi{s}"], R[f"stgo{s}"]
                em.dma("qg", stgi_ap[:, s, :], src_slice, writes=[sir],
                       lane=f"lcin{s}")
                for ec in range(NCH):
                    sl_ = ec % 8
                    em.op("pe", lambda e, ec=ec, sl_=sl_, s=s: e.transpose(
                        psbf[:, sl_ * 128:(sl_ + 1) * 128],
                        stgi_ap.rearrange("p t (a b) -> p t a b", a=NCH)[:, s, ec, :],
                        ident[:, :]),
                        reads=[sir, R["ident"]], writes=[slotbf[sl_]])
                    if sl_ == 7:
                        em.op("dve", lambda e, ec=ec, s=s: e.tensor_copy(
                            stgo_ap[:, s, ec - 7:ec + 1, :],
                            psbf.ap().rearrange("p (kb q) -> p kb q", kb=8)),
                            reads=[slotbf[i] for i in range(8)],
                            writes=[sor])
                em.dma("qs", (dc_flat if dstf is None else dstf)
                       [dst_off:dst_off + BLK]
                       .rearrange("(p a b) -> p a b", p=128, a=NCH),
                       stgo_ap[:, s, :, :], reads=[sor],
                       writes=[dcw if wres is None else wres],
                       lane=f"lcout{s}")

            wo0c_flat = wo0_c.rearrange("a b -> (a b)")
            for b in range(2):
                prep_block(wq_s[b * 128:(b + 1) * 128, :], OFF_WQ_B + b * BLK)
                prep_block(wk_s[b * 128:(b + 1) * 128, :], OFF_WK_B + b * BLK)
                prep_block(wo_s[b * 128:(b + 1) * 128, :], OFF_WO_B + b * BLK)
                prep_block(wo0t_s[b * 128:(b + 1) * 128, :], b * BLK,
                           dstf=wo0c_flat, wres=R["wo0_c"])
            for j in range(16):
                prep_block(wgu_s[j * 128:(j + 1) * 128, :], OFF_WGU_B + j * BLK)
            for gi in range(4):
                for b in range(2):
                    prep_block(wdn_s[b * 128:(b + 1) * 128, gi * E:(gi + 1) * E],
                               OFF_WDN_B + (gi * 2 + b) * BLK)
            vv = dc_flat[OFF_WV_B:OFF_WV_B + 16 * 128 * 256]\
                .rearrange("(ec p f2) -> p ec f2", ec=16, p=128)
            for fr in range(2):
                s = prep_bi[0] % 4
                prep_bi[0] += 1
                sir, sor = R[f"stgi{s}"], R[f"stgo{s}"]
                em.dma("qg", stgi_ap[:, s, :], wv_s[fr * 128:(fr + 1) * 128, :],
                       writes=[sir], lane=f"lcin{s}")
                for ec in range(NCH):
                    sl_ = ec % 8
                    em.op("pe", lambda e, ec=ec, sl_=sl_, s=s: e.transpose(
                        psbf[:, sl_ * 128:(sl_ + 1) * 128],
                        stgi_ap.rearrange("p t (a b) -> p t a b", a=NCH)[:, s, ec, :],
                        ident[:, :]),
                        reads=[sir, R["ident"]], writes=[slotbf[sl_]])
                    if sl_ == 7:
                        em.op("dve", lambda e, ec=ec, s=s: e.tensor_copy(
                            stgo_ap[:, s, ec - 7:ec + 1, :],
                            psbf.ap().rearrange("p (kb q) -> p kb q", kb=8)),
                            reads=[slotbf[i] for i in range(8)],
                            writes=[sor])
                em.dma("qs", vv[:, :, fr * 128:(fr + 1) * 128], stgo_ap[:, s, :, :],
                       reads=[sor], writes=[dcw], lane=f"lcout{s}")
            em.barrier()

            def ln():
                for th in range(2):
                    sl = slice(512 * th, 512 * (th + 1))
                    a1 = Acc(em, bank[4], NCH)
                    a2 = Acc(em, bank[5], NCH)
                    for ec in range(NCH):
                        em.op("act", lambda e, ec=ec: e.activation(
                            xsq[:, ec % 2, :], x_sb[:, ec, sl], AF.Square),
                            reads=[xr[ec]], writes=[R[f"xsq{ec % 2}"]])
                        a1.add(pb(4)[0:1, :], onescol[:, :], x_sb[:, ec, sl],
                               [R["onescol"], xr[ec]])
                        a2.add(pb(5)[0:1, :], onescol[:, :], xsq[:, ec % 2, :],
                               [R["onescol"], R[f"xsq{ec % 2}"]])
                    em.op("dve", lambda e: e.tensor_scalar(stat[:, 0, :], pb(4)[0:1, :],
                          INV_E, None, ALU.mult), reads=[bank[4]], writes=[R["st0"]])
                    em.op("dve", lambda e: e.tensor_scalar(stat[:, 1, :], pb(5)[0:1, :],
                          INV_E, None, ALU.mult), reads=[bank[5]], writes=[R["st1"]])
                    em.op("dve", lambda e: e.tensor_tensor(stat[:, 2, :], stat[:, 0, :],
                          stat[:, 0, :], ALU.mult), reads=[R["st0"]], writes=[R["st2"]])
                    em.op("dve", lambda e: e.tensor_tensor(stat[:, 1, :], stat[:, 1, :],
                          stat[:, 2, :], ALU.subtract), reads=[R["st1"], R["st2"]],
                          writes=[R["st1"]])
                    em.op("act", lambda e: e.activation(stat[:, 1, :], stat[:, 1, :],
                          AF.Ln, bias=epscol[0:1, :]), reads=[R["st1"], R["eps"]],
                          writes=[R["st1"]])
                    em.op("act", lambda e: e.activation(stat[:, 1, :], stat[:, 1, :],
                          AF.Exp, scale=-0.5), reads=[R["st1"]], writes=[R["st1"]])
                    em.op("dve", lambda e: e.scalar_tensor_tensor(stat[:, 2, :],
                          stat[:, 0, :], -1.0, stat[:, 1, :], ALU.mult, ALU.mult),
                          reads=[R["st0"], R["st1"]], writes=[R["st2"]])
                    aa = Acc(em, bank[4], 1)
                    aa.add(pb(4), onesr[0:1, :], stat[0:1, 1, :], [R["onesr"], R["st1"]])
                    ab = Acc(em, bank[5], 1)
                    ab.add(pb(5), onesr[0:1, :], stat[0:1, 2, :], [R["onesr"], R["st2"]])
                    for ec in range(NCH):
                        em.op("dve", lambda e, ec=ec: e.tensor_tensor(
                            lnt[:, :], x_sb[:, ec, sl], pb(4), ALU.mult),
                            reads=[xr[ec], bank[4]], writes=[R["lnt"]])
                        em.op("dve", lambda e, ec=ec: e.tensor_tensor(
                            xn_sb[:, ec, sl], lnt[:, :], pb(5), ALU.add),
                            reads=[R["lnt"], bank[5]], writes=[xnr[ec]])

            o = 0
            wc_ap, o = carve(o, NCH, D_C)
            wdq_ap, o = carve(o, 4, 1536)
            c_ap, o = carve(o, 4, NT)
            cf0s_ap, o = carve(o, 2, NT)
            bdvrow_ap, o = carve(o, E)
            assert o <= AR

            ln()
            if stop <= 0:
                dump("xn_d", [128, NCH, NT], BF16, xn_sb[:, :, :], list(xnr))

            if stop > 0:
                cr = [R[f"c{c}"] for c in range(4)]
                em.dma("qg", wc_ap, wc_in.rearrange("(c p) f -> p c f", p=128),
                       writes=[R["wc"]], lane="lwc")
                em.dma("qs", cf0s_ap[:, 0, :], cf0_in[:, :], writes=[R["cf0"]], lane="lcf")
                em.dma("qs", cf0s_ap[:, 1, :], sf0_in[:, :], writes=[R["sf0"]], lane="lsf")
                em.dma("qg", bdvrow_ap[0:1, :], bdv_in[:, :], writes=[R["bdv"]], lane="lbdv")
                for fc in range(4):
                    for th in range(2):
                        sl = slice(512 * th, 512 * (th + 1))
                        bk = 5 + (fc + th) % 2
                        a = Acc(em, bank[bk], NCH)
                        for ec in range(NCH):
                            a.add(pb(bk), wc_ap[:, ec, fc * 128:(fc + 1) * 128],
                                  xn_sb[:, ec, sl], [R["wc"], xnr[ec]])
                        em.op("act", lambda e, fc=fc, sl=sl, bk=bk: e.activation(
                            c_ap[:, fc, sl], pb(bk), AF.Identity, bias=bc_sb[:, fc:fc + 1]),
                            reads=[R["bc"], bank[bk]], writes=[cr[fc]])
                for th in range(2):
                    sl = slice(512 * th, 512 * (th + 1))
                    a = Acc(em, bank[4], 1)
                    a.add(pb(4), r0t[:, :], c_ap[:, 0, sl], [R["r0t"], cr[0]])
                    em.op("dve", lambda e, sl=sl: e.tensor_tensor(
                        pb(3), pb(4), cf0s_ap[:, 1, sl], ALU.mult),
                        reads=[bank[4], R["sf0"]], writes=[bank[3]])
                    em.op("dve", lambda e, sl=sl: e.tensor_tensor(
                        sg_sb[:, :], c_ap[:, 0, sl], cf0s_ap[:, 0, sl], ALU.mult),
                        reads=[cr[0], R["cf0"]], writes=[R["sg0"]])
                    em.op("dve", lambda e, sl=sl: e.tensor_tensor(
                        c_ap[:, 0, sl], sg_sb[:, :], pb(3), ALU.add),
                        reads=[R["sg0"], bank[3]], writes=[cr[0]])

                def wd_cols(qd):
                    em.dma("qg", wdq_ap,
                           wd_in.rearrange("(c p) f -> p c f", p=128)
                           [:, :, qd * 1536:(qd + 1) * 1536], writes=[R["wd"]], lane="lwd")

                def hcol(fcol, width, evac):
                    qd = fcol // 1536
                    loc = fcol % 1536
                    for th in range(2):
                        sl = slice(512 * th, 512 * (th + 1))
                        bk = 5 + th % 2
                        a = Acc(em, bank[bk], 4)
                        for cc in range(4):
                            a.add(pb(bk), wdq_ap[:, cc, loc:loc + 128],
                                  c_ap[:, cc, sl], [R["wd"], cr[cc]])
                        evac(bk, th, sl)

                wd_cols(0)
                for hd in range(12):
                    hcol(hd * 128, 128, lambda bk, th, sl, hd=hd: em.op(
                        "act", lambda e: e.activation(q_sb[:, hd, sl], pb(bk), AF.Identity,
                                                      bias=bdqs_sb[:, hd:hd + 1],
                                                      scale=SCALE),
                        reads=[R["bdqs"], bank[bk]], writes=[qr[hd]]))
                wd_cols(1)
                for hd in range(12, 16):
                    hcol(hd * 128, 128, lambda bk, th, sl, hd=hd: em.op(
                        "act", lambda e: e.activation(q_sb[:, hd, sl], pb(bk), AF.Identity,
                                                      bias=bdqs_sb[:, hd:hd + 1],
                                                      scale=SCALE),
                        reads=[R["bdqs"], bank[bk]], writes=[qr[hd]]))
                def k_evac(hd):
                    kres = R["kst"]
                    def f(bk, th, sl):
                        em.op("act", lambda e: e.activation(
                            kst[:, sl], pb(bk), AF.Identity,
                            bias=bdqk_sb[:, 16 + hd:17 + hd]),
                            reads=[R["bdqk"], bank[bk]], writes=[kres])
                        if th == 1:
                            em.dma("qg", kvk_c[0][hd * 128:(hd + 1) * 128, :],
                                   kst[:, :], reads=[kres], writes=[R["kvk_c0"]], lane=f"lkvc{next(_kvrr)}")
                    return f
                for hd in range(8):
                    hcol(2048 + hd * 128, 128, k_evac(hd))
                wd_cols(2)
                for hd in range(8, 16):
                    hcol(2048 + hd * 128, 128, k_evac(hd))
                def vpart(fv):
                    qd = (4096 + fv * 512) // 1536
                    loc = (4096 + fv * 512) % 1536
                    a = Acc(em, bank[6], 1)
                    a.add(pb(6), onesrb[0:1, :], bdvrow_ap[0:1, fv * 512:(fv + 1) * 512],
                          [R["onesrb"], R["bdv"]])
                    em.op("dve", lambda e: e.tensor_copy(vbias[:, :], pb(6)),
                          reads=[bank[6]], writes=[R["vb0"]])
                    for ts_ in range(8):
                        vres = R["vst"]
                        bk = 4 + ts_ % 2
                        a = Acc(em, bank[bk], 4)
                        for cc in range(4):
                            a.add(pb(bk), c_ap[:, cc, ts_ * 128:(ts_ + 1) * 128],
                                  wdq_ap[:, cc, loc:loc + 512], [cr[cc], R["wd"]])
                        em.op("dve", lambda e, bk=bk: e.tensor_tensor(
                            vst[:, :], pb(bk), vbias[:, :], ALU.add),
                            reads=[bank[bk], R["vb0"]], writes=[vres])
                        for hl in range(4):
                            hd = fv * 4 + hl
                            em.dma("qg",
                                   kvv_c[0][hd * NT + ts_ * 128: hd * NT + (ts_ + 1) * 128, :],
                                   vst[:, hl * 128:(hl + 1) * 128],
                                   reads=[vres], writes=[R["kvv_c0"]], lane=f"lkvc{next(_kvrr)}")
                vpart(0)
                wd_cols(3)
                for fv in range(1, 4):
                    vpart(fv)

                em.collective("AllGather", ALU.bypass, [[0, 1], [2, 3], [4, 5], [6, 7]],
                              [kvk_c[0].opt()], [kvk_g[0].opt()],
                              reads=[R["kvk_c0"]], writes=[R["kvk_g0"]])
                em.collective("AllGather", ALU.bypass, [[0, 1], [2, 3], [4, 5], [6, 7]],
                              [kvv_c[0].opt()], [kvv_g[0].opt()],
                              reads=[R["kvv_c0"]], writes=[R["kvv_g0"]])
                em.collective("AllGather", ALU.bypass, [list(range(8))],
                              [wo0_c.opt()], [wo0_g.opt()],
                              reads=[R["wo0_c"]], writes=[R["wo0_done"]])
                em.collective("AllGather", ALU.bypass, [list(range(8))],
                              [dense_c.opt()], [dense_g.opt()],
                              reads=[dcw], writes=[R["dense_done"]])
            if stop == 1:
                dump("q_d", [128, NCH, NT], BF16, q_sb[:, :, :], list(qr))
                dump("c_d", [128, 4, NT], BF16, c_ap, [R[f"c{c}"] for c in range(4)])

            em.barrier()
            o = 0
            decb_ap, o = carve(o, 512)
            dect_ap, o = carve(o, 512)
            deca_ap, o = carve(o, 512)
            dstg_ap, o = carve(o, 4, 128)
            kh_ap, o = carve(o, 2 * NT)
            vh_ap, o = carve(o, 2, 16, 128)
            p_ap, o = carve(o, 2, 2 * NT)
            pt_ap, o = carve(o, 16, 128)
            osb_ap, o = carve(o, NCH, 512)
            assert o <= AR, o

            gate_f = gate_c.rearrange("a b -> (a b)")
            up_f = up_c.rearrange("a b -> (a b)")
            down_f = down_c.rearrange("a b -> (a b)")

            def decode_steps():
                tiles = []
                for t in range(6):
                    tiles.append(("g", gix_s, gate_f, t, E))
                for t in range(6):
                    tiles.append(("u", uix_s, up_f, t, E))
                for t in range(2):
                    tiles.append(("d", dix_s, down_f, t, INTER0))
                for (which, isrc, dstf, t, width) in tiles:
                    nsub = (width + 511) // 512
                    nic = width // 128
                    icq = []
                    for su in range(nsub):
                        w = min(512, width - su * 512)
                        ires = R["dec_i"]
                        ares = R["dec_a"]
                        def load(isrc=isrc, t=t, su=su, w=w, ires=ires):
                            em.dma("qs", dec_i[:, 0:w],
                                   isrc[t * 128:(t + 1) * 128, su * 512:su * 512 + w],
                                   writes=[ires], lane="ldeci")
                        yield load
                        def conv(w=w, ires=ires):
                            em.op("dve", lambda e: e.tensor_copy(
                                decb_ap[:, 0:w], dec_i[:, 0:w]),
                                reads=[ires], writes=[R["dec_b"]])
                        yield conv
                        for v in range(LUTN):
                            def step(v=v, w=w, ares=ares):
                                if v == 0:
                                    em.op("dve", lambda e: e.tensor_scalar(
                                        deca_ap[:, 0:w], decb_ap[:, 0:w],
                                        0.0, lutb[:, 0:1],
                                        ALU.is_equal, ALU.mult),
                                        reads=[R["dec_b"], R["lutb"]],
                                        writes=[ares])
                                else:
                                    em.op("dve", lambda e: e.tensor_scalar(
                                        dect_ap[:, 0:w], decb_ap[:, 0:w],
                                        float(v), lutb[:, v:v + 1],
                                        ALU.is_equal, ALU.mult),
                                        reads=[R["dec_b"], R["lutb"]],
                                        writes=[R["dec_t"]])
                                    em.op("dve", lambda e: e.tensor_tensor(
                                        deca_ap[:, 0:w], deca_ap[:, 0:w],
                                        dect_ap[:, 0:w], ALU.add),
                                        reads=[R["dec_t"]], writes=[ares])
                            yield step
                        nj = w // 128
                        ic0 = su * 4
                        if which in ("g", "u"):
                            dst = dstf[t * BLK:(t + 1) * BLK]\
                                .rearrange("(p a b) -> p a b", p=128, a=NCH)\
                                [:, ic0:ic0 + nj, :]
                        else:
                            gi = ic0 // 16
                            ii0 = ic0 - 16 * gi
                            base = t * 704512 + DOWN_GRP_OFF[gi]
                            glen = len(L0_GRPS[gi])
                            dst = dstf[base:base + 128 * glen * 128]\
                                .rearrange("(p a b) -> p a b", p=128, a=glen)\
                                [:, ii0:ii0 + nj, :]
                        def tpflush(nj=nj, dst=dst, ares=ares):
                            for j in range(nj):
                                em.op("pe", lambda e, j=j: e.transpose(
                                    psbf[:, j * 128:(j + 1) * 128],
                                    deca_ap[:, j * 128:(j + 1) * 128], ident[:, :]),
                                    reads=[ares, R["ident"]], writes=[slotbf[j]])
                            em.op("dve", lambda e: e.tensor_copy(
                                dstg_ap[:, 0:nj, :],
                                psbf.ap().rearrange("p (kb q) -> p kb q", kb=8)
                                [:, 0:nj, :]),
                                reads=[slotbf[i] for i in range(nj)],
                                writes=[R["dstg"]])
                            em.dma("qs", dst, dstg_ap[:, 0:nj, :],
                                   reads=[R["dstg"]],
                                   writes=[R["gh_cw"]], lane="lghc")
                        yield tpflush
                    if (which, t) == ("g", 5):
                        def cc():
                            em.collective("AllGather", ALU.bypass, [list(range(8))],
                                          [gate_c.opt()], [gate_g.opt()],
                                          reads=[R["gh_cw"]], writes=[R["gate_done"]])
                        yield cc
                    if (which, t) == ("u", 5):
                        def cc():
                            em.collective("AllGather", ALU.bypass, [list(range(8))],
                                          [up_c.opt()], [up_g.opt()],
                                          reads=[R["gh_cw"]], writes=[R["up_done"]])
                        yield cc
                    if (which, t) == ("d", 1):
                        def cc():
                            em.collective("AllGather", ALU.bypass, [list(range(8))],
                                          [down_c.opt()], [down_g.opt()],
                                          reads=[R["gh_cw"]], writes=[R["down_done"]])
                        yield cc

            dec_gen = decode_steps() if stop > 1 else iter(())

            def pump(k):
                for _ in range(k):
                    s = next(dec_gen, None)
                    if s is None:
                        return
                    s()

            def attention(layer):
                kvkg, kvvg = kvk_g[layer], kvv_g[layer]
                gres_k, gres_v = R[f"kvk_g{layer}"], R[f"kvv_g{layer}"]
                maskres = R["masks"]

                def kv_load(u):
                    g_, hd_ = divmod(u, HEADS)
                    sb_ = u % 2
                    for r in range(2):
                        em.dma("qa", kh_ap[:, r * NT:(r + 1) * NT],
                               kvkg[(r * HEADS + hd_) * 128:(r * HEADS + hd_ + 1) * 128, :],
                               reads=[gres_k], writes=[R["kh"]],
                               lane="lkh", group=("kh", layer, g_, hd_))
                        em.dma("qa", vh_ap[:, sb_, r * 8:(r + 1) * 8, :],
                               kvvg[(r * HEADS + hd_) * NT:(r * HEADS + hd_ + 1) * NT, :]
                               .rearrange("(kb p) dd -> p kb dd", p=128),
                               reads=[gres_v], writes=[R[f"vh{sb_}"]],
                               lane=f"lvh{sb_}", group=("vh", layer, g_, hd_))

                kv_load(0)
                for g in range(2):
                    for hd in range(HEADS):
                        u = g * HEADS + hd
                        sb = u % 2
                        khres, vhres = R["kh"], R[f"vh{sb}"]

                        def s_stage(qs_):
                            j = g * 4 + qs_
                            nblk = (j + 1) if layer == 1 else 8
                            L = 2 * nblk * 128
                            qt = q_sb[:, hd, j * 128:(j + 1) * 128]
                            pieces = []
                            for r in range(2):
                                seg0 = r * nblk * 128
                                rem = nblk * 128
                                off = 0
                                while rem > 0:
                                    col = seg0 + off
                                    n = min(512 - col % 512, rem, 512)
                                    pieces.append((r, off, col, n))
                                    off += n
                                    rem -= n
                            for (r, off, col, n) in pieces:
                                b0 = col // 512
                                a = Acc(em, bank[b0], 1)
                                a.add(ps32[:, col:col + n], qt,
                                      kh_ap[:, r * NT + off: r * NT + off + n],
                                      [qr[hd], khres])
                            return nblk, L

                        def softmax_stage(qs_, nblk, L):
                            j = g * 4 + qs_
                            pp = qs_ % 2
                            nbk = (L + 511) // 512
                            if layer == 1:
                                for r in range(2):
                                    cm = (nblk * (r + 1) - 1) * 128
                                    em.op("dve", lambda e, cm=cm, r=r, j=j: e.tensor_tensor(
                                        ps32[:, cm:cm + 128], ps32[:, cm:cm + 128],
                                        masks_ap[:, j * 2 + r, :], ALU.add),
                                        reads=[maskres, bank[cm // 512]],
                                        writes=[bank[cm // 512]])
                            for b0 in range(nbk):
                                n = min(512, L - b0 * 512)
                                em.op("dve", lambda e, b0=b0, n=n: e.reduce_max(
                                    mx[:, b0:b0 + 1], ps32[:, b0 * 512:b0 * 512 + n],
                                    axis=AX.X), reads=[bank[b0]], writes=[R["mx"]])
                            em.op("dve", lambda e, nbk=nbk: e.reduce_max(
                                mx[:, 6:7], mx[:, 0:nbk], axis=AX.X),
                                reads=[R["mx"]], writes=[R["mx6"]])
                            em.op("dve", lambda e: e.tensor_scalar(
                                mx[:, 7:8], mx[:, 6:7], -1.0, None, ALU.mult),
                                reads=[R["mx6"]], writes=[R["mx7"]])
                            for b0 in range(nbk):
                                n = min(512, L - b0 * 512)
                                em.op("act", lambda e, b0=b0, n=n, pp=pp: e.activation(
                                    p_ap[:, pp, b0 * 512:b0 * 512 + n],
                                    ps32[:, b0 * 512:b0 * 512 + n], AF.Exp,
                                    bias=mx[:, 7:8], accum_out=lsum[:, b0:b0 + 1]),
                                    reads=[R["mx7"], bank[b0]],
                                    writes=[R[f"p{pp}"], R["ls"]])
                            em.op("dve", lambda e, nbk=nbk: e.reduce_sum(
                                lsum[:, 6:7], lsum[:, 0:nbk], axis=AX.X),
                                reads=[R["ls"]], writes=[R["ls6"]])
                            em.op("dve", lambda e: e.reciprocal(lsum[:, 7:8],
                                  lsum[:, 6:7]), reads=[R["ls6"]], writes=[R["ls7"]])
                            em.op("act", lambda e, L=L, pp=pp: e.activation(
                                p_ap[:, pp, 0:L], p_ap[:, pp, 0:L], AF.Identity,
                                scale=lsum[:, 7:8]),
                                reads=[R["ls7"]], writes=[R[f"p{pp}"]])

                        def pe_stage(qs_, nblk):
                            pp = qs_ % 2
                            nkb = 2 * nblk
                            for kb in range(nkb):
                                sl_ = kb % 8
                                em.op("pe", lambda e, kb=kb, sl_=sl_, pp=pp: e.transpose(
                                    psbf[:, sl_ * 128:(sl_ + 1) * 128],
                                    p_ap[:, pp, kb * 128:(kb + 1) * 128], ident[:, :]),
                                    reads=[R[f"p{pp}"], R["ident"]], writes=[slotbf[sl_]])
                                if sl_ == 7 or kb == nkb - 1:
                                    k0 = kb - sl_
                                    em.op("dve", lambda e, k0=k0, sl_=sl_, kb=kb:
                                          e.tensor_copy(
                                              pt_ap[:, k0:kb + 1, :],
                                              psbf.ap().rearrange(
                                                  "p (kb qq) -> p kb qq", kb=8)
                                              [:, 0:sl_ + 1, :]),
                                          reads=[slotbf[i] for i in range(sl_ + 1)],
                                          writes=[ptr[i] for i in range(k0, kb + 1)])
                            a = Acc(em, bank[4], nkb)
                            for kb in range(nkb):
                                r, loc = divmod(kb, nblk)
                                vb = r * 8 + loc
                                a.add(ps32[:, 4 * 512:4 * 512 + 128],
                                      vh_ap[:, sb, vb, :], pt_ap[:, kb, :],
                                      [vhres, ptr[kb]])
                            em.op("act", lambda e, hd=hd, qs_=qs_: e.activation(
                                osb_ap[:, hd, qs_ * 128:(qs_ + 1) * 128],
                                ps32[:, 4 * 512:4 * 512 + 128], AF.Copy),
                                reads=[bank[4]], writes=[osr[hd]])

                        meta = {}
                        meta[0] = s_stage(0)
                        softmax_stage(0, *meta[0])
                        for j in range(1, 4):
                            meta[j] = s_stage(j)
                            if j == 3 and u + 1 < 2 * HEADS:
                                kv_load(u + 1)
                            pe_stage(j - 1, meta[j - 1][0])
                            softmax_stage(j, *meta[j])
                            if layer == 0:
                                pump(11)
                        pe_stage(3, meta[3][0])
                        if layer == 0:
                            pump(12)
                    tsl = slice(g * 512, (g + 1) * 512)
                    if True:
                        wo_bias = bo0_sb if layer == 0 else bo1_sb
                        wo_bres = R["bo0"] if layer == 0 else R["bo1"]
                        def wo_fill(fc, q):
                            wres = R[f"wb{fc % 2}"]
                            rank, b = fc // 2, fc % 2
                            if layer == 0:
                                flat = wo0_g.rearrange("a b -> (a b)")
                                off = rank * 2 * BLK + b * BLK
                                rd = R["wo0_done"]
                            else:
                                flat = dense_g.rearrange("a b -> (a b)")
                                off = rank * DENSE_SEG + OFF_WO_B + b * BLK
                                rd = R["dense_done"]
                            em.dma(q, wbuf[:, fc % 2, :, :],
                                   flat[off:off + BLK].rearrange(
                                       "(p a c) -> p a c", p=128, a=NCH),
                                   reads=[rd],
                                   writes=[wres], lane=f"lwb{fc % 2}",
                                   group=("wo", layer, g, fc))
                            return wres
                        wo_fill(0, "qs")
                        for fc in range(NCH):
                            wres = R[f"wb{fc % 2}"]
                            if fc + 1 < NCH:
                                wo_fill(fc + 1, "qs")
                            a = Acc(em, bank[5], NCH)
                            for ec in range(NCH):
                                a.add(pb(5), wbuf[:, fc % 2, ec, :], osb_ap[:, ec, :],
                                      [wres, osr[ec]])
                            em.op("dve", lambda e, fc=fc, wo_bias=wo_bias,
                                  wo_bres=wo_bres: e.scalar_tensor_tensor(
                                x_sb[:, fc, tsl], pb(5), wo_bias[:, fc:fc + 1],
                                x_sb[:, fc, tsl], ALU.add, ALU.add),
                                reads=[bank[5], wo_bres], writes=[xr[fc]])

            def ffn(layer, h_ap, wx_ap):
                hgroups = ([list(range(0, 16)), list(range(16, 32)), list(range(32, 43))]
                           if layer == 0 else
                           [list(range(g * 16, (g + 1) * 16)) for g in range(4)])

                def slotap(s):
                    return wbuf[:, s, :, :] if s < 3 else wx_ap[:, s - 3, :, :]

                def fill(which, ic, s, q):
                    wres = R[f"ws{s}"]
                    if which == "g":
                        if layer == 0:
                            rank, lo = ic // 6, ic % 6
                            flat = gate_g.rearrange("a b -> (a b)")
                            off = rank * G_ROWS * E + lo * BLK
                            rd = R["gate_done"]
                        else:
                            rank, lo = ic // 16, ic % 16
                            flat = dense_g.rearrange("a b -> (a b)")
                            off = rank * DENSE_SEG + OFF_WGU_B + lo * BLK
                            rd = R["dense_done"]
                        dst = slotap(s)[:, :, :]
                        srcap = flat[off:off + BLK].rearrange(
                            "(p a b) -> p a b", p=128, a=NCH)
                        em.dma(q, dst, srcap, reads=[rd], writes=[wres],
                               lane=f"lws{s}", group=("f", layer, which, str(ic)))
                        return wres
                    if which == "u":
                        if layer == 0:
                            rank, lo = ic // 6, ic % 6
                            flat = up_g.rearrange("a b -> (a b)")
                            off = rank * G_ROWS * E + lo * BLK
                            rd = R["up_done"]
                        else:
                            icu = 64 + ic
                            rank, lo = icu // 16, icu % 16
                            flat = dense_g.rearrange("a b -> (a b)")
                            off = rank * DENSE_SEG + OFF_WGU_B + lo * BLK
                            rd = R["dense_done"]
                        dst = slotap(s)[:, :, :]
                        srcap = flat[off:off + BLK].rearrange(
                            "(p a b) -> p a b", p=128, a=NCH)
                        em.dma(q, dst, srcap, reads=[rd], writes=[wres],
                               lane=f"lws{s}", group=("f", layer, which, str(ic)))
                        return wres
                    gi, fc, grp = ic
                    rank, b = fc // 2, fc % 2
                    glen = len(grp)
                    if layer == 0:
                        flat = down_g.rearrange("a b -> (a b)")
                        off = rank * DOWN_SEG + b * 704512 + DOWN_GRP_OFF[gi]
                        rd = R["down_done"]
                    else:
                        flat = dense_g.rearrange("a b -> (a b)")
                        off = rank * DENSE_SEG + OFF_WDN_B + (gi * 2 + b) * BLK
                        rd = R["dense_done"]
                    dst = slotap(s)[:, 0:glen, :]
                    srcap = flat[off:off + 128 * glen * 128].rearrange(
                        "(p a b) -> p a b", p=128, a=glen)
                    em.dma(q, dst, srcap, reads=[rd], writes=[wres],
                           lane=f"lws{s}", group=("f", layer, which, str(ic)))
                    return wres

                allc = [ic for grp in hgroups for ic in grp]
                fill("g", allc[0], 0, "qs")
                fill("u", allc[0], 2, "qs")
                k = 0
                for gi, grp in enumerate(hgroups):
                    for ii, ic in enumerate(grp):
                        if k + 1 < len(allc):
                            fill("g", allc[k + 1], (k + 1) % 2, "qs")
                            fill("u", allc[k + 1], 2 + (k + 1) % 2, "qs")
                        wg = R[f"ws{k % 2}"]
                        wu = R[f"ws{2 + k % 2}"]
                        gs = slotap(k % 2)
                        us = slotap(2 + k % 2)
                        for th in range(2):
                            sl = slice(512 * th, 512 * (th + 1))
                            a5 = Acc(em, bank[5], NCH)
                            for ec in range(NCH):
                                a5.add(pb(5), gs[:, ec, :], xn_sb[:, ec, sl],
                                       [wg, xnr[ec]])
                            a6 = Acc(em, bank[6], NCH)
                            for ec in range(NCH):
                                a6.add(pb(6), us[:, ec, :], xn_sb[:, ec, sl],
                                       [wu, xnr[ec]])
                            em.op("act", lambda e: e.activation(
                                sg_sb[:, :], pb(5), AF.Silu),
                                reads=[bank[5]], writes=[R["sg0"]])
                            em.op("dve", lambda e, ii=ii, sl=sl: e.tensor_tensor(
                                h_ap[:, ii, sl], pb(6), sg_sb[:, :], ALU.mult),
                                reads=[bank[6], R["sg0"]], writes=[hr[ii]])
                        k += 1
                    fill("d", (gi, 0, grp), 4, "qs")
                    for fc in range(NCH):
                        if fc + 1 < NCH:
                            fill("d", (gi, fc + 1, grp), 4 + (fc + 1) % 2, "qs")
                        wres = R[f"ws{4 + fc % 2}"]
                        ds = slotap(4 + fc % 2)
                        for th in range(2):
                            sl = slice(512 * th, 512 * (th + 1))
                            a = Acc(em, bank[4], len(grp))
                            for ii in range(len(grp)):
                                a.add(pb(4), ds[:, ii, :], h_ap[:, ii, sl],
                                      [wres, hr[ii]])
                            em.op("dve", lambda e, fc=fc, sl=sl: e.tensor_tensor(
                                x_sb[:, fc, sl], x_sb[:, fc, sl], pb(4), ALU.add),
                                reads=[bank[4]], writes=[xr[fc]])

            masks_ap = None
            if stop > 1:
                attention(0)
                pump(10 ** 9)
            if stop == 2:
                dump("x_d", [128, NCH, NT], F32, x_sb[:, :, :], list(xr))

            if stop > 2:
                em.barrier()
                o = 0
                h_ap, o = carve(o, NCH, NT)
                wx_ap, o = carve(o, 3, NCH, 128)
                ffn(0, h_ap, wx_ap)
            if stop == 3:
                dump("x_d", [128, NCH, NT], F32, x_sb[:, :, :], list(xr))

            if stop > 3:
                em.barrier()
                o = 0
                cf1_ap, o = carve(o, 4, NT)
                bv1row_ap, o = carve(o, E)
                vw2_ap, o = carve(o, 2, 512)
                ln()
                em.dma("qs", cf1_ap[:, 0, :], cf1q_in[:, :], writes=[R["cf1q"]], lane="lcf")
                em.dma("qs", cf1_ap[:, 1, :], sf1q_in[:, :], writes=[R["sf1q"]], lane="lsf")
                em.dma("qs", cf1_ap[:, 2, :], cf1k_in[:, :], writes=[R["cf1k"]], lane="lcf2")
                em.dma("qs", cf1_ap[:, 3, :], sf1k_in[:, :], writes=[R["sf1k"]], lane="lsf2")
                em.dma("qg", bv1row_ap[0:1, :], bv1_in[:, :], writes=[R["bv1"]], lane="lbdv")

                dg_flat = dense_g.rearrange("a b -> (a b)")
                LIN_OFF = {0: OFF_WQ_B, QKVO_R: OFF_WK_B, 3 * QKVO_R: OFF_WO_B}

                def lin_fill(base, fc, slot, q):
                    wres = R[f"wb{slot}"]
                    rank, b = fc // 2, fc % 2
                    off = rank * DENSE_SEG + LIN_OFF[base] + b * BLK
                    em.dma(q, wbuf[:, slot, :, :],
                           dg_flat[off:off + BLK].rearrange(
                               "(p a c) -> p a c", p=128, a=NCH),
                           reads=[R["dense_done"]], writes=[wres],
                           lane=f"lwb{slot}", group=("lin", base, fc))
                    return wres

                def lin_comp(fc, slot, out_fn):
                    wres = R[f"wb{slot}"]
                    for th in range(2):
                        sl = slice(512 * th, 512 * (th + 1))
                        bk = 5 + th % 2
                        a = Acc(em, bank[bk], NCH)
                        for ec in range(NCH):
                            a.add(pb(bk), wbuf[:, slot, ec, :], xn_sb[:, ec, sl],
                                  [wres, xnr[ec]])
                        out_fn(bk, th, sl)

                lin_fill(QKVO_R, 0, 0, "qs")
                for fc in range(NCH):
                    kres = R["kst"]
                    def kevac(bk, th, sl, fc=fc, kres=kres):
                        em.op("act", lambda e: e.activation(
                            kst[:, sl], pb(bk), AF.Identity,
                            bias=bk1_sb[:, fc:fc + 1]),
                            reads=[R["bk1"], bank[bk]], writes=[kres])
                    if fc + 1 < NCH:
                        lin_fill(QKVO_R, fc + 1, (fc + 1) % 2, "qs")
                    lin_comp(fc, fc % 2, kevac)
                    for th in range(2):
                        sl = slice(512 * th, 512 * (th + 1))
                        a = Acc(em, bank[4], 1)
                        a.add(pb(4), r1t[:, :], kst[:, sl], [R["r1t"], kres])
                        em.op("dve", lambda e, sl=sl: e.tensor_tensor(
                            pb(3), pb(4), cf1_ap[:, 3, sl], ALU.mult),
                            reads=[bank[4], R["sf1k"]], writes=[bank[3]])
                        em.op("dve", lambda e, sl=sl: e.tensor_tensor(
                            sg_sb[:, :], kst[:, sl], cf1_ap[:, 2, sl], ALU.mult),
                            reads=[kres, R["cf1k"]], writes=[R["sg0"]])
                        em.op("dve", lambda e, sl=sl: e.tensor_tensor(
                            kst[:, sl], sg_sb[:, :], pb(3), ALU.add),
                            reads=[R["sg0"], bank[3]], writes=[kres])
                    em.dma("qa", kvk_c[1][fc * 128:(fc + 1) * 128, :], kst[:, :],
                           reads=[kres], writes=[R["kvk_c1"]], lane=f"lkvc{next(_kvrr)}")
                em.collective("AllGather", ALU.bypass, [[0, 1], [2, 3], [4, 5], [6, 7]],
                              [kvk_c[1].opt()], [kvk_g[1].opt()],
                              reads=[R["kvk_c1"]], writes=[R["kvk_g1"]])
                for fv in range(4):
                    a = Acc(em, bank[2], 1)
                    a.add(pb(2), onesrb[0:1, :], bv1row_ap[0:1, fv * 512:(fv + 1) * 512],
                          [R["onesrb"], R["bv1"]])
                    em.op("dve", lambda e: e.tensor_copy(vbias[:, :], pb(2)),
                          reads=[bank[2]], writes=[R["vb0"]])
                    def v_wload(fv_, ec_):
                        s_ = ec_ % 2
                        vw_ = R[f"vw2{s_}"]
                        for rr in range(2):
                            rank = 2 * fv_ + rr
                            off = rank * DENSE_SEG + OFF_WV_B + ec_ * 32768
                            em.dma("qs", vw2_ap[:, s_, rr * 256:(rr + 1) * 256],
                                   dg_flat[off:off + 32768].rearrange(
                                       "(p f2) -> p f2", p=128),
                                   reads=[R["dense_done"]],
                                   writes=[vw_], lane=f"lvw2{s_}",
                                   group=("vw", fv_, ec_))
                        return vw_
                    for half in range(2):
                        v_wload(fv, 0)
                        for ec in range(NCH):
                            vw = R[f"vw2{ec % 2}"]
                            if ec + 1 < NCH:
                                v_wload(fv, ec + 1)
                            for tg in range(4):
                                ts_ = half * 4 + tg
                                bk = 3 + tg
                                em._deps("pe", [vw, xnr[ec]],
                                         [bank[bk]] if ec == 0 else [])
                                instr = em.eng["pe"].matmul(
                                    pb(bk), xn_sb[:, ec, ts_ * 128:(ts_ + 1) * 128],
                                    vw2_ap[:, ec % 2, :], start=(ec == 0),
                                    stop=(ec == NCH - 1))
                                instr.then_inc(em.sem["pe"], 1)
                                em.cnt["pe"] += 1
                                t = em.cnt["pe"]
                                vw.r.append(("pe", t))
                                xnr[ec].r.append(("pe", t))
                                if ec == NCH - 1:
                                    bank[bk].w = ("pe", t)
                                    bank[bk].r = []
                        for tg in range(4):
                            ts_ = half * 4 + tg
                            bk = 3 + tg
                            vres = R["vst"]
                            em.op("dve", lambda e, bk=bk: e.tensor_tensor(
                                vst[:, :], pb(bk), vbias[:, :], ALU.add),
                                reads=[bank[bk], R["vb0"]], writes=[vres])
                            for hl in range(4):
                                hd = fv * 4 + hl
                                em.dma("qa",
                                       kvv_c[1][hd * NT + ts_ * 128:
                                                hd * NT + (ts_ + 1) * 128, :],
                                       vst[:, hl * 128:(hl + 1) * 128],
                                       reads=[vres], writes=[R["kvv_c1"]], lane=f"lkvc{next(_kvrr)}")
                em.collective("AllGather", ALU.bypass, [[0, 1], [2, 3], [4, 5], [6, 7]],
                              [kvv_c[1].opt()], [kvv_g[1].opt()],
                              reads=[R["kvv_c1"]], writes=[R["kvv_g1"]])
                lin_fill(0, 0, 0, "qs")
                for fc in range(NCH):
                    def qevac(bk, th, sl, fc=fc):
                        em.op("act", lambda e: e.activation(
                            q_sb[:, fc, sl], pb(bk), AF.Identity,
                            bias=bq1_sb[:, fc:fc + 1]),
                            reads=[R["bq1"], bank[bk]], writes=[qr[fc]])
                    if fc + 1 < NCH:
                        lin_fill(0, fc + 1, (fc + 1) % 2, "qs")
                    lin_comp(fc, fc % 2, qevac)
                    for th in range(2):
                        sl = slice(512 * th, 512 * (th + 1))
                        a = Acc(em, bank[4], 1)
                        a.add(pb(4), r1t[:, :], q_sb[:, fc, sl], [R["r1t"], qr[fc]])
                        em.op("dve", lambda e, sl=sl: e.tensor_tensor(
                            pb(3), pb(4), cf1_ap[:, 1, sl], ALU.mult),
                            reads=[bank[4], R["sf1q"]], writes=[bank[3]])
                        em.op("dve", lambda e, fc=fc, sl=sl: e.tensor_tensor(
                            sg_sb[:, :], q_sb[:, fc, sl], cf1_ap[:, 0, sl], ALU.mult),
                            reads=[qr[fc], R["cf1q"]], writes=[R["sg0"]])
                        em.op("dve", lambda e, fc=fc, sl=sl: e.tensor_tensor(
                            q_sb[:, fc, sl], sg_sb[:, :], pb(3), ALU.add),
                            reads=[R["sg0"], bank[3]], writes=[qr[fc]])
            if stop == 4:
                dump("q_d", [128, NCH, NT], BF16, q_sb[:, :, :], list(qr))
                dump("xn_d", [128, NCH, NT], BF16, xn_sb[:, :, :], list(xnr))

            if stop > 4:
                em.barrier()
                o = 0
                kh_ap, o = carve(o, 2 * NT)
                vh_ap, o = carve(o, 2, 16, 128)
                p_ap, o = carve(o, 2, 2 * NT)
                pt_ap, o = carve(o, 16, 128)
                osb_ap, o = carve(o, NCH, 512)
                masks_ap, o = carve(o, NTB * 2, 128)
                em.dma("qs", masks_ap, masks_in.rearrange("j r p k -> p (j r) k"),
                       writes=[R["masks"]], lane="lmask")
                attention(1)
            if stop == 5:
                dump("x_d", [128, NCH, NT], F32, x_sb[:, :, :], list(xr))

            if stop > 5:
                em.barrier()
                o = 0
                h_ap, o = carve(o, NCH, NT)
                wx_ap, o = carve(o, 3, NCH, 128)
                ffn(1, h_ap, wx_ap)
                em.dma("qg", y_out.rearrange("(c p) t -> p c t", p=128), x_sb[:, :, :],
                       reads=list(xr), writes=[R["y"]])

            for s, v in sorted(em.cnt.items()):
                if v and s != "pool":
                    em.eng["qg"].wait_ge(em.sem[s], v)
    return nc, dbg_outs


def _core_inputs(inputs, c):
    b, h = divmod(c, 2)
    f32 = np.float32
    x4 = np.asarray(inputs["L0_attn_x"], f32)[b]
    xs = x4.reshape(16, 128, E)[h::2].reshape(NT, E)

    gix = np.asarray(inputs["L0_ffn_gate_idx"], np.int32)
    uix = np.asarray(inputs["L0_ffn_up_idx"], np.int32)
    dix = np.asarray(inputs["L0_ffn_down_idx"], np.int32)

    def gslice(a):
        out = np.zeros((G_ROWS, E), np.int32)
        lo = c * G_ROWS
        hi = min(INTER0, lo + G_ROWS)
        if hi > lo:
            out[0:hi - lo] = a[lo:hi]
        return out

    cf0, sf0 = _rope0_fields(h)
    cf1q, sf1q = _rope1_fields(h, SCALE)
    cf1k, sf1k = _rope1_fields(h, 1.0)
    lut = np.asarray(inputs["L0_ffn_lut"], f32).reshape(1, LUTN)
    lutvals = np.concatenate([lut, np.arange(LUTN, dtype=f32).reshape(1, LUTN)], 1)
    b_d = np.asarray(inputs["L0_attn_b_d"], f32)

    ck = lambda a: np.ascontiguousarray(a)
    return {
        "x_shard": ck(xs.T),
        "W_c": np.asarray(inputs["L0_attn_W_c"], f32),
        "W_d": np.asarray(inputs["L0_attn_W_d"], f32),
        "W_o0": np.asarray(inputs["L0_attn_W_o"], f32),
        "wq_s": ck(np.asarray(inputs["L1_attn_Wq"], f32)[c * QKVO_R:(c + 1) * QKVO_R]),
        "wk_s": ck(np.asarray(inputs["L1_attn_Wk"], f32)[c * QKVO_R:(c + 1) * QKVO_R]),
        "wv_s": ck(np.asarray(inputs["L1_attn_Wv"], f32)[c * QKVO_R:(c + 1) * QKVO_R]),
        "wo_s": ck(np.asarray(inputs["L1_attn_Wo"], f32)[c * QKVO_R:(c + 1) * QKVO_R]),
        "wo0t_s": ck(np.asarray(inputs["L0_attn_W_o"], f32).T[c * QKVO_R:(c + 1) * QKVO_R]),
        "wgu_s": ck(np.asarray(inputs["L1_ffn_w_gate_up"], f32)[c * WGU_R:(c + 1) * WGU_R]),
        "wdn_s": ck(np.asarray(inputs["L1_ffn_w_down"], f32)[c * WDN_R:(c + 1) * WDN_R]),
        "gix_s": gslice(gix),
        "uix_s": gslice(uix),
        "dix_s": ck(dix[c * D_ROWS:(c + 1) * D_ROWS]),
        "lutvals": lutvals,
        "ident": np.eye(128, dtype=BF),
        "r0t": _rot0T(), "r1t": _rot1T(),
        "cf0": cf0, "sf0": sf0,
        "cf1q": cf1q, "sf1q": sf1q, "cf1k": cf1k, "sf1k": sf1k,
        "masks": _masks(h),
        "bc_col": ck(np.asarray(inputs["L0_attn_b_c"], f32).reshape(4, 128).T),
        "bdqk_col": ck(b_d[:4096].reshape(32, 128).T),
        "bdv_row": ck(b_d[4096:].reshape(1, E)),
        "bo0_col": ck(np.asarray(inputs["L0_attn_b_o"], f32).reshape(NCH, 128).T),
        "bq1_col": ck(np.asarray(inputs["L1_attn_bq"], f32).reshape(NCH, 128).T),
        "bk1_col": ck(np.asarray(inputs["L1_attn_bk"], f32).reshape(NCH, 128).T),
        "bv1_row": ck(np.asarray(inputs["L1_attn_bv"], f32).reshape(1, E)),
        "bo1_col": ck(np.asarray(inputs["L1_attn_bo"], f32).reshape(NCH, 128).T),
    }


def kernel(**inputs):
    dbg = os.environ.get("MK_DBG", "")
    nc, dbg_names = build_program(dbg)
    in_maps = [_core_inputs(inputs, c) for c in range(8)]
    res = run_bass_kernel_spmd(nc, in_maps, core_ids=list(range(8)))
    if dbg:
        return res
    out = np.empty((B, S, E), np.float32)
    for c in range(8):
        b, h = divmod(c, 2)
        y = res.results[c]["y"]
        out[b].reshape(16, 128, E)[h::2] = y.T.reshape(NTB, 128, E)
    return out


# version 11
# speedup vs baseline: 1.1744x; 1.0567x over previous
import itertools
import os
import contextlib
from collections import defaultdict

import numpy as np
import ml_dtypes
import concourse.bass as bass
import concourse.mybir as mybir
from concourse.bass_utils import run_bass_kernel_spmd

F32 = mybir.dt.float32
BF16 = mybir.dt.bfloat16
I32 = mybir.dt.int32
AF = mybir.ActivationFunctionType
ALU = mybir.AluOpType
AX = mybir.AxisListType

B, S, E = 4, 2048, 2048
NCH = E // 128
NT = 1024
NTB = NT // 128
D_C, D_ROT = 512, 64
HEADS, HD = 16, 128
INTER0, INTER1 = 5504, 8192
LUTN = 16
SCALE = float(1.0 / np.sqrt(128.0))
EPS = 1e-5
NEG = -1e30
INV_E = 1.0 / 2048.0

G_ROWS = 768
D_ROWS = 256
QKVO_R, WGU_R, WDN_R = 256, 2048, 256
BF = ml_dtypes.bfloat16

_kvrr = itertools.cycle([0, 1, 2, 3])
_castrr = itertools.cycle([0, 1])

BLK = 128 * 16 * 128
OFF_WQ_B = 0
OFF_WK_B = 2 * BLK
OFF_WV_B = 4 * BLK
OFF_WO_B = OFF_WV_B + 16 * 128 * 256
OFF_WGU_B = OFF_WO_B + 2 * BLK
OFF_WDN_B = OFF_WGU_B + 16 * BLK
OFF_WO0_B = OFF_WDN_B + 8 * BLK
DENSE_SEG = OFF_WO0_B + 2 * BLK
L0_GRPS = [list(range(0, 16)), list(range(16, 32)), list(range(32, 43))]
DOWN_GRP_OFF = [0, 262144, 524288]
DOWN_SEG = 2 * 704512

PHASES = ["p1", "p2", "p3", "p4", "p5", "p6", "out"]


class Res:
    __slots__ = ("w", "r")
    def __init__(self):
        self.w = None
        self.r = []


class Em:
    NAMES = ("pe", "dve", "act", "pool", "qs", "qa", "qg")

    def __init__(self, nc, stack):
        self.nc = nc
        self.eng = {"pe": nc.tensor, "dve": nc.vector, "act": nc.scalar,
                    "pool": nc.gpsimd, "qs": nc.sync, "qa": nc.scalar,
                    "qg": nc.gpsimd}
        self.canon = {"pe": "pe", "dve": "dve", "act": "act", "pool": "pool",
                      "qs": "qs", "qa": "act", "qg": "pool"}
        self.sem = {n: stack.enter_context(nc.semaphore(f"sem_{n}")) for n in self.NAMES}
        self.dma_lanes = {"qs", "qa", "qg"}
        self.last_group = {"qs": None, "qa": None, "qg": None}
        self.auto_group = 0
        self.cc_sems = []
        self.stack = stack
        self.cnt = defaultdict(int)
        self.known = defaultdict(int)

    def lane(self, name):
        if name not in self.sem:
            self.sem[name] = self.stack.enter_context(self.nc.semaphore(f"lane_{name}"))
            self.dma_lanes.add(name)
            self.last_group[name] = None
        return name

    def _wait_one(self, waiter, sname, val):
        w = self.canon[waiter]
        if w == "pe" and sname == "pe":
            return
        if isinstance(sname, tuple):
            if self.known[(w, sname)] >= 1:
                return
            self.eng[waiter].wait_ge(self.cc_sems[sname[1]], 1)
            self.known[(w, sname)] = 1
        else:
            if self.known[(w, sname)] >= val:
                return
            self.eng[waiter].wait_ge(self.sem[sname], val)
            self.known[(w, sname)] = val

    def _deps(self, waiter, reads, writes):
        need = {}
        def add(s, t):
            need[s] = max(need.get(s, 0), t)
        for r in reads:
            if r.w is not None:
                s, t = r.w
                if isinstance(s, tuple):
                    self._wait_one(waiter, s, 1)
                else:
                    add(s, t)
        for wr in writes:
            if wr.w is not None:
                s, t = wr.w
                if isinstance(s, tuple):
                    self._wait_one(waiter, s, 1)
                else:
                    add(s, t)
            for (s, t) in wr.r:
                add(s, t)
        for s in sorted(need):
            self._wait_one(waiter, s, need[s])

    def op(self, eng, fn, reads=(), writes=()):
        self._deps(eng, reads, writes)
        instr = fn(self.eng[eng])
        instr.then_inc(self.sem[eng], 1)
        self.cnt[eng] += 1
        t = self.cnt[eng]
        for r in reads:
            r.r.append((eng, t))
        for w in writes:
            w.w = (eng, t)
            w.r = []
        return t

    def dma(self, q, out, in_, reads=(), writes=(), transpose=False, lane=None,
            group=None, **kw):
        ln_ = self.lane(lane) if lane else q
        if group is None:
            self.auto_group += 1
            group = ("__auto", self.auto_group)
        if self.last_group.get(ln_) != group:
            if self.cnt[ln_] > 0:
                self._wait_one(q, ln_, self.cnt[ln_])
            self.last_group[ln_] = group
        self._deps(q, reads, writes)
        e = self.eng[q]
        if transpose:
            instr = e.dma_start_transpose(out, in_)
        else:
            instr = e.dma_start(out=out, in_=in_, **kw)
        instr.then_inc(self.sem[ln_], 16)
        self.cnt[ln_] += 16
        t = self.cnt[ln_]
        for r in reads:
            r.r.append((ln_, t))
        for w in writes:
            w.w = (ln_, t)
            w.r = []
        return t

    def collective(self, kind, op, groups, ins, outs, reads=(), writes=()):
        self._deps("qg", reads, writes)
        idx = len(self.cc_sems)
        sem = self.stack.enter_context(self.nc.semaphore(f"cc{idx}"))
        self.cc_sems.append(sem)
        self.eng["qg"].collective_compute(
            kind, op, replica_groups=groups, ins=ins, outs=outs).then_inc(sem)
        for w in writes:
            w.w = (("__cc", idx), 1)
            w.r = []

    def barrier(self):
        snap = dict(self.cnt)
        for waiter in self.NAMES:
            for s, v in snap.items():
                if v:
                    self._wait_one(waiter, s, v)


class Acc:
    def __init__(self, em, bank_res, n):
        self.em = em
        self.bank = bank_res
        self.n = n
        self.i = 0
        self.touched = []

    def add(self, out_ap, lhsT_ap, rhs_ap, reads):
        em = self.em
        if self.i == 0:
            em._deps("pe", [], [self.bank])
        em._deps("pe", reads, [])
        last = (self.i == self.n - 1)
        instr = em.eng["pe"].matmul(out_ap, lhsT_ap, rhs_ap,
                                    start=(self.i == 0), stop=last)
        instr.then_inc(em.sem["pe"], 1)
        em.cnt["pe"] += 1
        t = em.cnt["pe"]
        for r in reads:
            r.r.append(("pe", t))
        self.i += 1
        if last:
            mode = os.environ.get("MK_NONOP", "1")
            if mode == "2":
                em.eng["pe"].nop(nofuse=True)
            elif mode != "1":
                em.eng["pe"].nop(nofuse=True).then_inc(em.sem["pe"], 1)
                em.cnt["pe"] += 1
            self.bank.w = ("pe", em.cnt["pe"])
            self.bank.r = []


def _positions(h):
    return np.concatenate([np.arange((2 * j + h) * 128, (2 * j + h + 1) * 128)
                           for j in range(NTB)])


def _rope0_fields(h):
    pos = _positions(h).astype(np.float64)
    inv = 1.0 / 10000.0 ** (np.arange(0, D_ROT, 2) / D_ROT)
    ang = pos[None, :] * inv[:, None]
    cf = np.ones((128, NT)); sf = np.zeros((128, NT))
    cf[0:32] = np.cos(ang); cf[32:64] = np.cos(ang)
    sf[0:32] = np.sin(ang); sf[32:64] = np.sin(ang)
    return cf.astype(BF), sf.astype(BF)


def _rope1_fields(h, scale):
    pos = _positions(h).astype(np.float64)
    inv = 10000.0 ** (-np.arange(0, HD, 2) / HD)
    ang = pos[None, :] * inv[:, None]
    cf = np.empty((128, NT)); sf = np.empty((128, NT))
    cf[0::2] = np.cos(ang); cf[1::2] = np.cos(ang)
    sf[0::2] = np.sin(ang); sf[1::2] = np.sin(ang)
    return (cf * scale).astype(BF), (sf * scale).astype(BF)


def _rot0T():
    r = np.zeros((128, 128), np.float32)
    for i in range(32):
        r[32 + i, i] = -1.0
        r[i, 32 + i] = 1.0
    return r.astype(BF)


def _rot1T():
    r = np.zeros((128, 128), np.float32)
    for i in range(64):
        r[2 * i + 1, 2 * i] = -1.0
        r[2 * i, 2 * i + 1] = 1.0
    return r.astype(BF)


def _masks(h):
    m = np.zeros((NTB, 2, 128, 128), np.float32)
    qi = np.arange(128); ki = np.arange(128)
    for j in range(NTB):
        for r in range(2):
            bad = ((2 * j + r) * 128 + ki[None, :]) > ((2 * j + h) * 128 + qi[:, None])
            m[j, r][bad] = NEG
    return m.astype(BF)


def build_program(dbg_stop=""):
    nc = bass.Bass("TRN2", target_bir_lowering=False)
    stack = contextlib.ExitStack()
    d = {}

    def din(name, shape, dtype):
        d[name] = nc.dram_tensor(name, shape, dtype, kind="ExternalInput").ap()
        return d[name]

    def dint(name, shape, dtype):
        d[name] = nc.dram_tensor(name, shape, dtype).ap()
        return d[name]

    def dshare(name, shape, dtype):
        d[name] = nc.dram_tensor(name, shape, dtype, addr_space="Shared").ap()
        return d[name]

    x_in = din("x_shard", [E, NT], F32)
    wc_in = din("W_c", [E, D_C], F32)
    wd_in = din("W_d", [D_C, 3 * E], F32)
    wo0_in = din("W_o0", [E, E], F32)
    wq_s = din("wq_s", [QKVO_R, E], F32)
    wk_s = din("wk_s", [QKVO_R, E], F32)
    wv_s = din("wv_s", [QKVO_R, E], F32)
    wo_s = din("wo_s", [QKVO_R, E], F32)
    wo0t_s = din("wo0t_s", [QKVO_R, E], F32)
    wgu_s = din("wgu_s", [WGU_R, E], F32)
    wdn_s = din("wdn_s", [WDN_R, INTER1], F32)
    gix_s = din("gix_s", [G_ROWS, E], I32)
    uix_s = din("uix_s", [G_ROWS, E], I32)
    dix_s = din("dix_s", [D_ROWS, INTER0], I32)
    lutvals = din("lutvals", [1, 2 * LUTN], F32)
    ident_in = din("ident", [128, 128], BF16)
    r0t_in = din("r0t", [128, 128], BF16)
    r1t_in = din("r1t", [128, 128], BF16)
    cf0_in = din("cf0", [128, NT], BF16)
    sf0_in = din("sf0", [128, NT], BF16)
    cf1q_in = din("cf1q", [128, NT], BF16)
    sf1q_in = din("sf1q", [128, NT], BF16)
    cf1k_in = din("cf1k", [128, NT], BF16)
    sf1k_in = din("sf1k", [128, NT], BF16)
    masks_in = din("masks", [NTB, 2, 128, 128], BF16)
    bc_in = din("bc_col", [128, 4], F32)
    bdqk_in = din("bdqk_col", [128, 32], F32)
    bdv_in = din("bdv_row", [1, E], F32)
    bo0_in = din("bo0_col", [128, NCH], F32)
    bq1_in = din("bq1_col", [128, NCH], F32)
    bk1_in = din("bk1_col", [128, NCH], F32)
    bv1_in = din("bv1_row", [1, E], F32)
    bo1_in = din("bo1_col", [128, NCH], F32)

    y_out = nc.dram_tensor("y", [E, NT], F32, kind="ExternalOutput").ap()

    gate_c = dint("gate_c", [G_ROWS, E], BF16)
    up_c = dint("up_c", [G_ROWS, E], BF16)
    down_c = dint("down_c", [D_ROWS, INTER0], BF16)
    gate_g = dshare("gate_g", [8 * G_ROWS, E], BF16)
    up_g = dshare("up_g", [8 * G_ROWS, E], BF16)
    down_g = dshare("down_g", [8 * D_ROWS, INTER0], BF16)
    dense_c = dint("dense_c", [4352, E], BF16)
    dense_g = dshare("dense_g", [8 * 4352, E], BF16)
    wo0_c = dint("wo0_c", [256, E], BF16)
    wo0_g = dshare("wo0_g", [8 * 256, E], BF16)
    kvk_c = [dint(f"kvk_c{l}", [HEADS * 128, NT], BF16) for l in range(2)]
    kvv_c = [dint(f"kvv_c{l}", [HEADS * NT, 128], BF16) for l in range(2)]
    kvk_g = [dint(f"kvk_g{l}", [2 * HEADS * 128, NT], BF16) for l in range(2)]
    kvv_g = [dint(f"kvv_g{l}", [2 * HEADS * NT, 128], BF16) for l in range(2)]

    sbufs = {}
    def sbuf(name, shape, dtype):
        t = stack.enter_context(nc.sbuf_tensor(name, shape, dtype))
        sbufs[name] = t
        return t

    x_sb = sbuf("x_sb", [128, NCH, NT], F32)
    xn_sb = sbuf("xn_sb", [128, NCH, NT], BF16)
    q_sb = sbuf("q_sb", [128, NCH, NT], BF16)
    ident = sbuf("ident_sb", [128, 128], BF16)
    r0t = sbuf("r0t_sb", [128, 128], BF16)
    r1t = sbuf("r1t_sb", [128, 128], BF16)
    kst = sbuf("kst", [128, NT], BF16)
    vst = sbuf("vst", [128, 512], BF16)
    sg_sb = sbuf("sg_sb", [128, 512], BF16)
    wbuf = sbuf("wbuf", [128, 2, NCH, 128], BF16)
    xsq = sbuf("xsq", [128, 2, 512], F32)
    lnt = sbuf("lnt", [128, 512], F32)
    vbias = sbuf("vbias", [128, 512], F32)
    stat = sbuf("stat", [1, 3, 512], F32)
    mx = sbuf("mx", [128, 8], F32)
    lsum = sbuf("lsum", [128, 8], F32)
    bc_sb = sbuf("bc_sb", [128, 4], F32)
    bdqk_sb = sbuf("bdqk_sb", [128, 32], F32)
    bdqs_sb = sbuf("bdqs_sb", [128, 16], F32)
    bo0_sb = sbuf("bo0_sb", [128, NCH], F32)
    bq1_sb = sbuf("bq1_sb", [128, NCH], F32)
    bk1_sb = sbuf("bk1_sb", [128, NCH], F32)
    bo1_sb = sbuf("bo1_sb", [128, NCH], F32)
    lutrow = sbuf("lutrow", [1, 2 * LUTN], F32)
    onesr = sbuf("onesr", [1, 128], F32)
    onesrb = sbuf("onesrb", [1, 128], BF16)
    onescol = sbuf("onescol", [128, 1], F32)
    epscol = sbuf("epscol", [128, 1], F32)
    lutb = sbuf("lutb", [128, LUTN], F32)
    dec_i = sbuf("dec_i", [128, 512], I32)
    AR = 24576
    abf = sbuf("abf", [128, AR], BF16)

    ps32 = stack.enter_context(nc.psum_tensor("ps32", [128, 3584], F32))
    psbf = stack.enter_context(nc.psum_tensor("psbf", [128, 1024], BF16))

    def pb(b):
        return ps32[:, 512 * b:512 * (b + 1)]

    def carve(off, *dims):
        n = int(np.prod(dims))
        ap = abf[:, off:off + n]
        if len(dims) > 1:
            pat = " ".join(f"d{i}" for i in range(len(dims)))
            ap = ap.rearrange(f"p ({pat}) -> p {pat}",
                              **{f"d{i}": dims[i] for i in range(len(dims))})
        return ap, off + n

    dbg_outs = []

    with stack:
        with nc.Block():
            em = Em(nc, stack)
            R = defaultdict(Res)
            bank = [R[f"bank{i}"] for i in range(7)]
            slotbf = [R[f"psbf{i}"] for i in range(8)]
            xr = [R[f"x{c}"] for c in range(NCH)]
            xnr = [R[f"xn{c}"] for c in range(NCH)]
            qr = [R[f"q{c}"] for c in range(NCH)]
            hr = [R[f"h{c}"] for c in range(NCH)]
            osr = [R[f"os{c}"] for c in range(NCH)]
            ptr = [R[f"pt{k}"] for k in range(16)]

            stop = PHASES.index(dbg_stop) if dbg_stop in PHASES else 99

            def dump(nm, shape, dt, ap, rds):
                o = nc.dram_tensor(nm, shape, dt, kind="ExternalOutput").ap()
                idxall = tuple(slice(None) for _ in shape)
                em.dma("qg", o[idxall], ap, reads=rds, writes=[R["dump" + nm]])
                dbg_outs.append(nm)

            em.dma("qs", x_sb[:, :, :], x_in.rearrange("(c p) t -> p c t", p=128),
                   writes=xr, lane="lx")
            for nm, dst, src in [("ident", ident, ident_in), ("r0t", r0t, r0t_in),
                                 ("r1t", r1t, r1t_in), ("bc", bc_sb, bc_in),
                                 ("bdqk", bdqk_sb, bdqk_in), ("bo0", bo0_sb, bo0_in),
                                 ("bq1", bq1_sb, bq1_in), ("bk1", bk1_sb, bk1_in),
                                 ("bo1", bo1_sb, bo1_in), ("lutrow", lutrow, lutvals)]:
                em.dma("qs", dst[:, :], src[:, :], writes=[R[nm]], lane="l" + nm)
            em.op("pool", lambda e: e.memset(onesr[:, :], 1.0), writes=[R["onesr"]])
            em.op("pool", lambda e: e.memset(onesrb[:, :], 1.0), writes=[R["onesrb"]])
            em.op("pool", lambda e: e.memset(onescol[:, :], 1.0), writes=[R["onescol"]])
            em.op("pool", lambda e: e.memset(epscol[:, :], EPS), writes=[R["eps"]])
            em.op("dve", lambda e: e.tensor_scalar(bdqs_sb[:, :], bdqk_sb[:, 0:16],
                  SCALE, None, ALU.mult), reads=[R["bdqk"]], writes=[R["bdqs"]])
            ga = Acc(em, bank[6], 1)
            ga.add(ps32[:, 6 * 512:6 * 512 + LUTN], onesr[0:1, :], lutrow[0:1, 0:LUTN],
                   [R["onesr"], R["lutrow"]])
            em.op("dve", lambda e: e.tensor_copy(lutb[:, :],
                  ps32[:, 6 * 512:6 * 512 + LUTN]), reads=[bank[6]],
                  writes=[R["lutb"]])

            dcw = R["dense_c"]
            dc_flat = dense_c.rearrange("a b -> (a b)")
            o0 = 0
            stgi_ap, o0 = carve(o0, 4, E)
            stgo_ap, o0 = carve(o0, 4, NCH, 128)
            prep_bi = [0]

            def prep_block(src_slice, dst_off, dstf=None, wres=None):
                s = prep_bi[0] % 4
                prep_bi[0] += 1
                sir, sor = R[f"stgi{s}"], R[f"stgo{s}"]
                em.dma("qg", stgi_ap[:, s, :], src_slice, writes=[sir],
                       lane=f"lcin{s}")
                for ec in range(NCH):
                    sl_ = ec % 8
                    em.op("pe", lambda e, ec=ec, sl_=sl_, s=s: e.transpose(
                        psbf[:, sl_ * 128:(sl_ + 1) * 128],
                        stgi_ap.rearrange("p t (a b) -> p t a b", a=NCH)[:, s, ec, :],
                        ident[:, :]),
                        reads=[sir, R["ident"]], writes=[slotbf[sl_]])
                    if sl_ == 7:
                        em.op("dve", lambda e, ec=ec, s=s: e.tensor_copy(
                            stgo_ap[:, s, ec - 7:ec + 1, :],
                            psbf.ap().rearrange("p (kb q) -> p kb q", kb=8)),
                            reads=[slotbf[i] for i in range(8)],
                            writes=[sor])
                em.dma("qs", (dc_flat if dstf is None else dstf)
                       [dst_off:dst_off + BLK]
                       .rearrange("(p a b) -> p a b", p=128, a=NCH),
                       stgo_ap[:, s, :, :], reads=[sor],
                       writes=[dcw if wres is None else wres],
                       lane=f"lcout{s}")

            wo0c_flat = wo0_c.rearrange("a b -> (a b)")
            for b in range(2):
                prep_block(wq_s[b * 128:(b + 1) * 128, :], OFF_WQ_B + b * BLK)
                prep_block(wk_s[b * 128:(b + 1) * 128, :], OFF_WK_B + b * BLK)
                prep_block(wo_s[b * 128:(b + 1) * 128, :], OFF_WO_B + b * BLK)
                prep_block(wo0t_s[b * 128:(b + 1) * 128, :], b * BLK,
                           dstf=wo0c_flat, wres=R["wo0_c"])
            for j in range(16):
                prep_block(wgu_s[j * 128:(j + 1) * 128, :], OFF_WGU_B + j * BLK)
            for gi in range(4):
                for b in range(2):
                    prep_block(wdn_s[b * 128:(b + 1) * 128, gi * E:(gi + 1) * E],
                               OFF_WDN_B + (gi * 2 + b) * BLK)
            vv = dc_flat[OFF_WV_B:OFF_WV_B + 16 * 128 * 256]\
                .rearrange("(ec p f2) -> p ec f2", ec=16, p=128)
            for fr in range(2):
                s = prep_bi[0] % 4
                prep_bi[0] += 1
                sir, sor = R[f"stgi{s}"], R[f"stgo{s}"]
                em.dma("qg", stgi_ap[:, s, :], wv_s[fr * 128:(fr + 1) * 128, :],
                       writes=[sir], lane=f"lcin{s}")
                for ec in range(NCH):
                    sl_ = ec % 8
                    em.op("pe", lambda e, ec=ec, sl_=sl_, s=s: e.transpose(
                        psbf[:, sl_ * 128:(sl_ + 1) * 128],
                        stgi_ap.rearrange("p t (a b) -> p t a b", a=NCH)[:, s, ec, :],
                        ident[:, :]),
                        reads=[sir, R["ident"]], writes=[slotbf[sl_]])
                    if sl_ == 7:
                        em.op("dve", lambda e, ec=ec, s=s: e.tensor_copy(
                            stgo_ap[:, s, ec - 7:ec + 1, :],
                            psbf.ap().rearrange("p (kb q) -> p kb q", kb=8)),
                            reads=[slotbf[i] for i in range(8)],
                            writes=[sor])
                em.dma("qs", vv[:, :, fr * 128:(fr + 1) * 128], stgo_ap[:, s, :, :],
                       reads=[sor], writes=[dcw], lane=f"lcout{s}")
            em.barrier()

            def ln():
                for th in range(2):
                    sl = slice(512 * th, 512 * (th + 1))
                    a1 = Acc(em, bank[4], NCH)
                    a2 = Acc(em, bank[5], NCH)
                    for ec in range(NCH):
                        em.op("act", lambda e, ec=ec: e.activation(
                            xsq[:, ec % 2, :], x_sb[:, ec, sl], AF.Square),
                            reads=[xr[ec]], writes=[R[f"xsq{ec % 2}"]])
                        a1.add(pb(4)[0:1, :], onescol[:, :], x_sb[:, ec, sl],
                               [R["onescol"], xr[ec]])
                        a2.add(pb(5)[0:1, :], onescol[:, :], xsq[:, ec % 2, :],
                               [R["onescol"], R[f"xsq{ec % 2}"]])
                    em.op("dve", lambda e: e.tensor_scalar(stat[:, 0, :], pb(4)[0:1, :],
                          INV_E, None, ALU.mult), reads=[bank[4]], writes=[R["st0"]])
                    em.op("dve", lambda e: e.tensor_scalar(stat[:, 1, :], pb(5)[0:1, :],
                          INV_E, None, ALU.mult), reads=[bank[5]], writes=[R["st1"]])
                    em.op("dve", lambda e: e.tensor_tensor(stat[:, 2, :], stat[:, 0, :],
                          stat[:, 0, :], ALU.mult), reads=[R["st0"]], writes=[R["st2"]])
                    em.op("dve", lambda e: e.tensor_tensor(stat[:, 1, :], stat[:, 1, :],
                          stat[:, 2, :], ALU.subtract), reads=[R["st1"], R["st2"]],
                          writes=[R["st1"]])
                    em.op("act", lambda e: e.activation(stat[:, 1, :], stat[:, 1, :],
                          AF.Ln, bias=epscol[0:1, :]), reads=[R["st1"], R["eps"]],
                          writes=[R["st1"]])
                    em.op("act", lambda e: e.activation(stat[:, 1, :], stat[:, 1, :],
                          AF.Exp, scale=-0.5), reads=[R["st1"]], writes=[R["st1"]])
                    em.op("dve", lambda e: e.scalar_tensor_tensor(stat[:, 2, :],
                          stat[:, 0, :], -1.0, stat[:, 1, :], ALU.mult, ALU.mult),
                          reads=[R["st0"], R["st1"]], writes=[R["st2"]])
                    aa = Acc(em, bank[4], 1)
                    aa.add(pb(4), onesr[0:1, :], stat[0:1, 1, :], [R["onesr"], R["st1"]])
                    ab = Acc(em, bank[5], 1)
                    ab.add(pb(5), onesr[0:1, :], stat[0:1, 2, :], [R["onesr"], R["st2"]])
                    for ec in range(NCH):
                        em.op("dve", lambda e, ec=ec: e.tensor_tensor(
                            lnt[:, :], x_sb[:, ec, sl], pb(4), ALU.mult),
                            reads=[xr[ec], bank[4]], writes=[R["lnt"]])
                        em.op("dve", lambda e, ec=ec: e.tensor_tensor(
                            xn_sb[:, ec, sl], lnt[:, :], pb(5), ALU.add),
                            reads=[R["lnt"], bank[5]], writes=[xnr[ec]])

            o = 0
            wc_ap, o = carve(o, NCH, D_C)
            wdq_ap, o = carve(o, 4, 1536)
            c_ap, o = carve(o, 4, NT)
            cf0s_ap, o = carve(o, 2, NT)
            bdvrow_ap, o = carve(o, E)
            assert o <= AR

            ln()
            if stop <= 0:
                dump("xn_d", [128, NCH, NT], BF16, xn_sb[:, :, :], list(xnr))

            if stop > 0:
                cr = [R[f"c{c}"] for c in range(4)]
                em.dma("qg", wc_ap, wc_in.rearrange("(c p) f -> p c f", p=128),
                       writes=[R["wc"]], lane="lwc")
                em.dma("qs", cf0s_ap[:, 0, :], cf0_in[:, :], writes=[R["cf0"]], lane="lcf")
                em.dma("qs", cf0s_ap[:, 1, :], sf0_in[:, :], writes=[R["sf0"]], lane="lsf")
                em.dma("qg", bdvrow_ap[0:1, :], bdv_in[:, :], writes=[R["bdv"]], lane="lbdv")
                for fc in range(4):
                    for th in range(2):
                        sl = slice(512 * th, 512 * (th + 1))
                        bk = 5 + (fc + th) % 2
                        a = Acc(em, bank[bk], NCH)
                        for ec in range(NCH):
                            a.add(pb(bk), wc_ap[:, ec, fc * 128:(fc + 1) * 128],
                                  xn_sb[:, ec, sl], [R["wc"], xnr[ec]])
                        em.op("act", lambda e, fc=fc, sl=sl, bk=bk: e.activation(
                            c_ap[:, fc, sl], pb(bk), AF.Identity, bias=bc_sb[:, fc:fc + 1]),
                            reads=[R["bc"], bank[bk]], writes=[cr[fc]])
                for th in range(2):
                    sl = slice(512 * th, 512 * (th + 1))
                    a = Acc(em, bank[4], 1)
                    a.add(pb(4), r0t[:, :], c_ap[:, 0, sl], [R["r0t"], cr[0]])
                    em.op("dve", lambda e, sl=sl: e.tensor_tensor(
                        pb(3), pb(4), cf0s_ap[:, 1, sl], ALU.mult),
                        reads=[bank[4], R["sf0"]], writes=[bank[3]])
                    em.op("dve", lambda e, sl=sl: e.tensor_tensor(
                        sg_sb[:, :], c_ap[:, 0, sl], cf0s_ap[:, 0, sl], ALU.mult),
                        reads=[cr[0], R["cf0"]], writes=[R["sg0"]])
                    em.op("dve", lambda e, sl=sl: e.tensor_tensor(
                        c_ap[:, 0, sl], sg_sb[:, :], pb(3), ALU.add),
                        reads=[R["sg0"], bank[3]], writes=[cr[0]])

                def wd_cols(qd):
                    em.dma("qg", wdq_ap,
                           wd_in.rearrange("(c p) f -> p c f", p=128)
                           [:, :, qd * 1536:(qd + 1) * 1536], writes=[R["wd"]], lane="lwd")

                def hcol(fcol, width, evac):
                    qd = fcol // 1536
                    loc = fcol % 1536
                    for th in range(2):
                        sl = slice(512 * th, 512 * (th + 1))
                        bk = 5 + th % 2
                        a = Acc(em, bank[bk], 4)
                        for cc in range(4):
                            a.add(pb(bk), wdq_ap[:, cc, loc:loc + 128],
                                  c_ap[:, cc, sl], [R["wd"], cr[cc]])
                        evac(bk, th, sl)

                wd_cols(0)
                for hd in range(12):
                    hcol(hd * 128, 128, lambda bk, th, sl, hd=hd: em.op(
                        "act", lambda e: e.activation(q_sb[:, hd, sl], pb(bk), AF.Identity,
                                                      bias=bdqs_sb[:, hd:hd + 1],
                                                      scale=SCALE),
                        reads=[R["bdqs"], bank[bk]], writes=[qr[hd]]))
                wd_cols(1)
                for hd in range(12, 16):
                    hcol(hd * 128, 128, lambda bk, th, sl, hd=hd: em.op(
                        "act", lambda e: e.activation(q_sb[:, hd, sl], pb(bk), AF.Identity,
                                                      bias=bdqs_sb[:, hd:hd + 1],
                                                      scale=SCALE),
                        reads=[R["bdqs"], bank[bk]], writes=[qr[hd]]))
                def k_evac(hd):
                    kres = R["kst"]
                    def f(bk, th, sl):
                        em.op("act", lambda e: e.activation(
                            kst[:, sl], pb(bk), AF.Identity,
                            bias=bdqk_sb[:, 16 + hd:17 + hd]),
                            reads=[R["bdqk"], bank[bk]], writes=[kres])
                        if th == 1:
                            em.dma("qg", kvk_c[0][hd * 128:(hd + 1) * 128, :],
                                   kst[:, :], reads=[kres], writes=[R["kvk_c0"]], lane=f"lkvc{next(_kvrr)}")
                    return f
                for hd in range(8):
                    hcol(2048 + hd * 128, 128, k_evac(hd))
                wd_cols(2)
                for hd in range(8, 16):
                    hcol(2048 + hd * 128, 128, k_evac(hd))
                def vpart(fv):
                    qd = (4096 + fv * 512) // 1536
                    loc = (4096 + fv * 512) % 1536
                    a = Acc(em, bank[6], 1)
                    a.add(pb(6), onesrb[0:1, :], bdvrow_ap[0:1, fv * 512:(fv + 1) * 512],
                          [R["onesrb"], R["bdv"]])
                    em.op("dve", lambda e: e.tensor_copy(vbias[:, :], pb(6)),
                          reads=[bank[6]], writes=[R["vb0"]])
                    for ts_ in range(8):
                        vres = R["vst"]
                        bk = 4 + ts_ % 2
                        a = Acc(em, bank[bk], 4)
                        for cc in range(4):
                            a.add(pb(bk), c_ap[:, cc, ts_ * 128:(ts_ + 1) * 128],
                                  wdq_ap[:, cc, loc:loc + 512], [cr[cc], R["wd"]])
                        em.op("dve", lambda e, bk=bk: e.tensor_tensor(
                            vst[:, :], pb(bk), vbias[:, :], ALU.add),
                            reads=[bank[bk], R["vb0"]], writes=[vres])
                        for hl in range(4):
                            hd = fv * 4 + hl
                            em.dma("qg",
                                   kvv_c[0][hd * NT + ts_ * 128: hd * NT + (ts_ + 1) * 128, :],
                                   vst[:, hl * 128:(hl + 1) * 128],
                                   reads=[vres], writes=[R["kvv_c0"]], lane=f"lkvc{next(_kvrr)}")
                vpart(0)
                wd_cols(3)
                for fv in range(1, 4):
                    vpart(fv)

                em.collective("AllGather", ALU.bypass, [[0, 1], [2, 3], [4, 5], [6, 7]],
                              [kvk_c[0].opt()], [kvk_g[0].opt()],
                              reads=[R["kvk_c0"]], writes=[R["kvk_g0"]])
                em.collective("AllGather", ALU.bypass, [[0, 1], [2, 3], [4, 5], [6, 7]],
                              [kvv_c[0].opt()], [kvv_g[0].opt()],
                              reads=[R["kvv_c0"]], writes=[R["kvv_g0"]])
                em.collective("AllGather", ALU.bypass, [list(range(8))],
                              [wo0_c.opt()], [wo0_g.opt()],
                              reads=[R["wo0_c"]], writes=[R["wo0_done"]])
                em.collective("AllGather", ALU.bypass, [list(range(8))],
                              [dense_c.opt()], [dense_g.opt()],
                              reads=[dcw], writes=[R["dense_done"]])
            if stop == 1:
                dump("q_d", [128, NCH, NT], BF16, q_sb[:, :, :], list(qr))
                dump("c_d", [128, 4, NT], BF16, c_ap, [R[f"c{c}"] for c in range(4)])

            em.barrier()
            o = 0
            decb_ap, o = carve(o, 512)
            dect_ap, o = carve(o, 512)
            deca_ap, o = carve(o, 512)
            dstg_ap, o = carve(o, 4, 128)
            kh_ap, o = carve(o, 2, 2 * NT)
            vh_ap, o = carve(o, 2, 16, 128)
            p_ap, o = carve(o, 2, 2 * NT)
            pt_ap, o = carve(o, 16, 128)
            osb_ap, o = carve(o, NCH, 512)
            assert o <= AR, o

            gate_f = gate_c.rearrange("a b -> (a b)")
            up_f = up_c.rearrange("a b -> (a b)")
            down_f = down_c.rearrange("a b -> (a b)")

            def decode_steps():
                tiles = []
                for t in range(6):
                    tiles.append(("g", gix_s, gate_f, t, E))
                for t in range(6):
                    tiles.append(("u", uix_s, up_f, t, E))
                for t in range(2):
                    tiles.append(("d", dix_s, down_f, t, INTER0))
                for (which, isrc, dstf, t, width) in tiles:
                    nsub = (width + 511) // 512
                    nic = width // 128
                    icq = []
                    for su in range(nsub):
                        w = min(512, width - su * 512)
                        ires = R["dec_i"]
                        ares = R["dec_a"]
                        def load(isrc=isrc, t=t, su=su, w=w, ires=ires):
                            em.dma("qs", dec_i[:, 0:w],
                                   isrc[t * 128:(t + 1) * 128, su * 512:su * 512 + w],
                                   writes=[ires], lane="ldeci")
                        yield load
                        def conv(w=w, ires=ires):
                            em.op("dve", lambda e: e.tensor_copy(
                                decb_ap[:, 0:w], dec_i[:, 0:w]),
                                reads=[ires], writes=[R["dec_b"]])
                        yield conv
                        for v in range(LUTN):
                            def step(v=v, w=w, ares=ares):
                                if v == 0:
                                    em.op("dve", lambda e: e.tensor_scalar(
                                        deca_ap[:, 0:w], decb_ap[:, 0:w],
                                        0.0, lutb[:, 0:1],
                                        ALU.is_equal, ALU.mult),
                                        reads=[R["dec_b"], R["lutb"]],
                                        writes=[ares])
                                else:
                                    em.op("dve", lambda e: e.tensor_scalar(
                                        dect_ap[:, 0:w], decb_ap[:, 0:w],
                                        float(v), lutb[:, v:v + 1],
                                        ALU.is_equal, ALU.mult),
                                        reads=[R["dec_b"], R["lutb"]],
                                        writes=[R["dec_t"]])
                                    em.op("dve", lambda e: e.tensor_tensor(
                                        deca_ap[:, 0:w], deca_ap[:, 0:w],
                                        dect_ap[:, 0:w], ALU.add),
                                        reads=[R["dec_t"]], writes=[ares])
                            yield step
                        nj = w // 128
                        ic0 = su * 4
                        if which in ("g", "u"):
                            dst = dstf[t * BLK:(t + 1) * BLK]\
                                .rearrange("(p a b) -> p a b", p=128, a=NCH)\
                                [:, ic0:ic0 + nj, :]
                        else:
                            gi = ic0 // 16
                            ii0 = ic0 - 16 * gi
                            base = t * 704512 + DOWN_GRP_OFF[gi]
                            glen = len(L0_GRPS[gi])
                            dst = dstf[base:base + 128 * glen * 128]\
                                .rearrange("(p a b) -> p a b", p=128, a=glen)\
                                [:, ii0:ii0 + nj, :]
                        def tpflush(nj=nj, dst=dst, ares=ares):
                            for j in range(nj):
                                em.op("pe", lambda e, j=j: e.transpose(
                                    psbf[:, j * 128:(j + 1) * 128],
                                    deca_ap[:, j * 128:(j + 1) * 128], ident[:, :]),
                                    reads=[ares, R["ident"]], writes=[slotbf[j]])
                            em.op("dve", lambda e: e.tensor_copy(
                                dstg_ap[:, 0:nj, :],
                                psbf.ap().rearrange("p (kb q) -> p kb q", kb=8)
                                [:, 0:nj, :]),
                                reads=[slotbf[i] for i in range(nj)],
                                writes=[R["dstg"]])
                            em.dma("qs", dst, dstg_ap[:, 0:nj, :],
                                   reads=[R["dstg"]],
                                   writes=[R["gh_cw"]], lane="lghc")
                        yield tpflush
                    if (which, t) == ("g", 5):
                        def cc():
                            em.collective("AllGather", ALU.bypass, [list(range(8))],
                                          [gate_c.opt()], [gate_g.opt()],
                                          reads=[R["gh_cw"]], writes=[R["gate_done"]])
                        yield cc
                    if (which, t) == ("u", 5):
                        def cc():
                            em.collective("AllGather", ALU.bypass, [list(range(8))],
                                          [up_c.opt()], [up_g.opt()],
                                          reads=[R["gh_cw"]], writes=[R["up_done"]])
                        yield cc
                    if (which, t) == ("d", 1):
                        def cc():
                            em.collective("AllGather", ALU.bypass, [list(range(8))],
                                          [down_c.opt()], [down_g.opt()],
                                          reads=[R["gh_cw"]], writes=[R["down_done"]])
                        yield cc

            dec_gen = decode_steps() if stop > 1 else iter(())

            def pump(k):
                for _ in range(k):
                    s = next(dec_gen, None)
                    if s is None:
                        return
                    s()

            def attention(layer):
                kvkg, kvvg = kvk_g[layer], kvv_g[layer]
                gres_k, gres_v = R[f"kvk_g{layer}"], R[f"kvv_g{layer}"]
                maskres = R["masks"]

                def kv_load(u):
                    g_, hd_ = divmod(u, HEADS)
                    sb_ = u % 2
                    for r in range(2):
                        em.dma("qa", kh_ap[:, sb_, r * NT:(r + 1) * NT],
                               kvkg[(r * HEADS + hd_) * 128:(r * HEADS + hd_ + 1) * 128, :],
                               reads=[gres_k], writes=[R[f"kh{sb_}"]],
                               lane="lkh", group=("kh", layer, g_, hd_))
                        em.dma("qa", vh_ap[:, sb_, r * 8:(r + 1) * 8, :],
                               kvvg[(r * HEADS + hd_) * NT:(r * HEADS + hd_ + 1) * NT, :]
                               .rearrange("(kb p) dd -> p kb dd", p=128),
                               reads=[gres_v], writes=[R[f"vh{sb_}"]],
                               lane=f"lvh{sb_}", group=("vh", layer, g_, hd_))

                kv_load(0)
                for g in range(2):
                    for hd in range(HEADS):
                        u = g * HEADS + hd
                        sb = u % 2
                        khres, vhres = R[f"kh{sb}"], R[f"vh{sb}"]

                        def s_stage(qs_):
                            j = g * 4 + qs_
                            nblk = (j + 1) if layer == 1 else 8
                            L = 2 * nblk * 128
                            qt = q_sb[:, hd, j * 128:(j + 1) * 128]
                            pieces = []
                            for r in range(2):
                                seg0 = r * nblk * 128
                                rem = nblk * 128
                                off = 0
                                while rem > 0:
                                    col = seg0 + off
                                    n = min(512 - col % 512, rem, 512)
                                    pieces.append((r, off, col, n))
                                    off += n
                                    rem -= n
                            for (r, off, col, n) in pieces:
                                b0 = col // 512
                                a = Acc(em, bank[b0], 1)
                                a.add(ps32[:, col:col + n], qt,
                                      kh_ap[:, sb, r * NT + off: r * NT + off + n],
                                      [qr[hd], khres])
                            return nblk, L

                        def softmax_stage(qs_, nblk, L):
                            j = g * 4 + qs_
                            pp = qs_ % 2
                            nbk = (L + 511) // 512
                            if layer == 1:
                                for r in range(2):
                                    cm = (nblk * (r + 1) - 1) * 128
                                    em.op("dve", lambda e, cm=cm, r=r, j=j: e.tensor_tensor(
                                        ps32[:, cm:cm + 128], ps32[:, cm:cm + 128],
                                        masks_ap[:, j * 2 + r, :], ALU.add),
                                        reads=[maskres, bank[cm // 512]],
                                        writes=[bank[cm // 512]])
                            bks = [bank[b0] for b0 in range(nbk)]
                            em.op("dve", lambda e, L=L: e.reduce_max(
                                mx[:, 6:7], ps32[:, 0:L], axis=AX.X),
                                reads=bks, writes=[R["mx6"]])
                            em.op("dve", lambda e: e.tensor_scalar(
                                mx[:, 7:8], mx[:, 6:7], -1.0, None, ALU.mult),
                                reads=[R["mx6"]], writes=[R["mx7"]])
                            em.op("act", lambda e, L=L, pp=pp: e.activation(
                                p_ap[:, pp, 0:L], ps32[:, 0:L], AF.Exp,
                                bias=mx[:, 7:8], accum_out=lsum[:, 6:7]),
                                reads=[R["mx7"]] + bks,
                                writes=[R[f"p{pp}"], R["ls6"]])
                            em.op("dve", lambda e: e.reciprocal(lsum[:, 7:8],
                                  lsum[:, 6:7]), reads=[R["ls6"]], writes=[R["ls7"]])
                            em.op("act", lambda e, L=L, pp=pp: e.activation(
                                p_ap[:, pp, 0:L], p_ap[:, pp, 0:L], AF.Identity,
                                scale=lsum[:, 7:8]),
                                reads=[R["ls7"]], writes=[R[f"p{pp}"]])

                        def pe_stage(qs_, nblk):
                            pp = qs_ % 2
                            nkb = 2 * nblk
                            for kb in range(nkb):
                                sl_ = kb % 8
                                em.op("pe", lambda e, kb=kb, sl_=sl_, pp=pp: e.transpose(
                                    psbf[:, sl_ * 128:(sl_ + 1) * 128],
                                    p_ap[:, pp, kb * 128:(kb + 1) * 128], ident[:, :]),
                                    reads=[R[f"p{pp}"], R["ident"]], writes=[slotbf[sl_]])
                                if sl_ == 7 or kb == nkb - 1:
                                    k0 = kb - sl_
                                    em.op("dve", lambda e, k0=k0, sl_=sl_, kb=kb:
                                          e.tensor_copy(
                                              pt_ap[:, k0:kb + 1, :],
                                              psbf.ap().rearrange(
                                                  "p (kb qq) -> p kb qq", kb=8)
                                              [:, 0:sl_ + 1, :]),
                                          reads=[slotbf[i] for i in range(sl_ + 1)],
                                          writes=[ptr[i] for i in range(k0, kb + 1)])
                            a = Acc(em, bank[4], nkb)
                            for kb in range(nkb):
                                r, loc = divmod(kb, nblk)
                                vb = r * 8 + loc
                                a.add(ps32[:, 4 * 512:4 * 512 + 128],
                                      vh_ap[:, sb, vb, :], pt_ap[:, kb, :],
                                      [vhres, ptr[kb]])
                            em.op("act", lambda e, hd=hd, qs_=qs_: e.activation(
                                osb_ap[:, hd, qs_ * 128:(qs_ + 1) * 128],
                                ps32[:, 4 * 512:4 * 512 + 128], AF.Copy),
                                reads=[bank[4]], writes=[osr[hd]])

                        meta = {}
                        meta[0] = s_stage(0)
                        softmax_stage(0, *meta[0])
                        for j in range(1, 4):
                            meta[j] = s_stage(j)
                            if j == 3 and u + 1 < 2 * HEADS:
                                kv_load(u + 1)
                            pe_stage(j - 1, meta[j - 1][0])
                            softmax_stage(j, *meta[j])
                            if layer == 0:
                                pump(11)
                        pe_stage(3, meta[3][0])
                        if layer == 0:
                            pump(12)
                    tsl = slice(g * 512, (g + 1) * 512)
                    if True:
                        wo_bias = bo0_sb if layer == 0 else bo1_sb
                        wo_bres = R["bo0"] if layer == 0 else R["bo1"]
                        def wo_fill(fc, q):
                            wres = R[f"wb{fc % 2}"]
                            rank, b = fc // 2, fc % 2
                            if layer == 0:
                                flat = wo0_g.rearrange("a b -> (a b)")
                                off = rank * 2 * BLK + b * BLK
                                rd = R["wo0_done"]
                            else:
                                flat = dense_g.rearrange("a b -> (a b)")
                                off = rank * DENSE_SEG + OFF_WO_B + b * BLK
                                rd = R["dense_done"]
                            em.dma(q, wbuf[:, fc % 2, :, :],
                                   flat[off:off + BLK].rearrange(
                                       "(p a c) -> p a c", p=128, a=NCH),
                                   reads=[rd],
                                   writes=[wres], lane=f"lwb{fc % 2}",
                                   group=("wo", layer, g, fc))
                            return wres
                        wo_fill(0, "qs")
                        for fc in range(NCH):
                            wres = R[f"wb{fc % 2}"]
                            if fc + 1 < NCH:
                                wo_fill(fc + 1, "qs")
                            a = Acc(em, bank[5], NCH)
                            for ec in range(NCH):
                                a.add(pb(5), wbuf[:, fc % 2, ec, :], osb_ap[:, ec, :],
                                      [wres, osr[ec]])
                            em.op("dve", lambda e, fc=fc, wo_bias=wo_bias,
                                  wo_bres=wo_bres: e.scalar_tensor_tensor(
                                x_sb[:, fc, tsl], pb(5), wo_bias[:, fc:fc + 1],
                                x_sb[:, fc, tsl], ALU.add, ALU.add),
                                reads=[bank[5], wo_bres], writes=[xr[fc]])

            def ffn(layer, h_ap, wx_ap):
                hgroups = ([list(range(0, 16)), list(range(16, 32)), list(range(32, 43))]
                           if layer == 0 else
                           [list(range(g * 16, (g + 1) * 16)) for g in range(4)])

                def slotap(s):
                    return wbuf[:, s, :, :] if s < 2 else wx_ap[:, s - 2, :, :]

                def fill(which, ic, s, q):
                    wres = R[f"ws{s}"]
                    if which == "g":
                        if layer == 0:
                            rank, lo = ic // 6, ic % 6
                            flat = gate_g.rearrange("a b -> (a b)")
                            off = rank * G_ROWS * E + lo * BLK
                            rd = R["gate_done"]
                        else:
                            rank, lo = ic // 16, ic % 16
                            flat = dense_g.rearrange("a b -> (a b)")
                            off = rank * DENSE_SEG + OFF_WGU_B + lo * BLK
                            rd = R["dense_done"]
                        dst = slotap(s)[:, :, :]
                        srcap = flat[off:off + BLK].rearrange(
                            "(p a b) -> p a b", p=128, a=NCH)
                        em.dma(q, dst, srcap, reads=[rd], writes=[wres],
                               lane=f"lws{s}", group=("f", layer, which, str(ic)))
                        return wres
                    if which == "u":
                        if layer == 0:
                            rank, lo = ic // 6, ic % 6
                            flat = up_g.rearrange("a b -> (a b)")
                            off = rank * G_ROWS * E + lo * BLK
                            rd = R["up_done"]
                        else:
                            icu = 64 + ic
                            rank, lo = icu // 16, icu % 16
                            flat = dense_g.rearrange("a b -> (a b)")
                            off = rank * DENSE_SEG + OFF_WGU_B + lo * BLK
                            rd = R["dense_done"]
                        dst = slotap(s)[:, :, :]
                        srcap = flat[off:off + BLK].rearrange(
                            "(p a b) -> p a b", p=128, a=NCH)
                        em.dma(q, dst, srcap, reads=[rd], writes=[wres],
                               lane=f"lws{s}", group=("f", layer, which, str(ic)))
                        return wres
                    gi, fc, grp = ic
                    rank, b = fc // 2, fc % 2
                    glen = len(grp)
                    if layer == 0:
                        flat = down_g.rearrange("a b -> (a b)")
                        off = rank * DOWN_SEG + b * 704512 + DOWN_GRP_OFF[gi]
                        rd = R["down_done"]
                    else:
                        flat = dense_g.rearrange("a b -> (a b)")
                        off = rank * DENSE_SEG + OFF_WDN_B + (gi * 2 + b) * BLK
                        rd = R["dense_done"]
                    dst = slotap(s)[:, 0:glen, :]
                    srcap = flat[off:off + 128 * glen * 128].rearrange(
                        "(p a b) -> p a b", p=128, a=glen)
                    em.dma(q, dst, srcap, reads=[rd], writes=[wres],
                           lane=f"lws{s}", group=("f", layer, which, str(ic)))
                    return wres

                allc = [ic for grp in hgroups for ic in grp]
                fill("g", allc[0], 0, "qs")
                fill("u", allc[0], 2, "qs")
                k = 0
                for gi, grp in enumerate(hgroups):
                    for ii, ic in enumerate(grp):
                        if k + 1 < len(allc):
                            fill("g", allc[k + 1], (k + 1) % 2, "qs")
                            fill("u", allc[k + 1], 2 + (k + 1) % 2, "qs")
                        wg = R[f"ws{k % 2}"]
                        wu = R[f"ws{2 + k % 2}"]
                        gs = slotap(k % 2)
                        us = slotap(2 + k % 2)
                        for th in range(2):
                            sl = slice(512 * th, 512 * (th + 1))
                            a5 = Acc(em, bank[5], NCH)
                            for ec in range(NCH):
                                a5.add(pb(5), gs[:, ec, :], xn_sb[:, ec, sl],
                                       [wg, xnr[ec]])
                            a6 = Acc(em, bank[6], NCH)
                            for ec in range(NCH):
                                a6.add(pb(6), us[:, ec, :], xn_sb[:, ec, sl],
                                       [wu, xnr[ec]])
                            em.op("act", lambda e: e.activation(
                                sg_sb[:, :], pb(5), AF.Silu),
                                reads=[bank[5]], writes=[R["sg0"]])
                            em.op("dve", lambda e, ii=ii, sl=sl: e.tensor_tensor(
                                h_ap[:, ii, sl], pb(6), sg_sb[:, :], ALU.mult),
                                reads=[bank[6], R["sg0"]], writes=[hr[ii]])
                        k += 1
                    fill("d", (gi, 0, grp), 4, "qs")
                    for fc in range(NCH):
                        if fc + 1 < NCH:
                            fill("d", (gi, fc + 1, grp), 4 + (fc + 1) % 2, "qs")
                        wres = R[f"ws{4 + fc % 2}"]
                        ds = slotap(4 + fc % 2)
                        for th in range(2):
                            sl = slice(512 * th, 512 * (th + 1))
                            a = Acc(em, bank[4], len(grp))
                            for ii in range(len(grp)):
                                a.add(pb(4), ds[:, ii, :], h_ap[:, ii, sl],
                                      [wres, hr[ii]])
                            em.op("dve", lambda e, fc=fc, sl=sl: e.tensor_tensor(
                                x_sb[:, fc, sl], x_sb[:, fc, sl], pb(4), ALU.add),
                                reads=[bank[4]], writes=[xr[fc]])

            masks_ap = None
            if stop > 1:
                attention(0)
                pump(10 ** 9)
            if stop == 2:
                dump("x_d", [128, NCH, NT], F32, x_sb[:, :, :], list(xr))

            if stop > 2:
                em.barrier()
                o = 0
                h_ap, o = carve(o, NCH, NT)
                wx_ap, o = carve(o, 4, NCH, 128)
                ffn(0, h_ap, wx_ap)
            if stop == 3:
                dump("x_d", [128, NCH, NT], F32, x_sb[:, :, :], list(xr))

            if stop > 3:
                em.barrier()
                o = 0
                cf1_ap, o = carve(o, 4, NT)
                bv1row_ap, o = carve(o, E)
                vw2_ap, o = carve(o, 2, 512)
                ln()
                em.dma("qs", cf1_ap[:, 0, :], cf1q_in[:, :], writes=[R["cf1q"]], lane="lcf")
                em.dma("qs", cf1_ap[:, 1, :], sf1q_in[:, :], writes=[R["sf1q"]], lane="lsf")
                em.dma("qs", cf1_ap[:, 2, :], cf1k_in[:, :], writes=[R["cf1k"]], lane="lcf2")
                em.dma("qs", cf1_ap[:, 3, :], sf1k_in[:, :], writes=[R["sf1k"]], lane="lsf2")
                em.dma("qg", bv1row_ap[0:1, :], bv1_in[:, :], writes=[R["bv1"]], lane="lbdv")

                dg_flat = dense_g.rearrange("a b -> (a b)")
                LIN_OFF = {0: OFF_WQ_B, QKVO_R: OFF_WK_B, 3 * QKVO_R: OFF_WO_B}

                def lin_fill(base, fc, slot, q):
                    wres = R[f"wb{slot}"]
                    rank, b = fc // 2, fc % 2
                    off = rank * DENSE_SEG + LIN_OFF[base] + b * BLK
                    em.dma(q, wbuf[:, slot, :, :],
                           dg_flat[off:off + BLK].rearrange(
                               "(p a c) -> p a c", p=128, a=NCH),
                           reads=[R["dense_done"]], writes=[wres],
                           lane=f"lwb{slot}", group=("lin", base, fc))
                    return wres

                def lin_comp(fc, slot, out_fn):
                    wres = R[f"wb{slot}"]
                    for th in range(2):
                        sl = slice(512 * th, 512 * (th + 1))
                        bk = 5 + th % 2
                        a = Acc(em, bank[bk], NCH)
                        for ec in range(NCH):
                            a.add(pb(bk), wbuf[:, slot, ec, :], xn_sb[:, ec, sl],
                                  [wres, xnr[ec]])
                        out_fn(bk, th, sl)

                lin_fill(QKVO_R, 0, 0, "qs")
                for fc in range(NCH):
                    kres = R["kst"]
                    def kevac(bk, th, sl, fc=fc, kres=kres):
                        em.op("act", lambda e: e.activation(
                            kst[:, sl], pb(bk), AF.Identity,
                            bias=bk1_sb[:, fc:fc + 1]),
                            reads=[R["bk1"], bank[bk]], writes=[kres])
                    if fc + 1 < NCH:
                        lin_fill(QKVO_R, fc + 1, (fc + 1) % 2, "qs")
                    lin_comp(fc, fc % 2, kevac)
                    for th in range(2):
                        sl = slice(512 * th, 512 * (th + 1))
                        a = Acc(em, bank[4], 1)
                        a.add(pb(4), r1t[:, :], kst[:, sl], [R["r1t"], kres])
                        em.op("dve", lambda e, sl=sl: e.tensor_tensor(
                            pb(3), pb(4), cf1_ap[:, 3, sl], ALU.mult),
                            reads=[bank[4], R["sf1k"]], writes=[bank[3]])
                        em.op("dve", lambda e, sl=sl: e.tensor_tensor(
                            sg_sb[:, :], kst[:, sl], cf1_ap[:, 2, sl], ALU.mult),
                            reads=[kres, R["cf1k"]], writes=[R["sg0"]])
                        em.op("dve", lambda e, sl=sl: e.tensor_tensor(
                            kst[:, sl], sg_sb[:, :], pb(3), ALU.add),
                            reads=[R["sg0"], bank[3]], writes=[kres])
                    em.dma("qa", kvk_c[1][fc * 128:(fc + 1) * 128, :], kst[:, :],
                           reads=[kres], writes=[R["kvk_c1"]], lane=f"lkvc{next(_kvrr)}")
                em.collective("AllGather", ALU.bypass, [[0, 1], [2, 3], [4, 5], [6, 7]],
                              [kvk_c[1].opt()], [kvk_g[1].opt()],
                              reads=[R["kvk_c1"]], writes=[R["kvk_g1"]])
                for fv in range(4):
                    a = Acc(em, bank[2], 1)
                    a.add(pb(2), onesrb[0:1, :], bv1row_ap[0:1, fv * 512:(fv + 1) * 512],
                          [R["onesrb"], R["bv1"]])
                    em.op("dve", lambda e: e.tensor_copy(vbias[:, :], pb(2)),
                          reads=[bank[2]], writes=[R["vb0"]])
                    def v_wload(fv_, ec_):
                        s_ = ec_ % 2
                        vw_ = R[f"vw2{s_}"]
                        for rr in range(2):
                            rank = 2 * fv_ + rr
                            off = rank * DENSE_SEG + OFF_WV_B + ec_ * 32768
                            em.dma("qs", vw2_ap[:, s_, rr * 256:(rr + 1) * 256],
                                   dg_flat[off:off + 32768].rearrange(
                                       "(p f2) -> p f2", p=128),
                                   reads=[R["dense_done"]],
                                   writes=[vw_], lane=f"lvw2{s_}",
                                   group=("vw", fv_, ec_))
                        return vw_
                    for half in range(2):
                        v_wload(fv, 0)
                        for ec in range(NCH):
                            vw = R[f"vw2{ec % 2}"]
                            if ec + 1 < NCH:
                                v_wload(fv, ec + 1)
                            for tg in range(4):
                                ts_ = half * 4 + tg
                                bk = 3 + tg
                                em._deps("pe", [vw, xnr[ec]],
                                         [bank[bk]] if ec == 0 else [])
                                instr = em.eng["pe"].matmul(
                                    pb(bk), xn_sb[:, ec, ts_ * 128:(ts_ + 1) * 128],
                                    vw2_ap[:, ec % 2, :], start=(ec == 0),
                                    stop=(ec == NCH - 1))
                                instr.then_inc(em.sem["pe"], 1)
                                em.cnt["pe"] += 1
                                t = em.cnt["pe"]
                                vw.r.append(("pe", t))
                                xnr[ec].r.append(("pe", t))
                                if ec == NCH - 1:
                                    bank[bk].w = ("pe", t)
                                    bank[bk].r = []
                        for tg in range(4):
                            ts_ = half * 4 + tg
                            bk = 3 + tg
                            vres = R["vst"]
                            em.op("dve", lambda e, bk=bk: e.tensor_tensor(
                                vst[:, :], pb(bk), vbias[:, :], ALU.add),
                                reads=[bank[bk], R["vb0"]], writes=[vres])
                            for hl in range(4):
                                hd = fv * 4 + hl
                                em.dma("qa",
                                       kvv_c[1][hd * NT + ts_ * 128:
                                                hd * NT + (ts_ + 1) * 128, :],
                                       vst[:, hl * 128:(hl + 1) * 128],
                                       reads=[vres], writes=[R["kvv_c1"]], lane=f"lkvc{next(_kvrr)}")
                em.collective("AllGather", ALU.bypass, [[0, 1], [2, 3], [4, 5], [6, 7]],
                              [kvv_c[1].opt()], [kvv_g[1].opt()],
                              reads=[R["kvv_c1"]], writes=[R["kvv_g1"]])
                lin_fill(0, 0, 0, "qs")
                for fc in range(NCH):
                    def qevac(bk, th, sl, fc=fc):
                        em.op("act", lambda e: e.activation(
                            q_sb[:, fc, sl], pb(bk), AF.Identity,
                            bias=bq1_sb[:, fc:fc + 1]),
                            reads=[R["bq1"], bank[bk]], writes=[qr[fc]])
                    if fc + 1 < NCH:
                        lin_fill(0, fc + 1, (fc + 1) % 2, "qs")
                    lin_comp(fc, fc % 2, qevac)
                    for th in range(2):
                        sl = slice(512 * th, 512 * (th + 1))
                        a = Acc(em, bank[4], 1)
                        a.add(pb(4), r1t[:, :], q_sb[:, fc, sl], [R["r1t"], qr[fc]])
                        em.op("dve", lambda e, sl=sl: e.tensor_tensor(
                            pb(3), pb(4), cf1_ap[:, 1, sl], ALU.mult),
                            reads=[bank[4], R["sf1q"]], writes=[bank[3]])
                        em.op("dve", lambda e, fc=fc, sl=sl: e.tensor_tensor(
                            sg_sb[:, :], q_sb[:, fc, sl], cf1_ap[:, 0, sl], ALU.mult),
                            reads=[qr[fc], R["cf1q"]], writes=[R["sg0"]])
                        em.op("dve", lambda e, fc=fc, sl=sl: e.tensor_tensor(
                            q_sb[:, fc, sl], sg_sb[:, :], pb(3), ALU.add),
                            reads=[R["sg0"], bank[3]], writes=[qr[fc]])
            if stop == 4:
                dump("q_d", [128, NCH, NT], BF16, q_sb[:, :, :], list(qr))
                dump("xn_d", [128, NCH, NT], BF16, xn_sb[:, :, :], list(xnr))

            if stop > 4:
                em.barrier()
                o = 0
                kh_ap, o = carve(o, 2, 2 * NT)
                vh_ap, o = carve(o, 2, 16, 128)
                p_ap, o = carve(o, 2, 2 * NT)
                pt_ap, o = carve(o, 16, 128)
                osb_ap, o = carve(o, NCH, 512)
                masks_ap, o = carve(o, NTB * 2, 128)
                em.dma("qs", masks_ap, masks_in.rearrange("j r p k -> p (j r) k"),
                       writes=[R["masks"]], lane="lmask")
                attention(1)
            if stop == 5:
                dump("x_d", [128, NCH, NT], F32, x_sb[:, :, :], list(xr))

            if stop > 5:
                em.barrier()
                o = 0
                h_ap, o = carve(o, NCH, NT)
                wx_ap, o = carve(o, 4, NCH, 128)
                ffn(1, h_ap, wx_ap)
                em.dma("qg", y_out.rearrange("(c p) t -> p c t", p=128), x_sb[:, :, :],
                       reads=list(xr), writes=[R["y"]])

            for s, v in sorted(em.cnt.items()):
                if v and s != "pool":
                    em.eng["qg"].wait_ge(em.sem[s], v)
    return nc, dbg_outs


def _core_inputs(inputs, c):
    b, h = divmod(c, 2)
    f32 = np.float32
    x4 = np.asarray(inputs["L0_attn_x"], f32)[b]
    xs = x4.reshape(16, 128, E)[h::2].reshape(NT, E)

    gix = np.asarray(inputs["L0_ffn_gate_idx"], np.int32)
    uix = np.asarray(inputs["L0_ffn_up_idx"], np.int32)
    dix = np.asarray(inputs["L0_ffn_down_idx"], np.int32)

    def gslice(a):
        out = np.zeros((G_ROWS, E), np.int32)
        lo = c * G_ROWS
        hi = min(INTER0, lo + G_ROWS)
        if hi > lo:
            out[0:hi - lo] = a[lo:hi]
        return out

    cf0, sf0 = _rope0_fields(h)
    cf1q, sf1q = _rope1_fields(h, SCALE)
    cf1k, sf1k = _rope1_fields(h, 1.0)
    lut = np.asarray(inputs["L0_ffn_lut"], f32).reshape(1, LUTN)
    lutvals = np.concatenate([lut, np.arange(LUTN, dtype=f32).reshape(1, LUTN)], 1)
    b_d = np.asarray(inputs["L0_attn_b_d"], f32)

    ck = lambda a: np.ascontiguousarray(a)
    return {
        "x_shard": ck(xs.T),
        "W_c": np.asarray(inputs["L0_attn_W_c"], f32),
        "W_d": np.asarray(inputs["L0_attn_W_d"], f32),
        "W_o0": np.asarray(inputs["L0_attn_W_o"], f32),
        "wq_s": ck(np.asarray(inputs["L1_attn_Wq"], f32)[c * QKVO_R:(c + 1) * QKVO_R]),
        "wk_s": ck(np.asarray(inputs["L1_attn_Wk"], f32)[c * QKVO_R:(c + 1) * QKVO_R]),
        "wv_s": ck(np.asarray(inputs["L1_attn_Wv"], f32)[c * QKVO_R:(c + 1) * QKVO_R]),
        "wo_s": ck(np.asarray(inputs["L1_attn_Wo"], f32)[c * QKVO_R:(c + 1) * QKVO_R]),
        "wo0t_s": ck(np.asarray(inputs["L0_attn_W_o"], f32).T[c * QKVO_R:(c + 1) * QKVO_R]),
        "wgu_s": ck(np.asarray(inputs["L1_ffn_w_gate_up"], f32)[c * WGU_R:(c + 1) * WGU_R]),
        "wdn_s": ck(np.asarray(inputs["L1_ffn_w_down"], f32)[c * WDN_R:(c + 1) * WDN_R]),
        "gix_s": gslice(gix),
        "uix_s": gslice(uix),
        "dix_s": ck(dix[c * D_ROWS:(c + 1) * D_ROWS]),
        "lutvals": lutvals,
        "ident": np.eye(128, dtype=BF),
        "r0t": _rot0T(), "r1t": _rot1T(),
        "cf0": cf0, "sf0": sf0,
        "cf1q": cf1q, "sf1q": sf1q, "cf1k": cf1k, "sf1k": sf1k,
        "masks": _masks(h),
        "bc_col": ck(np.asarray(inputs["L0_attn_b_c"], f32).reshape(4, 128).T),
        "bdqk_col": ck(b_d[:4096].reshape(32, 128).T),
        "bdv_row": ck(b_d[4096:].reshape(1, E)),
        "bo0_col": ck(np.asarray(inputs["L0_attn_b_o"], f32).reshape(NCH, 128).T),
        "bq1_col": ck(np.asarray(inputs["L1_attn_bq"], f32).reshape(NCH, 128).T),
        "bk1_col": ck(np.asarray(inputs["L1_attn_bk"], f32).reshape(NCH, 128).T),
        "bv1_row": ck(np.asarray(inputs["L1_attn_bv"], f32).reshape(1, E)),
        "bo1_col": ck(np.asarray(inputs["L1_attn_bo"], f32).reshape(NCH, 128).T),
    }


def kernel(**inputs):
    dbg = os.environ.get("MK_DBG", "")
    nc, dbg_names = build_program(dbg)
    in_maps = [_core_inputs(inputs, c) for c in range(8)]
    res = run_bass_kernel_spmd(nc, in_maps, core_ids=list(range(8)))
    if dbg:
        return res
    out = np.empty((B, S, E), np.float32)
    for c in range(8):
        b, h = divmod(c, 2)
        y = res.results[c]["y"]
        out[b].reshape(16, 128, E)[h::2] = y.T.reshape(NTB, 128, E)
    return out


# version 12
# speedup vs baseline: 1.1898x; 1.0131x over previous
import itertools
import os
import contextlib
from collections import defaultdict

import numpy as np
import ml_dtypes
import concourse.bass as bass
import concourse.mybir as mybir
from concourse.bass_utils import run_bass_kernel_spmd

F32 = mybir.dt.float32
BF16 = mybir.dt.bfloat16
I32 = mybir.dt.int32
AF = mybir.ActivationFunctionType
ALU = mybir.AluOpType
AX = mybir.AxisListType

B, S, E = 4, 2048, 2048
NCH = E // 128
NT = 1024
NTB = NT // 128
D_C, D_ROT = 512, 64
HEADS, HD = 16, 128
INTER0, INTER1 = 5504, 8192
LUTN = 16
SCALE = float(1.0 / np.sqrt(128.0))
EPS = 1e-5
NEG = -1e30
INV_E = 1.0 / 2048.0

G_ROWS = 768
D_ROWS = 256
QKVO_R, WGU_R, WDN_R = 256, 2048, 256
BF = ml_dtypes.bfloat16

_kvrr = itertools.cycle([0, 1, 2, 3])
_castrr = itertools.cycle([0, 1])

BLK = 128 * 16 * 128
OFF_WQ_B = 0
OFF_WK_B = 2 * BLK
OFF_WV_B = 4 * BLK
OFF_WO_B = OFF_WV_B + 16 * 128 * 256
OFF_WGU_B = OFF_WO_B + 2 * BLK
OFF_WDN_B = OFF_WGU_B + 16 * BLK
OFF_WO0_B = OFF_WDN_B + 8 * BLK
DENSE_SEG = OFF_WO0_B + 2 * BLK
L0_GRPS = [list(range(0, 16)), list(range(16, 32)), list(range(32, 43))]
DOWN_GRP_OFF = [0, 262144, 524288]
DOWN_SEG = 2 * 704512

PHASES = ["p1", "p2", "p3", "p4", "p5", "p6", "out"]


class Res:
    __slots__ = ("w", "r")
    def __init__(self):
        self.w = None
        self.r = []


class Em:
    NAMES = ("pe", "dve", "act", "pool", "qs", "qa", "qg")

    def __init__(self, nc, stack):
        self.nc = nc
        self.eng = {"pe": nc.tensor, "dve": nc.vector, "act": nc.scalar,
                    "pool": nc.gpsimd, "qs": nc.sync, "qa": nc.scalar,
                    "qg": nc.gpsimd}
        self.canon = {"pe": "pe", "dve": "dve", "act": "act", "pool": "pool",
                      "qs": "qs", "qa": "act", "qg": "pool"}
        self.sem = {n: stack.enter_context(nc.semaphore(f"sem_{n}")) for n in self.NAMES}
        self.dma_lanes = {"qs", "qa", "qg"}
        self.last_group = {"qs": None, "qa": None, "qg": None}
        self.auto_group = 0
        self.cc_sems = []
        self.stack = stack
        self.cnt = defaultdict(int)
        self.known = defaultdict(int)

    def lane(self, name):
        if name not in self.sem:
            self.sem[name] = self.stack.enter_context(self.nc.semaphore(f"lane_{name}"))
            self.dma_lanes.add(name)
            self.last_group[name] = None
        return name

    def _wait_one(self, waiter, sname, val):
        w = self.canon[waiter]
        if w == "pe" and sname == "pe":
            return
        if isinstance(sname, tuple):
            if self.known[(w, sname)] >= 1:
                return
            self.eng[waiter].wait_ge(self.cc_sems[sname[1]], 1)
            self.known[(w, sname)] = 1
        else:
            if self.known[(w, sname)] >= val:
                return
            self.eng[waiter].wait_ge(self.sem[sname], val)
            self.known[(w, sname)] = val

    def _deps(self, waiter, reads, writes):
        need = {}
        def add(s, t):
            need[s] = max(need.get(s, 0), t)
        for r in reads:
            if r.w is not None:
                s, t = r.w
                if isinstance(s, tuple):
                    self._wait_one(waiter, s, 1)
                else:
                    add(s, t)
        for wr in writes:
            if wr.w is not None:
                s, t = wr.w
                if isinstance(s, tuple):
                    self._wait_one(waiter, s, 1)
                else:
                    add(s, t)
            for (s, t) in wr.r:
                add(s, t)
        for s in sorted(need):
            self._wait_one(waiter, s, need[s])

    def op(self, eng, fn, reads=(), writes=()):
        self._deps(eng, reads, writes)
        instr = fn(self.eng[eng])
        instr.then_inc(self.sem[eng], 1)
        self.cnt[eng] += 1
        t = self.cnt[eng]
        for r in reads:
            r.r.append((eng, t))
        for w in writes:
            w.w = (eng, t)
            w.r = []
        return t

    def dma(self, q, out, in_, reads=(), writes=(), transpose=False, lane=None,
            group=None, **kw):
        ln_ = self.lane(lane) if lane else q
        if group is None:
            self.auto_group += 1
            group = ("__auto", self.auto_group)
        if self.last_group.get(ln_) != group:
            if self.cnt[ln_] > 0:
                self._wait_one(q, ln_, self.cnt[ln_])
            self.last_group[ln_] = group
        self._deps(q, reads, writes)
        e = self.eng[q]
        if transpose:
            instr = e.dma_start_transpose(out, in_)
        else:
            instr = e.dma_start(out=out, in_=in_, **kw)
        instr.then_inc(self.sem[ln_], 16)
        self.cnt[ln_] += 16
        t = self.cnt[ln_]
        for r in reads:
            r.r.append((ln_, t))
        for w in writes:
            w.w = (ln_, t)
            w.r = []
        return t

    def collective(self, kind, op, groups, ins, outs, reads=(), writes=()):
        self._deps("qg", reads, writes)
        idx = len(self.cc_sems)
        sem = self.stack.enter_context(self.nc.semaphore(f"cc{idx}"))
        self.cc_sems.append(sem)
        self.eng["qg"].collective_compute(
            kind, op, replica_groups=groups, ins=ins, outs=outs).then_inc(sem)
        for w in writes:
            w.w = (("__cc", idx), 1)
            w.r = []

    def barrier(self):
        snap = dict(self.cnt)
        for waiter in self.NAMES:
            for s, v in snap.items():
                if v:
                    self._wait_one(waiter, s, v)


class Acc:
    def __init__(self, em, bank_res, n):
        self.em = em
        self.bank = bank_res
        self.n = n
        self.i = 0
        self.touched = []

    def add(self, out_ap, lhsT_ap, rhs_ap, reads):
        em = self.em
        if self.i == 0:
            em._deps("pe", [], [self.bank])
        em._deps("pe", reads, [])
        last = (self.i == self.n - 1)
        instr = em.eng["pe"].matmul(out_ap, lhsT_ap, rhs_ap,
                                    start=(self.i == 0), stop=last)
        instr.then_inc(em.sem["pe"], 1)
        em.cnt["pe"] += 1
        t = em.cnt["pe"]
        for r in reads:
            r.r.append(("pe", t))
        self.i += 1
        if last:
            mode = os.environ.get("MK_NONOP", "1")
            if mode == "2":
                em.eng["pe"].nop(nofuse=True)
            elif mode != "1":
                em.eng["pe"].nop(nofuse=True).then_inc(em.sem["pe"], 1)
                em.cnt["pe"] += 1
            self.bank.w = ("pe", em.cnt["pe"])
            self.bank.r = []


def _positions(h):
    return np.concatenate([np.arange((2 * j + h) * 128, (2 * j + h + 1) * 128)
                           for j in range(NTB)])


def _rope0_fields(h):
    pos = _positions(h).astype(np.float64)
    inv = 1.0 / 10000.0 ** (np.arange(0, D_ROT, 2) / D_ROT)
    ang = pos[None, :] * inv[:, None]
    cf = np.ones((128, NT)); sf = np.zeros((128, NT))
    cf[0:32] = np.cos(ang); cf[32:64] = np.cos(ang)
    sf[0:32] = np.sin(ang); sf[32:64] = np.sin(ang)
    return cf.astype(BF), sf.astype(BF)


def _rope1_fields(h, scale):
    pos = _positions(h).astype(np.float64)
    inv = 10000.0 ** (-np.arange(0, HD, 2) / HD)
    ang = pos[None, :] * inv[:, None]
    cf = np.empty((128, NT)); sf = np.empty((128, NT))
    cf[0::2] = np.cos(ang); cf[1::2] = np.cos(ang)
    sf[0::2] = np.sin(ang); sf[1::2] = np.sin(ang)
    return (cf * scale).astype(BF), (sf * scale).astype(BF)


def _rot0T():
    r = np.zeros((128, 128), np.float32)
    for i in range(32):
        r[32 + i, i] = -1.0
        r[i, 32 + i] = 1.0
    return r.astype(BF)


def _rot1T():
    r = np.zeros((128, 128), np.float32)
    for i in range(64):
        r[2 * i + 1, 2 * i] = -1.0
        r[2 * i, 2 * i + 1] = 1.0
    return r.astype(BF)


def _masks(h):
    m = np.zeros((NTB, 2, 128, 128), np.float32)
    qi = np.arange(128); ki = np.arange(128)
    for j in range(NTB):
        for r in range(2):
            bad = ((2 * j + r) * 128 + ki[None, :]) > ((2 * j + h) * 128 + qi[:, None])
            m[j, r][bad] = NEG
    return m.astype(BF)


def build_program(dbg_stop=""):
    nc = bass.Bass("TRN2", target_bir_lowering=False)
    stack = contextlib.ExitStack()
    d = {}

    def din(name, shape, dtype):
        d[name] = nc.dram_tensor(name, shape, dtype, kind="ExternalInput").ap()
        return d[name]

    def dint(name, shape, dtype):
        d[name] = nc.dram_tensor(name, shape, dtype).ap()
        return d[name]

    def dshare(name, shape, dtype):
        d[name] = nc.dram_tensor(name, shape, dtype, addr_space="Shared").ap()
        return d[name]

    x_in = din("x_shard", [E, NT], F32)
    wc_in = din("W_c", [E, D_C], F32)
    wd_in = din("W_d", [D_C, 3 * E], F32)
    wo0_in = din("W_o0", [E, E], F32)
    wq_s = din("wq_s", [QKVO_R, E], F32)
    wk_s = din("wk_s", [QKVO_R, E], F32)
    wv_s = din("wv_s", [QKVO_R, E], F32)
    wo_s = din("wo_s", [QKVO_R, E], F32)
    wo0t_s = din("wo0t_s", [QKVO_R, E], F32)
    wgu_s = din("wgu_s", [WGU_R, E], F32)
    wdn_s = din("wdn_s", [WDN_R, INTER1], F32)
    gix_s = din("gix_s", [G_ROWS, E], I32)
    uix_s = din("uix_s", [G_ROWS, E], I32)
    dix_s = din("dix_s", [D_ROWS, INTER0], I32)
    lutvals = din("lutvals", [1, 2 * LUTN], F32)
    ident_in = din("ident", [128, 128], BF16)
    r0t_in = din("r0t", [128, 128], BF16)
    r1t_in = din("r1t", [128, 128], BF16)
    cf0_in = din("cf0", [128, NT], BF16)
    sf0_in = din("sf0", [128, NT], BF16)
    cf1q_in = din("cf1q", [128, NT], BF16)
    sf1q_in = din("sf1q", [128, NT], BF16)
    cf1k_in = din("cf1k", [128, NT], BF16)
    sf1k_in = din("sf1k", [128, NT], BF16)
    masks_in = din("masks", [NTB, 2, 128, 128], BF16)
    bc_in = din("bc_col", [128, 4], F32)
    bdqk_in = din("bdqk_col", [128, 32], F32)
    bdv_in = din("bdv_row", [1, E], F32)
    bo0_in = din("bo0_col", [128, NCH], F32)
    bq1_in = din("bq1_col", [128, NCH], F32)
    bk1_in = din("bk1_col", [128, NCH], F32)
    bv1_in = din("bv1_row", [1, E], F32)
    bo1_in = din("bo1_col", [128, NCH], F32)

    y_out = nc.dram_tensor("y", [E, NT], F32, kind="ExternalOutput").ap()

    gate_c = dint("gate_c", [G_ROWS, E], BF16)
    up_c = dint("up_c", [G_ROWS, E], BF16)
    down_c = dint("down_c", [D_ROWS, INTER0], BF16)
    gate_g = dshare("gate_g", [8 * G_ROWS, E], BF16)
    up_g = dshare("up_g", [8 * G_ROWS, E], BF16)
    down_g = dshare("down_g", [8 * D_ROWS, INTER0], BF16)
    dense_c = dint("dense_c", [4352, E], BF16)
    dense_g = dshare("dense_g", [8 * 4352, E], BF16)
    wo0_c = dint("wo0_c", [256, E], BF16)
    wo0_g = dshare("wo0_g", [8 * 256, E], BF16)
    kvk_c = [dint(f"kvk_c{l}", [HEADS * 128, NT], BF16) for l in range(2)]
    kvv_c = [dint(f"kvv_c{l}", [HEADS * NT, 128], BF16) for l in range(2)]
    kvk_g = [dint(f"kvk_g{l}", [2 * HEADS * 128, NT], BF16) for l in range(2)]
    kvv_g = [dint(f"kvv_g{l}", [2 * HEADS * NT, 128], BF16) for l in range(2)]

    sbufs = {}
    def sbuf(name, shape, dtype):
        t = stack.enter_context(nc.sbuf_tensor(name, shape, dtype))
        sbufs[name] = t
        return t

    x_sb = sbuf("x_sb", [128, NCH, NT], F32)
    xn_sb = sbuf("xn_sb", [128, NCH, NT], BF16)
    q_sb = sbuf("q_sb", [128, NCH, NT], BF16)
    ident = sbuf("ident_sb", [128, 128], BF16)
    r0t = sbuf("r0t_sb", [128, 128], BF16)
    r1t = sbuf("r1t_sb", [128, 128], BF16)
    kst = sbuf("kst", [128, NT], BF16)
    vst = sbuf("vst", [128, 512], BF16)
    sg_sb = sbuf("sg_sb", [128, 512], BF16)
    wbuf = sbuf("wbuf", [128, 2, NCH, 128], BF16)
    xsq = sbuf("xsq", [128, 2, 512], F32)
    lnt = sbuf("lnt", [128, 512], F32)
    vbias = sbuf("vbias", [128, 512], F32)
    stat = sbuf("stat", [1, 3, 512], F32)
    mx = sbuf("mx", [128, 8], F32)
    lsum = sbuf("lsum", [128, 8], F32)
    bc_sb = sbuf("bc_sb", [128, 4], F32)
    bdqk_sb = sbuf("bdqk_sb", [128, 32], F32)
    bdqs_sb = sbuf("bdqs_sb", [128, 16], F32)
    bo0_sb = sbuf("bo0_sb", [128, NCH], F32)
    bq1_sb = sbuf("bq1_sb", [128, NCH], F32)
    bk1_sb = sbuf("bk1_sb", [128, NCH], F32)
    bo1_sb = sbuf("bo1_sb", [128, NCH], F32)
    lutrow = sbuf("lutrow", [1, 2 * LUTN], F32)
    onesr = sbuf("onesr", [1, 128], F32)
    onesrb = sbuf("onesrb", [1, 128], BF16)
    onescol = sbuf("onescol", [128, 1], F32)
    epscol = sbuf("epscol", [128, 1], F32)
    lutb = sbuf("lutb", [128, LUTN], F32)
    dec_i = sbuf("dec_i", [128, 512], I32)
    AR = 24576
    abf = sbuf("abf", [128, AR], BF16)

    ps32 = stack.enter_context(nc.psum_tensor("ps32", [128, 3584], F32))
    psbf = stack.enter_context(nc.psum_tensor("psbf", [128, 1024], BF16))

    def pb(b):
        return ps32[:, 512 * b:512 * (b + 1)]

    def carve(off, *dims):
        n = int(np.prod(dims))
        ap = abf[:, off:off + n]
        if len(dims) > 1:
            pat = " ".join(f"d{i}" for i in range(len(dims)))
            ap = ap.rearrange(f"p ({pat}) -> p {pat}",
                              **{f"d{i}": dims[i] for i in range(len(dims))})
        return ap, off + n

    dbg_outs = []

    with stack:
        with nc.Block():
            em = Em(nc, stack)
            R = defaultdict(Res)
            bank = [R[f"bank{i}"] for i in range(7)]
            slotbf = [R[f"psbf{i}"] for i in range(8)]
            xr = [R[f"x{c}"] for c in range(NCH)]
            xnr = [R[f"xn{c}"] for c in range(NCH)]
            qr = [R[f"q{c}"] for c in range(NCH)]
            hr = [R[f"h{c}"] for c in range(NCH)]
            osr = [R[f"os{c}"] for c in range(NCH)]
            ptr = [R[f"pt{k}"] for k in range(16)]

            stop = PHASES.index(dbg_stop) if dbg_stop in PHASES else 99

            def dump(nm, shape, dt, ap, rds):
                o = nc.dram_tensor(nm, shape, dt, kind="ExternalOutput").ap()
                idxall = tuple(slice(None) for _ in shape)
                em.dma("qg", o[idxall], ap, reads=rds, writes=[R["dump" + nm]])
                dbg_outs.append(nm)

            em.dma("qs", x_sb[:, :, :], x_in.rearrange("(c p) t -> p c t", p=128),
                   writes=xr, lane="lx")
            for nm, dst, src in [("ident", ident, ident_in), ("r0t", r0t, r0t_in),
                                 ("r1t", r1t, r1t_in), ("bc", bc_sb, bc_in),
                                 ("bdqk", bdqk_sb, bdqk_in), ("bo0", bo0_sb, bo0_in),
                                 ("bq1", bq1_sb, bq1_in), ("bk1", bk1_sb, bk1_in),
                                 ("bo1", bo1_sb, bo1_in), ("lutrow", lutrow, lutvals)]:
                em.dma("qs", dst[:, :], src[:, :], writes=[R[nm]], lane="l" + nm)
            em.op("pool", lambda e: e.memset(onesr[:, :], 1.0), writes=[R["onesr"]])
            em.op("pool", lambda e: e.memset(onesrb[:, :], 1.0), writes=[R["onesrb"]])
            em.op("pool", lambda e: e.memset(onescol[:, :], 1.0), writes=[R["onescol"]])
            em.op("pool", lambda e: e.memset(epscol[:, :], EPS), writes=[R["eps"]])
            em.op("dve", lambda e: e.tensor_scalar(bdqs_sb[:, :], bdqk_sb[:, 0:16],
                  SCALE, None, ALU.mult), reads=[R["bdqk"]], writes=[R["bdqs"]])
            ga = Acc(em, bank[6], 1)
            ga.add(ps32[:, 6 * 512:6 * 512 + LUTN], onesr[0:1, :], lutrow[0:1, 0:LUTN],
                   [R["onesr"], R["lutrow"]])
            em.op("dve", lambda e: e.tensor_copy(lutb[:, :],
                  ps32[:, 6 * 512:6 * 512 + LUTN]), reads=[bank[6]],
                  writes=[R["lutb"]])

            dcw = R["dense_c"]
            dc_flat = dense_c.rearrange("a b -> (a b)")
            o0 = 0
            stgi_ap, o0 = carve(o0, 4, E)
            stgo_ap, o0 = carve(o0, 4, NCH, 128)
            prep_bi = [0]

            def prep_block(src_slice, dst_off, dstf=None, wres=None):
                s = prep_bi[0] % 4
                prep_bi[0] += 1
                sir, sor = R[f"stgi{s}"], R[f"stgo{s}"]
                em.dma("qg", stgi_ap[:, s, :], src_slice, writes=[sir],
                       lane=f"lcin{s}")
                for ec in range(NCH):
                    sl_ = ec % 8
                    em.op("pe", lambda e, ec=ec, sl_=sl_, s=s: e.transpose(
                        psbf[:, sl_ * 128:(sl_ + 1) * 128],
                        stgi_ap.rearrange("p t (a b) -> p t a b", a=NCH)[:, s, ec, :],
                        ident[:, :]),
                        reads=[sir, R["ident"]], writes=[slotbf[sl_]])
                    if sl_ == 7:
                        em.op("dve", lambda e, ec=ec, s=s: e.tensor_copy(
                            stgo_ap[:, s, ec - 7:ec + 1, :],
                            psbf.ap().rearrange("p (kb q) -> p kb q", kb=8)),
                            reads=[slotbf[i] for i in range(8)],
                            writes=[sor])
                em.dma("qs", (dc_flat if dstf is None else dstf)
                       [dst_off:dst_off + BLK]
                       .rearrange("(p a b) -> p a b", p=128, a=NCH),
                       stgo_ap[:, s, :, :], reads=[sor],
                       writes=[dcw if wres is None else wres],
                       lane=f"lcout{s}")

            wo0c_flat = wo0_c.rearrange("a b -> (a b)")
            for b in range(2):
                prep_block(wq_s[b * 128:(b + 1) * 128, :], OFF_WQ_B + b * BLK)
                prep_block(wk_s[b * 128:(b + 1) * 128, :], OFF_WK_B + b * BLK)
                prep_block(wo_s[b * 128:(b + 1) * 128, :], OFF_WO_B + b * BLK)
                prep_block(wo0t_s[b * 128:(b + 1) * 128, :], b * BLK,
                           dstf=wo0c_flat, wres=R["wo0_c"])
            for j in range(16):
                prep_block(wgu_s[j * 128:(j + 1) * 128, :], OFF_WGU_B + j * BLK)
            for gi in range(4):
                for b in range(2):
                    prep_block(wdn_s[b * 128:(b + 1) * 128, gi * E:(gi + 1) * E],
                               OFF_WDN_B + (gi * 2 + b) * BLK)
            vv = dc_flat[OFF_WV_B:OFF_WV_B + 16 * 128 * 256]\
                .rearrange("(ec p f2) -> p ec f2", ec=16, p=128)
            for fr in range(2):
                s = prep_bi[0] % 4
                prep_bi[0] += 1
                sir, sor = R[f"stgi{s}"], R[f"stgo{s}"]
                em.dma("qg", stgi_ap[:, s, :], wv_s[fr * 128:(fr + 1) * 128, :],
                       writes=[sir], lane=f"lcin{s}")
                for ec in range(NCH):
                    sl_ = ec % 8
                    em.op("pe", lambda e, ec=ec, sl_=sl_, s=s: e.transpose(
                        psbf[:, sl_ * 128:(sl_ + 1) * 128],
                        stgi_ap.rearrange("p t (a b) -> p t a b", a=NCH)[:, s, ec, :],
                        ident[:, :]),
                        reads=[sir, R["ident"]], writes=[slotbf[sl_]])
                    if sl_ == 7:
                        em.op("dve", lambda e, ec=ec, s=s: e.tensor_copy(
                            stgo_ap[:, s, ec - 7:ec + 1, :],
                            psbf.ap().rearrange("p (kb q) -> p kb q", kb=8)),
                            reads=[slotbf[i] for i in range(8)],
                            writes=[sor])
                em.dma("qs", vv[:, :, fr * 128:(fr + 1) * 128], stgo_ap[:, s, :, :],
                       reads=[sor], writes=[dcw], lane=f"lcout{s}")
            em.barrier()

            def ln():
                for th in range(2):
                    sl = slice(512 * th, 512 * (th + 1))
                    a1 = Acc(em, bank[4], NCH)
                    a2 = Acc(em, bank[5], NCH)
                    for ec in range(NCH):
                        em.op("act", lambda e, ec=ec: e.activation(
                            xsq[:, ec % 2, :], x_sb[:, ec, sl], AF.Square),
                            reads=[xr[ec]], writes=[R[f"xsq{ec % 2}"]])
                        a1.add(pb(4)[0:1, :], onescol[:, :], x_sb[:, ec, sl],
                               [R["onescol"], xr[ec]])
                        a2.add(pb(5)[0:1, :], onescol[:, :], xsq[:, ec % 2, :],
                               [R["onescol"], R[f"xsq{ec % 2}"]])
                    em.op("dve", lambda e: e.tensor_scalar(stat[:, 0, :], pb(4)[0:1, :],
                          INV_E, None, ALU.mult), reads=[bank[4]], writes=[R["st0"]])
                    em.op("dve", lambda e: e.tensor_scalar(stat[:, 1, :], pb(5)[0:1, :],
                          INV_E, None, ALU.mult), reads=[bank[5]], writes=[R["st1"]])
                    em.op("dve", lambda e: e.tensor_tensor(stat[:, 2, :], stat[:, 0, :],
                          stat[:, 0, :], ALU.mult), reads=[R["st0"]], writes=[R["st2"]])
                    em.op("dve", lambda e: e.tensor_tensor(stat[:, 1, :], stat[:, 1, :],
                          stat[:, 2, :], ALU.subtract), reads=[R["st1"], R["st2"]],
                          writes=[R["st1"]])
                    em.op("act", lambda e: e.activation(stat[:, 1, :], stat[:, 1, :],
                          AF.Ln, bias=epscol[0:1, :]), reads=[R["st1"], R["eps"]],
                          writes=[R["st1"]])
                    em.op("act", lambda e: e.activation(stat[:, 1, :], stat[:, 1, :],
                          AF.Exp, scale=-0.5), reads=[R["st1"]], writes=[R["st1"]])
                    em.op("dve", lambda e: e.scalar_tensor_tensor(stat[:, 2, :],
                          stat[:, 0, :], -1.0, stat[:, 1, :], ALU.mult, ALU.mult),
                          reads=[R["st0"], R["st1"]], writes=[R["st2"]])
                    aa = Acc(em, bank[4], 1)
                    aa.add(pb(4), onesr[0:1, :], stat[0:1, 1, :], [R["onesr"], R["st1"]])
                    ab = Acc(em, bank[5], 1)
                    ab.add(pb(5), onesr[0:1, :], stat[0:1, 2, :], [R["onesr"], R["st2"]])
                    for ec in range(NCH):
                        em.op("dve", lambda e, ec=ec: e.tensor_tensor(
                            lnt[:, :], x_sb[:, ec, sl], pb(4), ALU.mult),
                            reads=[xr[ec], bank[4]], writes=[R["lnt"]])
                        em.op("dve", lambda e, ec=ec: e.tensor_tensor(
                            xn_sb[:, ec, sl], lnt[:, :], pb(5), ALU.add),
                            reads=[R["lnt"], bank[5]], writes=[xnr[ec]])

            o = 0
            wc_ap, o = carve(o, NCH, D_C)
            wdq_ap, o = carve(o, 4, 1536)
            c_ap, o = carve(o, 4, NT)
            cf0s_ap, o = carve(o, 2, NT)
            bdvrow_ap, o = carve(o, E)
            assert o <= AR

            ln()
            if stop <= 0:
                dump("xn_d", [128, NCH, NT], BF16, xn_sb[:, :, :], list(xnr))

            if stop > 0:
                cr = [R[f"c{c}"] for c in range(4)]
                em.dma("qg", wc_ap, wc_in.rearrange("(c p) f -> p c f", p=128),
                       writes=[R["wc"]], lane="lwc")
                em.dma("qs", cf0s_ap[:, 0, :], cf0_in[:, :], writes=[R["cf0"]], lane="lcf")
                em.dma("qs", cf0s_ap[:, 1, :], sf0_in[:, :], writes=[R["sf0"]], lane="lsf")
                em.dma("qg", bdvrow_ap[0:1, :], bdv_in[:, :], writes=[R["bdv"]], lane="lbdv")
                for fc in range(4):
                    for th in range(2):
                        sl = slice(512 * th, 512 * (th + 1))
                        bk = 5 + (fc + th) % 2
                        a = Acc(em, bank[bk], NCH)
                        for ec in range(NCH):
                            a.add(pb(bk), wc_ap[:, ec, fc * 128:(fc + 1) * 128],
                                  xn_sb[:, ec, sl], [R["wc"], xnr[ec]])
                        em.op("act", lambda e, fc=fc, sl=sl, bk=bk: e.activation(
                            c_ap[:, fc, sl], pb(bk), AF.Identity, bias=bc_sb[:, fc:fc + 1]),
                            reads=[R["bc"], bank[bk]], writes=[cr[fc]])
                for th in range(2):
                    sl = slice(512 * th, 512 * (th + 1))
                    a = Acc(em, bank[4], 1)
                    a.add(pb(4), r0t[:, :], c_ap[:, 0, sl], [R["r0t"], cr[0]])
                    em.op("dve", lambda e, sl=sl: e.tensor_tensor(
                        pb(3), pb(4), cf0s_ap[:, 1, sl], ALU.mult),
                        reads=[bank[4], R["sf0"]], writes=[bank[3]])
                    em.op("dve", lambda e, sl=sl: e.tensor_tensor(
                        sg_sb[:, :], c_ap[:, 0, sl], cf0s_ap[:, 0, sl], ALU.mult),
                        reads=[cr[0], R["cf0"]], writes=[R["sg0"]])
                    em.op("dve", lambda e, sl=sl: e.tensor_tensor(
                        c_ap[:, 0, sl], sg_sb[:, :], pb(3), ALU.add),
                        reads=[R["sg0"], bank[3]], writes=[cr[0]])

                def wd_cols(qd):
                    em.dma("qg", wdq_ap,
                           wd_in.rearrange("(c p) f -> p c f", p=128)
                           [:, :, qd * 1536:(qd + 1) * 1536], writes=[R["wd"]], lane="lwd")

                def hcol(fcol, width, evac):
                    qd = fcol // 1536
                    loc = fcol % 1536
                    for th in range(2):
                        sl = slice(512 * th, 512 * (th + 1))
                        bk = 5 + th % 2
                        a = Acc(em, bank[bk], 4)
                        for cc in range(4):
                            a.add(pb(bk), wdq_ap[:, cc, loc:loc + 128],
                                  c_ap[:, cc, sl], [R["wd"], cr[cc]])
                        evac(bk, th, sl)

                wd_cols(0)
                for hd in range(12):
                    hcol(hd * 128, 128, lambda bk, th, sl, hd=hd: em.op(
                        "act", lambda e: e.activation(q_sb[:, hd, sl], pb(bk), AF.Identity,
                                                      bias=bdqs_sb[:, hd:hd + 1],
                                                      scale=SCALE),
                        reads=[R["bdqs"], bank[bk]], writes=[qr[hd]]))
                wd_cols(1)
                for hd in range(12, 16):
                    hcol(hd * 128, 128, lambda bk, th, sl, hd=hd: em.op(
                        "act", lambda e: e.activation(q_sb[:, hd, sl], pb(bk), AF.Identity,
                                                      bias=bdqs_sb[:, hd:hd + 1],
                                                      scale=SCALE),
                        reads=[R["bdqs"], bank[bk]], writes=[qr[hd]]))
                def k_evac(hd):
                    kres = R["kst"]
                    def f(bk, th, sl):
                        em.op("act", lambda e: e.activation(
                            kst[:, sl], pb(bk), AF.Identity,
                            bias=bdqk_sb[:, 16 + hd:17 + hd]),
                            reads=[R["bdqk"], bank[bk]], writes=[kres])
                        if th == 1:
                            em.dma("qg", kvk_c[0][hd * 128:(hd + 1) * 128, :],
                                   kst[:, :], reads=[kres], writes=[R["kvk_c0"]], lane=f"lkvc{next(_kvrr)}")
                    return f
                for hd in range(8):
                    hcol(2048 + hd * 128, 128, k_evac(hd))
                wd_cols(2)
                for hd in range(8, 16):
                    hcol(2048 + hd * 128, 128, k_evac(hd))
                def vpart(fv):
                    qd = (4096 + fv * 512) // 1536
                    loc = (4096 + fv * 512) % 1536
                    a = Acc(em, bank[6], 1)
                    a.add(pb(6), onesrb[0:1, :], bdvrow_ap[0:1, fv * 512:(fv + 1) * 512],
                          [R["onesrb"], R["bdv"]])
                    em.op("dve", lambda e: e.tensor_copy(vbias[:, :], pb(6)),
                          reads=[bank[6]], writes=[R["vb0"]])
                    for ts_ in range(8):
                        vres = R["vst"]
                        bk = 4 + ts_ % 2
                        a = Acc(em, bank[bk], 4)
                        for cc in range(4):
                            a.add(pb(bk), c_ap[:, cc, ts_ * 128:(ts_ + 1) * 128],
                                  wdq_ap[:, cc, loc:loc + 512], [cr[cc], R["wd"]])
                        em.op("dve", lambda e, bk=bk: e.tensor_tensor(
                            vst[:, :], pb(bk), vbias[:, :], ALU.add),
                            reads=[bank[bk], R["vb0"]], writes=[vres])
                        for hl in range(4):
                            hd = fv * 4 + hl
                            em.dma("qg",
                                   kvv_c[0][hd * NT + ts_ * 128: hd * NT + (ts_ + 1) * 128, :],
                                   vst[:, hl * 128:(hl + 1) * 128],
                                   reads=[vres], writes=[R["kvv_c0"]], lane=f"lkvc{next(_kvrr)}")
                vpart(0)
                wd_cols(3)
                for fv in range(1, 4):
                    vpart(fv)

                em.collective("AllGather", ALU.bypass, [[0, 1], [2, 3], [4, 5], [6, 7]],
                              [kvk_c[0].opt()], [kvk_g[0].opt()],
                              reads=[R["kvk_c0"]], writes=[R["kvk_g0"]])
                em.collective("AllGather", ALU.bypass, [[0, 1], [2, 3], [4, 5], [6, 7]],
                              [kvv_c[0].opt()], [kvv_g[0].opt()],
                              reads=[R["kvv_c0"]], writes=[R["kvv_g0"]])
                em.collective("AllGather", ALU.bypass, [list(range(8))],
                              [wo0_c.opt()], [wo0_g.opt()],
                              reads=[R["wo0_c"]], writes=[R["wo0_done"]])
                em.collective("AllGather", ALU.bypass, [list(range(8))],
                              [dense_c.opt()], [dense_g.opt()],
                              reads=[dcw], writes=[R["dense_done"]])
            if stop == 1:
                dump("q_d", [128, NCH, NT], BF16, q_sb[:, :, :], list(qr))
                dump("c_d", [128, 4, NT], BF16, c_ap, [R[f"c{c}"] for c in range(4)])

            em.barrier()
            o = 0
            decb_ap, o = carve(o, 512)
            dect_ap, o = carve(o, 512)
            deca_ap, o = carve(o, 512)
            dstg_ap, o = carve(o, 4, 128)
            kh_ap, o = carve(o, 2, 2 * NT)
            vh_ap, o = carve(o, 2, 16, 128)
            p_ap, o = carve(o, 2, 2 * NT)
            pt_ap, o = carve(o, 16, 128)
            osb_ap, o = carve(o, NCH, 512)
            assert o <= AR, o

            gate_f = gate_c.rearrange("a b -> (a b)")
            up_f = up_c.rearrange("a b -> (a b)")
            down_f = down_c.rearrange("a b -> (a b)")

            def decode_steps():
                tiles = []
                for t in range(6):
                    tiles.append(("g", gix_s, gate_f, t, E))
                for t in range(6):
                    tiles.append(("u", uix_s, up_f, t, E))
                for t in range(2):
                    tiles.append(("d", dix_s, down_f, t, INTER0))
                for (which, isrc, dstf, t, width) in tiles:
                    nsub = (width + 511) // 512
                    nic = width // 128
                    icq = []
                    for su in range(nsub):
                        w = min(512, width - su * 512)
                        ires = R["dec_i"]
                        ares = R["dec_a"]
                        def load(isrc=isrc, t=t, su=su, w=w, ires=ires):
                            em.dma("qs", dec_i[:, 0:w],
                                   isrc[t * 128:(t + 1) * 128, su * 512:su * 512 + w],
                                   writes=[ires], lane="ldeci")
                        yield load
                        def conv(w=w, ires=ires):
                            em.op("dve", lambda e: e.tensor_copy(
                                decb_ap[:, 0:w], dec_i[:, 0:w]),
                                reads=[ires], writes=[R["dec_b"]])
                        yield conv
                        for v in range(LUTN):
                            def step(v=v, w=w, ares=ares):
                                if v == 0:
                                    em.op("dve", lambda e: e.tensor_scalar(
                                        deca_ap[:, 0:w], decb_ap[:, 0:w],
                                        0.0, lutb[:, 0:1],
                                        ALU.is_equal, ALU.mult),
                                        reads=[R["dec_b"], R["lutb"]],
                                        writes=[ares])
                                else:
                                    em.op("dve", lambda e: e.tensor_scalar(
                                        dect_ap[:, 0:w], decb_ap[:, 0:w],
                                        float(v), lutb[:, v:v + 1],
                                        ALU.is_equal, ALU.mult),
                                        reads=[R["dec_b"], R["lutb"]],
                                        writes=[R["dec_t"]])
                                    em.op("dve", lambda e: e.tensor_tensor(
                                        deca_ap[:, 0:w], deca_ap[:, 0:w],
                                        dect_ap[:, 0:w], ALU.add),
                                        reads=[R["dec_t"]], writes=[ares])
                            yield step
                        nj = w // 128
                        ic0 = su * 4
                        if which in ("g", "u"):
                            dst = dstf[t * BLK:(t + 1) * BLK]\
                                .rearrange("(p a b) -> p a b", p=128, a=NCH)\
                                [:, ic0:ic0 + nj, :]
                        else:
                            gi = ic0 // 16
                            ii0 = ic0 - 16 * gi
                            base = t * 704512 + DOWN_GRP_OFF[gi]
                            glen = len(L0_GRPS[gi])
                            dst = dstf[base:base + 128 * glen * 128]\
                                .rearrange("(p a b) -> p a b", p=128, a=glen)\
                                [:, ii0:ii0 + nj, :]
                        def tpflush(nj=nj, dst=dst, ares=ares):
                            for j in range(nj):
                                em.op("pe", lambda e, j=j: e.transpose(
                                    psbf[:, j * 128:(j + 1) * 128],
                                    deca_ap[:, j * 128:(j + 1) * 128], ident[:, :]),
                                    reads=[ares, R["ident"]], writes=[slotbf[j]])
                            em.op("dve", lambda e: e.tensor_copy(
                                dstg_ap[:, 0:nj, :],
                                psbf.ap().rearrange("p (kb q) -> p kb q", kb=8)
                                [:, 0:nj, :]),
                                reads=[slotbf[i] for i in range(nj)],
                                writes=[R["dstg"]])
                            em.dma("qs", dst, dstg_ap[:, 0:nj, :],
                                   reads=[R["dstg"]],
                                   writes=[R["gh_cw"]], lane="lghc")
                        yield tpflush
                    if (which, t) == ("g", 5):
                        def cc():
                            em.collective("AllGather", ALU.bypass, [list(range(8))],
                                          [gate_c.opt()], [gate_g.opt()],
                                          reads=[R["gh_cw"]], writes=[R["gate_done"]])
                        yield cc
                    if (which, t) == ("u", 5):
                        def cc():
                            em.collective("AllGather", ALU.bypass, [list(range(8))],
                                          [up_c.opt()], [up_g.opt()],
                                          reads=[R["gh_cw"]], writes=[R["up_done"]])
                        yield cc
                    if (which, t) == ("d", 1):
                        def cc():
                            em.collective("AllGather", ALU.bypass, [list(range(8))],
                                          [down_c.opt()], [down_g.opt()],
                                          reads=[R["gh_cw"]], writes=[R["down_done"]])
                        yield cc

            dec_gen = decode_steps() if stop > 1 else iter(())

            def pump(k):
                for _ in range(k):
                    s = next(dec_gen, None)
                    if s is None:
                        return
                    s()

            def attention(layer):
                kvkg, kvvg = kvk_g[layer], kvv_g[layer]
                gres_k, gres_v = R[f"kvk_g{layer}"], R[f"kvv_g{layer}"]
                maskres = R["masks"]

                def kv_load(u):
                    g_, hd_ = divmod(u, HEADS)
                    sb_ = u % 2
                    for r in range(2):
                        em.dma("qa", kh_ap[:, sb_, r * NT:(r + 1) * NT],
                               kvkg[(r * HEADS + hd_) * 128:(r * HEADS + hd_ + 1) * 128, :],
                               reads=[gres_k], writes=[R[f"kh{sb_}"]],
                               lane="lkh", group=("kh", layer, g_, hd_))
                        em.dma("qa", vh_ap[:, sb_, r * 8:(r + 1) * 8, :],
                               kvvg[(r * HEADS + hd_) * NT:(r * HEADS + hd_ + 1) * NT, :]
                               .rearrange("(kb p) dd -> p kb dd", p=128),
                               reads=[gres_v], writes=[R[f"vh{sb_}"]],
                               lane=f"lvh{sb_}", group=("vh", layer, g_, hd_))

                kv_load(0)
                for g in range(2):
                    for hd in range(HEADS):
                        u = g * HEADS + hd
                        sb = u % 2
                        khres, vhres = R[f"kh{sb}"], R[f"vh{sb}"]

                        def s_stage(qs_):
                            j = g * 4 + qs_
                            nblk = (j + 1) if layer == 1 else 8
                            L = 2 * nblk * 128
                            qt = q_sb[:, hd, j * 128:(j + 1) * 128]
                            pieces = []
                            for r in range(2):
                                seg0 = r * nblk * 128
                                rem = nblk * 128
                                off = 0
                                while rem > 0:
                                    col = seg0 + off
                                    n = min(512 - col % 512, rem, 512)
                                    pieces.append((r, off, col, n))
                                    off += n
                                    rem -= n
                            for (r, off, col, n) in pieces:
                                b0 = col // 512
                                a = Acc(em, bank[b0], 1)
                                a.add(ps32[:, col:col + n], qt,
                                      kh_ap[:, sb, r * NT + off: r * NT + off + n],
                                      [qr[hd], khres])
                            return nblk, L

                        def softmax_stage(qs_, nblk, L):
                            j = g * 4 + qs_
                            pp = qs_ % 2
                            nbk = (L + 511) // 512
                            if layer == 1:
                                for r in range(2):
                                    cm = (nblk * (r + 1) - 1) * 128
                                    em.op("dve", lambda e, cm=cm, r=r, j=j: e.tensor_tensor(
                                        ps32[:, cm:cm + 128], ps32[:, cm:cm + 128],
                                        masks_ap[:, j * 2 + r, :], ALU.add),
                                        reads=[maskres, bank[cm // 512]],
                                        writes=[bank[cm // 512]])
                            bks = [bank[b0] for b0 in range(nbk)]
                            em.op("dve", lambda e, L=L: e.reduce_max(
                                mx[:, 6:7], ps32[:, 0:L], axis=AX.X),
                                reads=bks, writes=[R["mx6"]])
                            em.op("dve", lambda e: e.tensor_scalar(
                                mx[:, 7:8], mx[:, 6:7], -1.0, None, ALU.mult),
                                reads=[R["mx6"]], writes=[R["mx7"]])
                            em.op("act", lambda e, L=L, pp=pp: e.activation(
                                p_ap[:, pp, 0:L], ps32[:, 0:L], AF.Exp,
                                bias=mx[:, 7:8], accum_out=lsum[:, 6:7]),
                                reads=[R["mx7"]] + bks,
                                writes=[R[f"p{pp}"], R["ls6"]])
                            em.op("dve", lambda e: e.reciprocal(lsum[:, 7:8],
                                  lsum[:, 6:7]), reads=[R["ls6"]], writes=[R["ls7"]])
                            em.op("act", lambda e, L=L, pp=pp: e.activation(
                                p_ap[:, pp, 0:L], p_ap[:, pp, 0:L], AF.Identity,
                                scale=lsum[:, 7:8]),
                                reads=[R["ls7"]], writes=[R[f"p{pp}"]])

                        def pe_stage(qs_, nblk):
                            pp = qs_ % 2
                            nkb = 2 * nblk
                            for kb in range(nkb):
                                sl_ = kb % 8
                                em.op("pe", lambda e, kb=kb, sl_=sl_, pp=pp: e.transpose(
                                    psbf[:, sl_ * 128:(sl_ + 1) * 128],
                                    p_ap[:, pp, kb * 128:(kb + 1) * 128], ident[:, :]),
                                    reads=[R[f"p{pp}"], R["ident"]], writes=[slotbf[sl_]])
                                if sl_ == 7 or kb == nkb - 1:
                                    k0 = kb - sl_
                                    if layer == 0:
                                        em.op("act", lambda e, k0=k0, sl_=sl_,
                                              kb=kb: e.activation(
                                                  pt_ap[:, k0:kb + 1, :],
                                                  psbf.ap().rearrange(
                                                      "p (kb qq) -> p kb qq", kb=8)
                                                  [:, 0:sl_ + 1, :], AF.Copy),
                                              reads=[slotbf[i] for i in range(sl_ + 1)],
                                              writes=[ptr[i] for i in range(k0, kb + 1)])
                                    else:
                                        em.op("dve", lambda e, k0=k0, sl_=sl_, kb=kb:
                                              e.tensor_copy(
                                                  pt_ap[:, k0:kb + 1, :],
                                                  psbf.ap().rearrange(
                                                      "p (kb qq) -> p kb qq", kb=8)
                                                  [:, 0:sl_ + 1, :]),
                                              reads=[slotbf[i] for i in range(sl_ + 1)],
                                              writes=[ptr[i] for i in range(k0, kb + 1)])
                            a = Acc(em, bank[4], nkb)
                            for kb in range(nkb):
                                r, loc = divmod(kb, nblk)
                                vb = r * 8 + loc
                                a.add(ps32[:, 4 * 512:4 * 512 + 128],
                                      vh_ap[:, sb, vb, :], pt_ap[:, kb, :],
                                      [vhres, ptr[kb]])
                            em.op("act", lambda e, hd=hd, qs_=qs_: e.activation(
                                osb_ap[:, hd, qs_ * 128:(qs_ + 1) * 128],
                                ps32[:, 4 * 512:4 * 512 + 128], AF.Copy),
                                reads=[bank[4]], writes=[osr[hd]])

                        meta = {}
                        meta[0] = s_stage(0)
                        softmax_stage(0, *meta[0])
                        for j in range(1, 4):
                            meta[j] = s_stage(j)
                            if j == 1 and u + 1 < 2 * HEADS:
                                kv_load(u + 1)
                            pe_stage(j - 1, meta[j - 1][0])
                            softmax_stage(j, *meta[j])
                            if layer == 0:
                                pump(11)
                        pe_stage(3, meta[3][0])
                        if layer == 0:
                            pump(12)
                    tsl = slice(g * 512, (g + 1) * 512)
                    if True:
                        wo_bias = bo0_sb if layer == 0 else bo1_sb
                        wo_bres = R["bo0"] if layer == 0 else R["bo1"]
                        def wo_fill(fc, q):
                            wres = R[f"wb{fc % 2}"]
                            rank, b = fc // 2, fc % 2
                            if layer == 0:
                                flat = wo0_g.rearrange("a b -> (a b)")
                                off = rank * 2 * BLK + b * BLK
                                rd = R["wo0_done"]
                            else:
                                flat = dense_g.rearrange("a b -> (a b)")
                                off = rank * DENSE_SEG + OFF_WO_B + b * BLK
                                rd = R["dense_done"]
                            em.dma(q, wbuf[:, fc % 2, :, :],
                                   flat[off:off + BLK].rearrange(
                                       "(p a c) -> p a c", p=128, a=NCH),
                                   reads=[rd],
                                   writes=[wres], lane=f"lwb{fc % 2}",
                                   group=("wo", layer, g, fc))
                            return wres
                        wo_fill(0, "qs")
                        for fc in range(NCH):
                            wres = R[f"wb{fc % 2}"]
                            if fc + 1 < NCH:
                                wo_fill(fc + 1, "qs")
                            a = Acc(em, bank[5], NCH)
                            for ec in range(NCH):
                                a.add(pb(5), wbuf[:, fc % 2, ec, :], osb_ap[:, ec, :],
                                      [wres, osr[ec]])
                            em.op("dve", lambda e, fc=fc, wo_bias=wo_bias,
                                  wo_bres=wo_bres: e.scalar_tensor_tensor(
                                x_sb[:, fc, tsl], pb(5), wo_bias[:, fc:fc + 1],
                                x_sb[:, fc, tsl], ALU.add, ALU.add),
                                reads=[bank[5], wo_bres], writes=[xr[fc]])

            def ffn(layer, h_ap, wx_ap):
                hgroups = ([list(range(0, 16)), list(range(16, 32)), list(range(32, 43))]
                           if layer == 0 else
                           [list(range(g * 16, (g + 1) * 16)) for g in range(4)])

                def slotap(s):
                    return wbuf[:, s, :, :] if s < 2 else wx_ap[:, s - 2, :, :]

                def fill(which, ic, s, q):
                    wres = R[f"ws{s}"]
                    if which == "g":
                        if layer == 0:
                            rank, lo = ic // 6, ic % 6
                            flat = gate_g.rearrange("a b -> (a b)")
                            off = rank * G_ROWS * E + lo * BLK
                            rd = R["gate_done"]
                        else:
                            rank, lo = ic // 16, ic % 16
                            flat = dense_g.rearrange("a b -> (a b)")
                            off = rank * DENSE_SEG + OFF_WGU_B + lo * BLK
                            rd = R["dense_done"]
                        dst = slotap(s)[:, :, :]
                        srcap = flat[off:off + BLK].rearrange(
                            "(p a b) -> p a b", p=128, a=NCH)
                        em.dma(q, dst, srcap, reads=[rd], writes=[wres],
                               lane=f"lws{s}", group=("f", layer, which, str(ic)))
                        return wres
                    if which == "u":
                        if layer == 0:
                            rank, lo = ic // 6, ic % 6
                            flat = up_g.rearrange("a b -> (a b)")
                            off = rank * G_ROWS * E + lo * BLK
                            rd = R["up_done"]
                        else:
                            icu = 64 + ic
                            rank, lo = icu // 16, icu % 16
                            flat = dense_g.rearrange("a b -> (a b)")
                            off = rank * DENSE_SEG + OFF_WGU_B + lo * BLK
                            rd = R["dense_done"]
                        dst = slotap(s)[:, :, :]
                        srcap = flat[off:off + BLK].rearrange(
                            "(p a b) -> p a b", p=128, a=NCH)
                        em.dma(q, dst, srcap, reads=[rd], writes=[wres],
                               lane=f"lws{s}", group=("f", layer, which, str(ic)))
                        return wres
                    gi, fc, grp = ic
                    rank, b = fc // 2, fc % 2
                    glen = len(grp)
                    if layer == 0:
                        flat = down_g.rearrange("a b -> (a b)")
                        off = rank * DOWN_SEG + b * 704512 + DOWN_GRP_OFF[gi]
                        rd = R["down_done"]
                    else:
                        flat = dense_g.rearrange("a b -> (a b)")
                        off = rank * DENSE_SEG + OFF_WDN_B + (gi * 2 + b) * BLK
                        rd = R["dense_done"]
                    dst = slotap(s)[:, 0:glen, :]
                    srcap = flat[off:off + 128 * glen * 128].rearrange(
                        "(p a b) -> p a b", p=128, a=glen)
                    em.dma(q, dst, srcap, reads=[rd], writes=[wres],
                           lane=f"lws{s}", group=("f", layer, which, str(ic)))
                    return wres

                allc = [ic for grp in hgroups for ic in grp]
                fill("g", allc[0], 0, "qs")
                fill("u", allc[0], 2, "qs")
                k = 0
                for gi, grp in enumerate(hgroups):
                    for ii, ic in enumerate(grp):
                        if k + 1 < len(allc):
                            fill("g", allc[k + 1], (k + 1) % 2, "qs")
                            fill("u", allc[k + 1], 2 + (k + 1) % 2, "qs")
                        wg = R[f"ws{k % 2}"]
                        wu = R[f"ws{2 + k % 2}"]
                        gs = slotap(k % 2)
                        us = slotap(2 + k % 2)
                        for th in range(2):
                            sl = slice(512 * th, 512 * (th + 1))
                            a5 = Acc(em, bank[5], NCH)
                            for ec in range(NCH):
                                a5.add(pb(5), gs[:, ec, :], xn_sb[:, ec, sl],
                                       [wg, xnr[ec]])
                            a6 = Acc(em, bank[6], NCH)
                            for ec in range(NCH):
                                a6.add(pb(6), us[:, ec, :], xn_sb[:, ec, sl],
                                       [wu, xnr[ec]])
                            em.op("act", lambda e: e.activation(
                                sg_sb[:, :], pb(5), AF.Silu),
                                reads=[bank[5]], writes=[R["sg0"]])
                            em.op("dve", lambda e, ii=ii, sl=sl: e.tensor_tensor(
                                h_ap[:, ii, sl], pb(6), sg_sb[:, :], ALU.mult),
                                reads=[bank[6], R["sg0"]], writes=[hr[ii]])
                        k += 1
                    fill("d", (gi, 0, grp), 4, "qs")
                    for fc in range(NCH):
                        if fc + 1 < NCH:
                            fill("d", (gi, fc + 1, grp), 4 + (fc + 1) % 2, "qs")
                        wres = R[f"ws{4 + fc % 2}"]
                        ds = slotap(4 + fc % 2)
                        for th in range(2):
                            sl = slice(512 * th, 512 * (th + 1))
                            a = Acc(em, bank[4], len(grp))
                            for ii in range(len(grp)):
                                a.add(pb(4), ds[:, ii, :], h_ap[:, ii, sl],
                                      [wres, hr[ii]])
                            em.op("dve", lambda e, fc=fc, sl=sl: e.tensor_tensor(
                                x_sb[:, fc, sl], x_sb[:, fc, sl], pb(4), ALU.add),
                                reads=[bank[4]], writes=[xr[fc]])

            masks_ap = None
            if stop > 1:
                attention(0)
                pump(10 ** 9)
            if stop == 2:
                dump("x_d", [128, NCH, NT], F32, x_sb[:, :, :], list(xr))

            if stop > 2:
                em.barrier()
                o = 0
                h_ap, o = carve(o, NCH, NT)
                wx_ap, o = carve(o, 4, NCH, 128)
                ffn(0, h_ap, wx_ap)
            if stop == 3:
                dump("x_d", [128, NCH, NT], F32, x_sb[:, :, :], list(xr))

            if stop > 3:
                em.barrier()
                o = 0
                cf1_ap, o = carve(o, 4, NT)
                bv1row_ap, o = carve(o, E)
                vw2_ap, o = carve(o, 2, 512)
                ln()
                em.dma("qs", cf1_ap[:, 0, :], cf1q_in[:, :], writes=[R["cf1q"]], lane="lcf")
                em.dma("qs", cf1_ap[:, 1, :], sf1q_in[:, :], writes=[R["sf1q"]], lane="lsf")
                em.dma("qs", cf1_ap[:, 2, :], cf1k_in[:, :], writes=[R["cf1k"]], lane="lcf2")
                em.dma("qs", cf1_ap[:, 3, :], sf1k_in[:, :], writes=[R["sf1k"]], lane="lsf2")
                em.dma("qg", bv1row_ap[0:1, :], bv1_in[:, :], writes=[R["bv1"]], lane="lbdv")

                dg_flat = dense_g.rearrange("a b -> (a b)")
                LIN_OFF = {0: OFF_WQ_B, QKVO_R: OFF_WK_B, 3 * QKVO_R: OFF_WO_B}

                def lin_fill(base, fc, slot, q):
                    wres = R[f"wb{slot}"]
                    rank, b = fc // 2, fc % 2
                    off = rank * DENSE_SEG + LIN_OFF[base] + b * BLK
                    em.dma(q, wbuf[:, slot, :, :],
                           dg_flat[off:off + BLK].rearrange(
                               "(p a c) -> p a c", p=128, a=NCH),
                           reads=[R["dense_done"]], writes=[wres],
                           lane=f"lwb{slot}", group=("lin", base, fc))
                    return wres

                def lin_comp(fc, slot, out_fn):
                    wres = R[f"wb{slot}"]
                    for th in range(2):
                        sl = slice(512 * th, 512 * (th + 1))
                        bk = 5 + th % 2
                        a = Acc(em, bank[bk], NCH)
                        for ec in range(NCH):
                            a.add(pb(bk), wbuf[:, slot, ec, :], xn_sb[:, ec, sl],
                                  [wres, xnr[ec]])
                        out_fn(bk, th, sl)

                lin_fill(QKVO_R, 0, 0, "qs")
                for fc in range(NCH):
                    kres = R["kst"]
                    def kevac(bk, th, sl, fc=fc, kres=kres):
                        em.op("act", lambda e: e.activation(
                            kst[:, sl], pb(bk), AF.Identity,
                            bias=bk1_sb[:, fc:fc + 1]),
                            reads=[R["bk1"], bank[bk]], writes=[kres])
                    if fc + 1 < NCH:
                        lin_fill(QKVO_R, fc + 1, (fc + 1) % 2, "qs")
                    lin_comp(fc, fc % 2, kevac)
                    for th in range(2):
                        sl = slice(512 * th, 512 * (th + 1))
                        a = Acc(em, bank[4], 1)
                        a.add(pb(4), r1t[:, :], kst[:, sl], [R["r1t"], kres])
                        em.op("dve", lambda e, sl=sl: e.tensor_tensor(
                            pb(3), pb(4), cf1_ap[:, 3, sl], ALU.mult),
                            reads=[bank[4], R["sf1k"]], writes=[bank[3]])
                        em.op("dve", lambda e, sl=sl: e.tensor_tensor(
                            sg_sb[:, :], kst[:, sl], cf1_ap[:, 2, sl], ALU.mult),
                            reads=[kres, R["cf1k"]], writes=[R["sg0"]])
                        em.op("dve", lambda e, sl=sl: e.tensor_tensor(
                            kst[:, sl], sg_sb[:, :], pb(3), ALU.add),
                            reads=[R["sg0"], bank[3]], writes=[kres])
                    em.dma("qa", kvk_c[1][fc * 128:(fc + 1) * 128, :], kst[:, :],
                           reads=[kres], writes=[R["kvk_c1"]], lane=f"lkvc{next(_kvrr)}")
                em.collective("AllGather", ALU.bypass, [[0, 1], [2, 3], [4, 5], [6, 7]],
                              [kvk_c[1].opt()], [kvk_g[1].opt()],
                              reads=[R["kvk_c1"]], writes=[R["kvk_g1"]])
                for fv in range(4):
                    a = Acc(em, bank[2], 1)
                    a.add(pb(2), onesrb[0:1, :], bv1row_ap[0:1, fv * 512:(fv + 1) * 512],
                          [R["onesrb"], R["bv1"]])
                    em.op("dve", lambda e: e.tensor_copy(vbias[:, :], pb(2)),
                          reads=[bank[2]], writes=[R["vb0"]])
                    def v_wload(fv_, ec_):
                        s_ = ec_ % 2
                        vw_ = R[f"vw2{s_}"]
                        for rr in range(2):
                            rank = 2 * fv_ + rr
                            off = rank * DENSE_SEG + OFF_WV_B + ec_ * 32768
                            em.dma("qs", vw2_ap[:, s_, rr * 256:(rr + 1) * 256],
                                   dg_flat[off:off + 32768].rearrange(
                                       "(p f2) -> p f2", p=128),
                                   reads=[R["dense_done"]],
                                   writes=[vw_], lane=f"lvw2{s_}",
                                   group=("vw", fv_, ec_))
                        return vw_
                    for half in range(2):
                        v_wload(fv, 0)
                        for ec in range(NCH):
                            vw = R[f"vw2{ec % 2}"]
                            if ec + 1 < NCH:
                                v_wload(fv, ec + 1)
                            for tg in range(4):
                                ts_ = half * 4 + tg
                                bk = 3 + tg
                                em._deps("pe", [vw, xnr[ec]],
                                         [bank[bk]] if ec == 0 else [])
                                instr = em.eng["pe"].matmul(
                                    pb(bk), xn_sb[:, ec, ts_ * 128:(ts_ + 1) * 128],
                                    vw2_ap[:, ec % 2, :], start=(ec == 0),
                                    stop=(ec == NCH - 1))
                                instr.then_inc(em.sem["pe"], 1)
                                em.cnt["pe"] += 1
                                t = em.cnt["pe"]
                                vw.r.append(("pe", t))
                                xnr[ec].r.append(("pe", t))
                                if ec == NCH - 1:
                                    bank[bk].w = ("pe", t)
                                    bank[bk].r = []
                        for tg in range(4):
                            ts_ = half * 4 + tg
                            bk = 3 + tg
                            vres = R["vst"]
                            em.op("dve", lambda e, bk=bk: e.tensor_tensor(
                                vst[:, :], pb(bk), vbias[:, :], ALU.add),
                                reads=[bank[bk], R["vb0"]], writes=[vres])
                            for hl in range(4):
                                hd = fv * 4 + hl
                                em.dma("qa",
                                       kvv_c[1][hd * NT + ts_ * 128:
                                                hd * NT + (ts_ + 1) * 128, :],
                                       vst[:, hl * 128:(hl + 1) * 128],
                                       reads=[vres], writes=[R["kvv_c1"]], lane=f"lkvc{next(_kvrr)}")
                em.collective("AllGather", ALU.bypass, [[0, 1], [2, 3], [4, 5], [6, 7]],
                              [kvv_c[1].opt()], [kvv_g[1].opt()],
                              reads=[R["kvv_c1"]], writes=[R["kvv_g1"]])
                lin_fill(0, 0, 0, "qs")
                for fc in range(NCH):
                    def qevac(bk, th, sl, fc=fc):
                        em.op("act", lambda e: e.activation(
                            q_sb[:, fc, sl], pb(bk), AF.Identity,
                            bias=bq1_sb[:, fc:fc + 1]),
                            reads=[R["bq1"], bank[bk]], writes=[qr[fc]])
                    if fc + 1 < NCH:
                        lin_fill(0, fc + 1, (fc + 1) % 2, "qs")
                    lin_comp(fc, fc % 2, qevac)
                    for th in range(2):
                        sl = slice(512 * th, 512 * (th + 1))
                        a = Acc(em, bank[4], 1)
                        a.add(pb(4), r1t[:, :], q_sb[:, fc, sl], [R["r1t"], qr[fc]])
                        em.op("dve", lambda e, sl=sl: e.tensor_tensor(
                            pb(3), pb(4), cf1_ap[:, 1, sl], ALU.mult),
                            reads=[bank[4], R["sf1q"]], writes=[bank[3]])
                        em.op("dve", lambda e, fc=fc, sl=sl: e.tensor_tensor(
                            sg_sb[:, :], q_sb[:, fc, sl], cf1_ap[:, 0, sl], ALU.mult),
                            reads=[qr[fc], R["cf1q"]], writes=[R["sg0"]])
                        em.op("dve", lambda e, fc=fc, sl=sl: e.tensor_tensor(
                            q_sb[:, fc, sl], sg_sb[:, :], pb(3), ALU.add),
                            reads=[R["sg0"], bank[3]], writes=[qr[fc]])
            if stop == 4:
                dump("q_d", [128, NCH, NT], BF16, q_sb[:, :, :], list(qr))
                dump("xn_d", [128, NCH, NT], BF16, xn_sb[:, :, :], list(xnr))

            if stop > 4:
                em.barrier()
                o = 0
                kh_ap, o = carve(o, 2, 2 * NT)
                vh_ap, o = carve(o, 2, 16, 128)
                p_ap, o = carve(o, 2, 2 * NT)
                pt_ap, o = carve(o, 16, 128)
                osb_ap, o = carve(o, NCH, 512)
                masks_ap, o = carve(o, NTB * 2, 128)
                em.dma("qs", masks_ap, masks_in.rearrange("j r p k -> p (j r) k"),
                       writes=[R["masks"]], lane="lmask")
                attention(1)
            if stop == 5:
                dump("x_d", [128, NCH, NT], F32, x_sb[:, :, :], list(xr))

            if stop > 5:
                em.barrier()
                o = 0
                h_ap, o = carve(o, NCH, NT)
                wx_ap, o = carve(o, 4, NCH, 128)
                ffn(1, h_ap, wx_ap)
                em.dma("qg", y_out.rearrange("(c p) t -> p c t", p=128), x_sb[:, :, :],
                       reads=list(xr), writes=[R["y"]])

            for s, v in sorted(em.cnt.items()):
                if v and s != "pool":
                    em.eng["qg"].wait_ge(em.sem[s], v)
    return nc, dbg_outs


def _core_inputs(inputs, c):
    b, h = divmod(c, 2)
    f32 = np.float32
    x4 = np.asarray(inputs["L0_attn_x"], f32)[b]
    xs = x4.reshape(16, 128, E)[h::2].reshape(NT, E)

    gix = np.asarray(inputs["L0_ffn_gate_idx"], np.int32)
    uix = np.asarray(inputs["L0_ffn_up_idx"], np.int32)
    dix = np.asarray(inputs["L0_ffn_down_idx"], np.int32)

    def gslice(a):
        out = np.zeros((G_ROWS, E), np.int32)
        lo = c * G_ROWS
        hi = min(INTER0, lo + G_ROWS)
        if hi > lo:
            out[0:hi - lo] = a[lo:hi]
        return out

    cf0, sf0 = _rope0_fields(h)
    cf1q, sf1q = _rope1_fields(h, SCALE)
    cf1k, sf1k = _rope1_fields(h, 1.0)
    lut = np.asarray(inputs["L0_ffn_lut"], f32).reshape(1, LUTN)
    lutvals = np.concatenate([lut, np.arange(LUTN, dtype=f32).reshape(1, LUTN)], 1)
    b_d = np.asarray(inputs["L0_attn_b_d"], f32)

    ck = lambda a: np.ascontiguousarray(a)
    return {
        "x_shard": ck(xs.T),
        "W_c": np.asarray(inputs["L0_attn_W_c"], f32),
        "W_d": np.asarray(inputs["L0_attn_W_d"], f32),
        "W_o0": np.asarray(inputs["L0_attn_W_o"], f32),
        "wq_s": ck(np.asarray(inputs["L1_attn_Wq"], f32)[c * QKVO_R:(c + 1) * QKVO_R]),
        "wk_s": ck(np.asarray(inputs["L1_attn_Wk"], f32)[c * QKVO_R:(c + 1) * QKVO_R]),
        "wv_s": ck(np.asarray(inputs["L1_attn_Wv"], f32)[c * QKVO_R:(c + 1) * QKVO_R]),
        "wo_s": ck(np.asarray(inputs["L1_attn_Wo"], f32)[c * QKVO_R:(c + 1) * QKVO_R]),
        "wo0t_s": ck(np.asarray(inputs["L0_attn_W_o"], f32).T[c * QKVO_R:(c + 1) * QKVO_R]),
        "wgu_s": ck(np.asarray(inputs["L1_ffn_w_gate_up"], f32)[c * WGU_R:(c + 1) * WGU_R]),
        "wdn_s": ck(np.asarray(inputs["L1_ffn_w_down"], f32)[c * WDN_R:(c + 1) * WDN_R]),
        "gix_s": gslice(gix),
        "uix_s": gslice(uix),
        "dix_s": ck(dix[c * D_ROWS:(c + 1) * D_ROWS]),
        "lutvals": lutvals,
        "ident": np.eye(128, dtype=BF),
        "r0t": _rot0T(), "r1t": _rot1T(),
        "cf0": cf0, "sf0": sf0,
        "cf1q": cf1q, "sf1q": sf1q, "cf1k": cf1k, "sf1k": sf1k,
        "masks": _masks(h),
        "bc_col": ck(np.asarray(inputs["L0_attn_b_c"], f32).reshape(4, 128).T),
        "bdqk_col": ck(b_d[:4096].reshape(32, 128).T),
        "bdv_row": ck(b_d[4096:].reshape(1, E)),
        "bo0_col": ck(np.asarray(inputs["L0_attn_b_o"], f32).reshape(NCH, 128).T),
        "bq1_col": ck(np.asarray(inputs["L1_attn_bq"], f32).reshape(NCH, 128).T),
        "bk1_col": ck(np.asarray(inputs["L1_attn_bk"], f32).reshape(NCH, 128).T),
        "bv1_row": ck(np.asarray(inputs["L1_attn_bv"], f32).reshape(1, E)),
        "bo1_col": ck(np.asarray(inputs["L1_attn_bo"], f32).reshape(NCH, 128).T),
    }


def kernel(**inputs):
    dbg = os.environ.get("MK_DBG", "")
    nc, dbg_names = build_program(dbg)
    in_maps = [_core_inputs(inputs, c) for c in range(8)]
    res = run_bass_kernel_spmd(nc, in_maps, core_ids=list(range(8)))
    if dbg:
        return res
    out = np.empty((B, S, E), np.float32)
    for c in range(8):
        b, h = divmod(c, 2)
        y = res.results[c]["y"]
        out[b].reshape(16, 128, E)[h::2] = y.T.reshape(NTB, 128, E)
    return out
